# version 08
# baseline (speedup 1.0000x reference)
import numpy as np
import concourse.bass as bass
import concourse.mybir as mybir
from concourse.bass_utils import run_bass_kernel_spmd

F32 = mybir.dt.float32
BF16 = mybir.dt.bfloat16
U8 = mybir.dt.uint8
U16 = mybir.dt.uint16
AF = mybir.ActivationFunctionType
OP = mybir.AluOpType
AX = mybir.AxisListType

D = 2048
S = 2048
NSEQ = 2
MLP = 16384
EPS = 1e-5
ASCALE = 1.0 / float(np.sqrt(np.float32(128.0)))
C0 = 0.7978845608028654
C1 = 0.044715
NEG = -100000.0

CFG = {
    "seqs": [0, 1],
    "heads": list(range(16)),
    "ffn_groups": [0, 1, 2, 3],
    "do_attn": True,
    "do_wo": True,
    "do_ffn": True,
    "deq_upfront": False,
    "debug_outs": [],
}


def _merge(dst, src):
    for k, v in src.items():
        if dst.get(k, -1) < v:
            dst[k] = v


class Buf:
    __slots__ = ("w", "r")

    def __init__(self):
        self.w = {}
        self.r = {}


class Bld:
    ENG = ("pe", "dve", "act", "pool", "sp")

    NROT = 16

    def __init__(self, nc):
        self.nc = nc
        self.q = {e: [] for e in self.ENG}
        self.sem = {}
        names = ["Epe", "Edve", "Eact", "Epool"]
        for qn in ("Qsp", "Qact", "Qpool"):
            names += ["%s_%d" % (qn, i) for i in range(self.NROT)]
        for s in names:
            self.sem[s] = nc.alloc_semaphore(s)
        self.cnt = {s: 0 for s in self.sem}
        self.dma_i = {"Qsp": 0, "Qact": 0, "Qpool": 0}
        self.wm = {e: {} for e in self.ENG}

    def _waits(self, eng, dep):
        for s in sorted(dep):
            v = dep[s]
            if self.wm[eng].get(s, 0) < v:
                so = self.sem[s]
                self.q[eng].append(lambda E, so=so, v=v: E.wait_ge(so, v))
                self.wm[eng][s] = v

    def _dep(self, reads, writes):
        dep = {}
        for b in reads:
            _merge(dep, b.w)
        for b in writes:
            _merge(dep, b.w)
            _merge(dep, b.r)
        return dep

    def _fin(self, h, reads, writes):
        for b in reads:
            _merge(b.r, h)
        for b in writes:
            _merge(b.w, h)
            b.r = {}

    def op(self, eng, fn, reads=(), writes=()):
        self._waits(eng, self._dep(reads, writes))
        s = "E" + eng
        self.cnt[s] += 1
        so = self.sem[s]
        self.q[eng].append(lambda E, fn=fn, so=so: fn(E).then_inc(so, 1))
        h = {s: self.cnt[s]}
        self._fin(h, reads, writes)
        return h

    def group(self, eng, fns, reads=(), writes=()):
        self._waits(eng, self._dep(reads, writes))
        s = "E" + eng
        self.cnt[s] += 1
        so = self.sem[s]
        for fn in fns[:-1]:
            self.q[eng].append(lambda E, fn=fn: fn(E))
        self.q[eng].append(lambda E, fn=fns[-1], so=so: fn(E).then_inc(so, 1))
        h = {s: self.cnt[s]}
        self._fin(h, reads, writes)
        return h

    def dma(self, qeng, fn, reads=(), writes=()):
        dep = self._dep(reads, writes)
        qn = "Q" + qeng
        i = self.dma_i[qn]
        self.dma_i[qn] = i + 1
        s = "%s_%d" % (qn, i % self.NROT)
        if self.cnt[s] > 0:
            _merge(dep, {s: self.cnt[s]})
        self._waits(qeng, dep)
        self.cnt[s] += 16
        so = self.sem[s]
        self.q[qeng].append(lambda E, fn=fn, so=so: fn(E).then_inc(so, 16))
        h = {s: self.cnt[s]}
        self._fin(h, reads, writes)
        return h

    def finalize(self):
        for s in sorted(self.cnt):
            c = self.cnt[s]
            if c > 0:
                so = self.sem[s]
                self.q["sp"].append(lambda E, so=so, v=c: E.wait_ge(so, v))
        with self.nc.Block() as block:
            block.tensor(lambda E: ([f(E) for f in self.q["pe"]], None)[1])
            block.vector(lambda E: ([f(E) for f in self.q["dve"]], None)[1])
            block.scalar(lambda E: ([f(E) for f in self.q["act"]], None)[1])
            block.gpsimd(lambda E: ([f(E) for f in self.q["pool"]], None)[1])
            block.sync(lambda E: ([f(E) for f in self.q["sp"]], None)[1])


def build(cfg=None):
    if cfg is None:
        cfg = CFG
    nc = bass.Bass("TRN2", target_bir_lowering=False)
    DT = nc.dram_tensor
    x_in = DT("x", [NSEQ, 16, 128, S], F32, kind="ExternalInput").ap()
    wq_in = DT("wq", [16, 128, 2048], F32, kind="ExternalInput").ap()
    wk_in = DT("wk", [16, 128, 2048], F32, kind="ExternalInput").ap()
    wv_in = DT("wv", [8, 128, 4096], F32, kind="ExternalInput").ap()
    wo_in = DT("wo", [16, 128, 2048], F32, kind="ExternalInput").ap()
    bq_in = DT("bq", [16, 128], F32, kind="ExternalInput").ap()
    bk_in = DT("bk", [16, 128], F32, kind="ExternalInput").ap()
    bvg_in = DT("bvg", [1, 2048], F32, kind="ExternalInput").ap()
    bo_in = DT("bo", [16, 128], F32, kind="ExternalInput").ap()
    pg_in = DT("pg", [1024, MLP], U8, kind="ExternalInput").ap()
    pu_in = DT("pu", [1024, MLP], U8, kind="ExternalInput").ap()
    pd_in = DT("pd", [8192, 2048], U8, kind="ExternalInput").ap()
    sg_in = DT("sg", [64, MLP], F32, kind="ExternalInput").ap()
    su_in = DT("su", [64, MLP], F32, kind="ExternalInput").ap()
    sd_in = DT("sd", [512, 2048], F32, kind="ExternalInput").ap()
    idn_in = DT("idn", [128, 128], F32, kind="ExternalInput").ap()
    msk_in = DT("msk", [128, 128], F32, kind="ExternalInput").ap()

    xln1_s = DT("xln1_s", [NSEQ, 16, 128, S], BF16, kind="Internal").ap()
    xmid_s = DT("xmid_s", [NSEQ, 16, 128, S], F32, kind="Internal").ap()
    xn2_s = DT("xn2_s", [NSEQ, 16, 128, S], BF16, kind="Internal").ap()
    wg_s = DT("wg_s", [64, 128, 16, 256], BF16, kind="Internal").ap()
    wu_s = DT("wu_s", [64, 128, 16, 256], BF16, kind="Internal").ap()
    wd_s = DT("wd_s", [16, 16, 128, 8, 128], BF16, kind="Internal").ap()
    out_d = DT("out", [NSEQ, 16, 128, S], F32, kind="ExternalOutput").ap()

    dbg = []
    for name in cfg["debug_outs"]:
        src = {"xln1_s": xln1_s, "xmid_s": xmid_s, "xn2_s": xn2_s,
               "wg_s": wg_s, "wu_s": wu_s, "wd_s": wd_s}[name]
        n = int(np.prod(src.shape))
        dbg.append((name, src, DT("dbg_" + name, [n // src.shape[-1], src.shape[-1]],
                                  src.dtype, kind="ExternalOutput").ap()))

    SB = nc.alloc_sbuf_tensor
    arA = SB("arA", [128, 16384], F32)
    arB = SB("arB", [128, 8192], F32)
    arC = SB("arC", [128, 8192], F32)
    arD = SB("arD", [128, 12288], F32)
    arA16 = arA.bitcast(BF16)
    arB16 = arB.bitcast(BF16)
    arC16 = arC.bitcast(BF16)
    arD16 = arD.bitcast(BF16)
    arD8 = arD.bitcast(U8)
    arDu16 = arD.bitcast(U16)

    ident = SB("ident", [128, 128], BF16)
    mask = SB("mask", [128, 128], F32)
    ones_cb = SB("ones_cb", [128, 1], BF16)
    ones_r = SB("ones_r", [1, 128], F32)
    ones_rb = SB("ones_rb", [1, 128], BF16)
    rows = SB("rows", [1, 2560], F32)
    cols = SB("cols", [128, 12], F32)
    bvg_sb = SB("bvg_sb", [1, 2048], BF16)
    maskb = SB("maskb", [128, 128], BF16)
    aux8 = SB("aux8", [128, 2048], F32)
    x2t = SB("x2t", [128, 2048], BF16)
    xbt = SB("xbt", [128, 2048], BF16)

    pbig = nc.alloc_psum_tensor("pbig", [128, 2048], F32)
    p4b = nc.alloc_psum_tensor("p4b", [128, 1024], BF16)
    p5 = nc.alloc_psum_tensor("p5", [128, 512], F32)
    p6 = nc.alloc_psum_tensor("p6", [128, 512], F32)
    p7 = nc.alloc_psum_tensor("p7", [128, 512], F32)

    b = Bld(nc)
    AR = {"A": [], "B": [], "C": [], "D": []}

    def nb(ar=None, lo=0, hi=1 << 30):
        bf_ = Buf()
        if ar is not None:
            for (l2, h2, b2) in AR[ar]:
                if l2 < hi and lo < h2:
                    _merge(bf_.w, b2.w)
                    _merge(bf_.w, b2.r)
            AR[ar].append((lo, hi, bf_))
        return bf_

    PB = [nb() for _ in range(4)]
    P4, P5, P6, P7 = nb(), nb(), nb(), nb()
    B_rows, B_cols = nb(), nb()
    B_x2t = [nb(), nb()]
    B_xbt = [nb(), nb()]

    R_mean = rows[0:1, 0:512]
    R_msq = rows[0:1, 512:1024]
    R_var = rows[0:1, 1024:1536]
    R_rstd = rows[0:1, 1536:2048]
    R_nmr = rows[0:1, 2048:2560]
    R_tmp = rows[0:1, 512:1024]
    C_mx = [cols[:, 0:1], cols[:, 4:5]]
    C_nb = [cols[:, 1:2], cols[:, 5:6]]
    C_sum = [cols[:, 2:3], cols[:, 6:7]]
    C_rs = [cols[:, 3:4], cols[:, 7:8]]
    C_bq = cols[:, 8:9]
    C_bk = cols[:, 9:10]
    C_bo = cols[:, 10:11]

    c_ident = nb(); c_mask = nb(); c_ones = nb(); c_bvg = nb()
    b.dma("pool", lambda E: E.dma_start(out=ident[:], in_=idn_in[:, :]), writes=[c_ident])
    b.dma("sp", lambda E: E.dma_start(out=mask[:], in_=msk_in[:, :]), writes=[c_mask])
    b.op("dve", lambda E: E.tensor_copy(maskb[:], mask[:]), reads=[c_mask], writes=[c_mask])
    b.dma("pool", lambda E: E.dma_start(out=bvg_sb[:], in_=bvg_in[:, :]), writes=[c_bvg])
    b.op("pool", lambda E: E.memset(ones_cb[:], 1.0), writes=[c_ones])
    b.op("pool", lambda E: E.memset(ones_r[:], 1.0), writes=[c_ones])
    b.op("pool", lambda E: E.memset(ones_rb[:], 1.0), writes=[c_ones])

    def ln_stats(tiles, wide=None):
        for i in range(0, 16, 2):
            sl = (i // 2) % 2
            xb = xbt[:, sl * 1024:sl * 1024 + 1024]
            x2 = x2t[:, sl * 1024:sl * 1024 + 1024]
            if wide is not None:
                wap, wbufs = wide[i // 2]
                b.op("dve", lambda E, tl=wap, xb=xb: E.tensor_copy(xb, tl),
                     reads=wbufs, writes=[B_xbt[sl]])
                b.op("act", lambda E, tl=wap, x2=x2: E.activation(x2, tl, AF.Square),
                     reads=wbufs, writes=[B_x2t[sl]])
            else:
                for k in range(2):
                    tl, tb = tiles[i + k]
                    b.op("dve", lambda E, tl=tl, d=xb[:, k * 512:k * 512 + 512]:
                         E.tensor_copy(d, tl), reads=[tb], writes=[B_xbt[sl]])
                    b.op("act", lambda E, tl=tl, d=x2[:, k * 512:k * 512 + 512]:
                         E.activation(d, tl, AF.Square), reads=[tb], writes=[B_x2t[sl]])
            st, sp = (i == 0), (i == 14)
            for k in range(2):
                stk = st and k == 0
                spk = sp and k == 1
                b.group("pe", [lambda E, xb=xb[:, k * 512:k * 512 + 512], st=stk, sp=spk:
                               E.matmul(p7[0:1, :], ones_cb[:], xb, start=st, stop=sp)],
                        reads=[B_xbt[sl], c_ones], writes=[P7])
                b.group("pe", [lambda E, x2=x2[:, k * 512:k * 512 + 512], st=stk, sp=spk:
                               E.matmul(p5[0:1, :], ones_cb[:], x2, start=st, stop=sp)],
                        reads=[B_x2t[sl], c_ones], writes=[P5])
        b.op("dve", lambda E: E.tensor_scalar(R_mean, p7[0:1, :], 1.0 / D, None, OP.mult),
             reads=[P7], writes=[B_rows])
        b.op("dve", lambda E: E.tensor_scalar(R_msq, p5[0:1, :], 1.0 / D, None, OP.mult),
             reads=[P5], writes=[B_rows])
        b.op("dve", lambda E: E.tensor_mul(R_var, R_mean, R_mean), reads=[B_rows], writes=[B_rows])
        b.op("dve", lambda E: E.scalar_tensor_tensor(R_var, R_msq, EPS, R_var, OP.add, OP.subtract),
             reads=[B_rows], writes=[B_rows])
        b.op("act", lambda E: E.activation(R_tmp, R_var, AF.Ln), reads=[B_rows], writes=[B_rows])
        b.op("act", lambda E: E.activation(R_rstd, R_tmp, AF.Exp, scale=-0.5),
             reads=[B_rows], writes=[B_rows])
        b.op("dve", lambda E: E.tensor_mul(R_nmr, R_mean, R_rstd), reads=[B_rows], writes=[B_rows])
        b.group("pe", [lambda E: E.matmul(p6[:, :], ones_r[:], R_rstd, start=True, stop=True)],
                reads=[B_rows, c_ones], writes=[P6])
        b.group("pe", [lambda E: E.matmul(p5[:, :], ones_r[:], R_nmr, start=True, stop=True)],
                reads=[B_rows, c_ones], writes=[P5])

    B_xln1 = {s: nb() for s in range(NSEQ)}
    for seq in cfg["seqs"]:
        xw = [arA[:, i * 512:(i + 1) * 512] for i in range(16)]
        xw_bufs = [nb("A") for _ in range(16)]
        stg = [arA16[:, 16384 + i * 512:16384 + (i + 1) * 512] for i in range(16)]
        stg_bufs = [nb("A") for _ in range(16)]
        for tg in range(4):
            t0 = tg * 512
            for c in range(16):
                b.dma("sp", lambda E, c=c, t0=t0, seq=seq, d=xw[c]:
                      E.dma_start(out=d, in_=x_in[seq, c, :, t0:t0 + 512]),
                      writes=[xw_bufs[c]])
            ln_stats(None, wide=[(arA[:, i * 1024:(i + 1) * 1024],
                                  [xw_bufs[2 * i], xw_bufs[2 * i + 1]]) for i in range(8)])
            bc6 = bass.AP(p6, 0, [[512, 128], [0, 4], [1, 512]])
            bc5 = bass.AP(p5, 0, [[512, 128], [0, 4], [1, 512]])
            for i in range(4):
                src = arA[:, i * 2048:(i + 1) * 2048]
                d = arA16[:, 16384 + i * 2048:16384 + (i + 1) * 2048]
                rd = xw_bufs[4 * i:4 * i + 4]
                wr = stg_bufs[4 * i:4 * i + 4]
                b.op("dve", lambda E, s_=src, d=d, bc6=bc6: E.tensor_tensor(
                    d.rearrange("p (c t) -> p c t", c=4), s_.rearrange("p (c t) -> p c t", c=4),
                    bc6, OP.mult), reads=rd + [P6], writes=wr)
                b.op("dve", lambda E, d=d, bc5=bc5: E.tensor_tensor(
                    d.rearrange("p (c t) -> p c t", c=4), d.rearrange("p (c t) -> p c t", c=4),
                    bc5, OP.subtract), reads=[P5], writes=wr)
                o_ap = bass.AP(xln1_s.tensor, (seq * 16 + 4 * i) * 128 * S + t0,
                               [[S, 128], [128 * S, 4], [1, 512]])
                b.dma("act", lambda E, d=d, o_ap=o_ap:
                      E.dma_start(out=o_ap, in_=d.rearrange("p (c t) -> p c t", c=4)),
                      reads=wr, writes=[B_xln1[seq]])

    deq_tasks = []
    if cfg["do_ffn"]:
        for kpc in range(8):
            for ns in range(32):
                deq_tasks.append(("g", kpc, ns))
        for kpc in range(8):
            for ns in range(32):
                deq_tasks.append(("u", kpc, ns))
        for npc in range(64):
            for ds in range(4):
                deq_tasks.append(("d", npc, ds))
    B_wg = {i: nb() for i in range(64)}
    B_wu = {i: nb() for i in range(64)}
    B_wd = {i: nb() for i in range(16)}

    dq_pk = [arD8[:, 24576 + i * 512:24576 + (i + 1) * 512] for i in range(2)]
    dq_sc = [arD[:, 6400 + i * 512:6400 + (i + 1) * 512] for i in range(2)]
    DQ = {}
    for si in range(2):
        base = 29696 + si * 9216
        DQ[si] = {
            "n": arD8[:, base:base + 512],
            "m": arD8[:, base + 512:base + 1024],
            "out": arD16[:, (base + 1024) // 2:(base + 1024) // 2 + 512],
            "t1": arDu16[:, (base + 2048) // 2:(base + 2048) // 2 + 512],
            "b0": arDu16[:, (base + 3072) // 2:(base + 3072) // 2 + 512],
            "bits": arDu16[:, (base + 4096) // 2:(base + 4096) // 2 + 512],
            "g0": arD[:, (base + 5120) // 4:(base + 5120) // 4 + 512],
        }
    B_dq_pk = [nb("D"), nb("D")]
    B_dq_sc = [nb("D"), nb("D")]
    B_dq = [{k: nb("D") for k in ("n", "m", "t1", "b0", "bits", "g0", "out")}
            for _ in range(2)]
    dq_state = {"i": 0, "slot": 0}

    def emit_dequant(n_tiles):
        for _ in range(n_tiles):
            i = dq_state["i"]
            if i >= len(deq_tasks):
                return
            dq_state["i"] += 1
            kind, a0, a1 = deq_tasks[i]
            slot = dq_state["slot"]
            dq_state["slot"] ^= 1
            pk = dq_pk[slot]
            sc = dq_sc[slot]
            bpk = B_dq_pk[slot]
            bsc = B_dq_sc[slot]
            if kind in ("g", "u"):
                src = pg_in if kind == "g" else pu_in
                ssrc = sg_in if kind == "g" else su_in
                b.dma("sp", lambda E, src=src, a0=a0, a1=a1, pk=pk:
                      E.dma_start(out=pk, in_=src[a0 * 128:(a0 + 1) * 128,
                                                  a1 * 512:(a1 + 1) * 512]),
                      writes=[bpk])
                rep = bass.AP(ssrc.tensor, 8 * a0 * MLP + a1 * 512,
                              [[MLP, 8], [0, 16], [1, 512]])
                b.dma("sp", lambda E, rep=rep, sc=sc: E.dma_start(out=sc, in_=rep),
                      writes=[bsc])
            else:
                npc, ds = a0, a1
                b.dma("sp", lambda E, npc=npc, ds=ds, pk=pk:
                      E.dma_start(out=pk, in_=pd_in[npc * 128:(npc + 1) * 128,
                                                    ds * 512:(ds + 1) * 512]),
                      writes=[bpk])
                rep = bass.AP(sd_in.tensor, 8 * npc * 2048 + ds * 512,
                              [[2048, 8], [0, 16], [1, 512]])
                b.dma("sp", lambda E, rep=rep, sc=sc: E.dma_start(out=sc, in_=rep),
                      writes=[bsc])
            q = DQ[slot]
            Bq = B_dq[slot]
            for par in (0, 1):
                if par == 0:
                    b.op("dve", lambda E, pk=pk, n=q["n"]:
                         E.tensor_single_scalar(n, pk, 15, OP.bitwise_and),
                         reads=[bpk], writes=[Bq["n"]])
                else:
                    b.op("dve", lambda E, pk=pk, n=q["n"]:
                         E.tensor_single_scalar(n, pk, 4, OP.logical_shift_right),
                         reads=[bpk], writes=[Bq["n"]])
                b.op("dve", lambda E, n=q["n"], t1=q["t1"]:
                     E.tensor_scalar(t1, n, 8, 0x7E00, OP.is_ge, OP.mult),
                     reads=[Bq["n"]], writes=[Bq["t1"]])
                b.op("dve", lambda E, n=q["n"], b0=q["b0"]:
                     E.tensor_scalar(b0, n, 64, 0x3F00, OP.mult, OP.add),
                     reads=[Bq["n"]], writes=[Bq["b0"]])
                b.op("dve", lambda E, b0=q["b0"], t1=q["t1"], bi=q["bits"]:
                     E.tensor_tensor(bi, b0, t1, OP.add),
                     reads=[Bq["t1"], Bq["b0"]], writes=[Bq["bits"]])
                b.op("dve", lambda E, n=q["n"], m=q["m"]:
                     E.tensor_single_scalar(m, n, 7, OP.bitwise_and),
                     reads=[Bq["n"]], writes=[Bq["m"]])
                b.op("dve", lambda E, m=q["m"], g0=q["g0"]:
                     E.tensor_scalar(g0, m, 2.0 / 3.0, 1.0, OP.mult, OP.min),
                     reads=[Bq["m"]], writes=[Bq["g0"]])
                b.op("pool", lambda E, g0=q["g0"], sc=sc:
                     E.tensor_tensor(g0, g0, sc, OP.mult),
                     reads=[bsc], writes=[Bq["g0"]])
                b.op("pool", lambda E, g0=q["g0"], bi=q["bits"], ot=q["out"]:
                     E.tensor_tensor(ot, g0, bi.bitcast(BF16), OP.mult),
                     reads=[Bq["g0"], Bq["bits"]], writes=[Bq["out"]])
                ot = q["out"]
                if kind in ("g", "u"):
                    tgt = wg_s if kind == "g" else wu_s
                    Bt = B_wg if kind == "g" else B_wu
                    m_ = 2 * a0 + par
                    o_ap = bass.AP(tgt.tensor, (2 * a1) * 128 * 16 * 256 + m_ * 256,
                                   [[16 * 256, 128], [128 * 16 * 256, 2], [1, 256]])
                    b.dma("act", lambda E, o_ap=o_ap, ot=ot:
                          E.dma_start(out=o_ap, in_=ot.rearrange("p (w n) -> p w n", w=2)),
                          reads=[Bq["out"]], writes=[Bt[2 * a1], Bt[2 * a1 + 1]])
                else:
                    npc, ds = a0, a1
                    j = 2 * npc + par
                    sup, nt = j // 8, j % 8
                    o_ap = bass.AP(wd_s.tensor,
                                   (sup * 16 + ds * 4) * 128 * 8 * 128 + nt * 128,
                                   [[8 * 128, 128], [128 * 8 * 128, 4], [1, 128]])
                    b.dma("act", lambda E, o_ap=o_ap, ot=ot:
                          E.dma_start(out=o_ap, in_=ot.rearrange("p (t d) -> p t d", t=4)),
                          reads=[Bq["out"]], writes=[B_wd[sup]])

    if cfg.get("deq_upfront"):
        emit_dequant(len(deq_tasks))

    B_z = {}
    B_xmid = {(s, tg): nb() for s in range(NSEQ) for tg in range(4)}
    B_xn2 = {(s, tg): nb() for s in range(NSEQ) for tg in range(4)}
    B_out = nb()

    if cfg["do_attn"]:
        for seq in cfg["seqs"]:
            z_all = arA16
            B_z[seq] = nb("A")
            B_xw = nb("B")
            B_vtm = nb("B")
            B_ssb = [nb("B"), nb("B")]
            B_qk = [[nb("C"), nb("C")], [nb("C"), nb("C")]]
            B_wvst = nb("C")
            B_wq = nb("C")
            B_wk = nb("C")
            B_psb = [nb("D"), nb("D")]
            B_pT = [nb("D"), nb("D")]
            B_bqk = nb()
            B_colside = [nb(), nb()]
            xw = arB16[:, 0:8192]
            v_tm = arB16[:, 8192:12288]
            s_sb = [arB16[:, 12288 + sd * 2048:12288 + (sd + 1) * 2048] for sd in range(2)]
            qk_sb = [[arC16[:, (2 * sd) * 2048:(2 * sd) * 2048 + 2048],
                      arC16[:, (2 * sd + 1) * 2048:(2 * sd + 1) * 2048 + 2048]]
                     for sd in range(2)]
            wvst = arC16[:, 8192:12288]
            wqst = arC16[:, 12288:14336]
            wkst = arC16[:, 14336:16384]
            p_sb = [arD16[:, sd * 2048:(sd + 1) * 2048] for sd in range(2)]
            pT = [arD16[:, 4096 + sd * 2048:4096 + (sd + 1) * 2048] for sd in range(2)]
            for pair in range(8):
                heads = [h for h in cfg["heads"] if h // 2 == pair]
                if not heads:
                    continue
                b.dma("pool", lambda E, pair=pair, d=wvst:
                      E.dma_start(out=d, in_=wv_in[pair, :, :]), writes=[B_wvst])
                for ts_ in range(4):
                    t0 = ts_ * 512
                    b.dma("sp", lambda E, seq=seq, t0=t0, d=xw:
                          E.dma_start(out=d.rearrange("p (c t) -> p c t", c=16),
                                      in_=bass.AP(xln1_s.tensor, seq * 16 * 128 * S + t0,
                                                  [[S, 128], [128 * S, 16], [1, 512]])),
                          reads=[B_xln1[seq]], writes=[B_xw])
                    for tt in range(4):
                        ps, Bp = (p7, P7) if tt % 2 == 0 else (p6, P6)
                        fns = []
                        for c in range(16):
                            lhs = xw[:, c * 512 + tt * 128:c * 512 + tt * 128 + 128]
                            rhs = wvst[:, c * 256:(c + 1) * 256]
                            fns.append(lambda E, lhs=lhs, rhs=rhs, ps=ps, st=(c == 0):
                                       E.matmul(ps[:, 0:256], lhs, rhs, start=st, stop=False))
                        fns.append(lambda E, ps=ps, pair=pair:
                                   E.matmul(ps[:, 0:256], ones_rb[:],
                                            bvg_sb[0:1, pair * 256:(pair + 1) * 256],
                                            start=False, stop=True))
                        b.group("pe", fns, reads=[B_xw, B_wvst, c_bvg, c_ones], writes=[Bp])
                        gt = ts_ * 4 + tt
                        b.op("dve", lambda E, ps=ps, d=v_tm[:, gt * 256:(gt + 1) * 256]:
                             E.tensor_copy(d, ps[:, 0:256]), reads=[Bp], writes=[B_vtm])
                for sd, h_ in enumerate(heads):
                    b.dma("pool", lambda E, h_=h_, d=wqst:
                          E.dma_start(out=d, in_=wq_in[h_, :, :]), writes=[B_wq])
                    b.dma("pool", lambda E, h_=h_, d=wkst:
                          E.dma_start(out=d, in_=wk_in[h_, :, :]), writes=[B_wk])
                    b.dma("sp", lambda E, h_=h_:
                          E.dma_start(out=C_bq, in_=bq_in[h_:h_ + 1, :].rearrange("a p -> p a"),
                                      allow_slow_non_contiguous=True), writes=[B_bqk])
                    b.dma("sp", lambda E, h_=h_:
                          E.dma_start(out=C_bk, in_=bk_in[h_:h_ + 1, :].rearrange("a p -> p a"),
                                      allow_slow_non_contiguous=True), writes=[B_bqk])
                    for ts_ in range(4):
                        t0 = ts_ * 512
                        b.dma("sp", lambda E, seq=seq, t0=t0, d=xw:
                              E.dma_start(out=d.rearrange("p (c t) -> p c t", c=16),
                                          in_=bass.AP(xln1_s.tensor, seq * 16 * 128 * S + t0,
                                                      [[S, 128], [128 * S, 16], [1, 512]])),
                              reads=[B_xln1[seq]], writes=[B_xw])
                        for which, bcol, ps, Bp in ((0, C_bq, p7, P7), (1, C_bk, p6, P6)):
                            wst = wqst if which == 0 else wkst
                            bwst = B_wq if which == 0 else B_wk
                            fns = []
                            for c in range(16):
                                lhs = wst[:, c * 128:(c + 1) * 128]
                                rhs = xw[:, c * 512:(c + 1) * 512]
                                fns.append(lambda E, lhs=lhs, rhs=rhs, ps=ps,
                                           st=(c == 0), sp=(c == 15):
                                           E.matmul(ps[:, :], lhs, rhs, start=st, stop=sp))
                            b.group("pe", fns, reads=[B_xw, bwst], writes=[Bp])
                            b.op("dve", lambda E, ps=ps, d=qk_sb[sd][which][:, t0:t0 + 512], bc=bcol:
                                 E.tensor_single_scalar(d, ps[:, :], bc, OP.add),
                                 reads=[Bp, B_bqk], writes=[B_qk[sd][which]])
                for band in range(4):
                    for qs in range(4):
                        qi = band * 4 + qs
                        L = (qi + 1) * 128
                        nbank = (L + 511) // 512
                        for sd, h_ in enumerate(heads):
                            hh = h_ % 2
                            for j in range(nbank):
                                n = min(512, L - j * 512)
                                bank = 2 * sd + (j % 2)
                                b.group("pe", [lambda E, qi=qi, j=j, n=n, bank=bank, sd=sd:
                                               E.matmul(pbig[:, bank * 512:bank * 512 + n],
                                                        qk_sb[sd][0][:, qi * 128:qi * 128 + 128],
                                                        qk_sb[sd][1][:, j * 512:j * 512 + n],
                                                        start=True, stop=True)],
                                        reads=[B_qk[sd][0], B_qk[sd][1]], writes=[PB[bank]])
                                b.op("act", lambda E, j=j, n=n, bank=bank, sd=sd:
                                     E.activation(s_sb[sd][:, j * 512:j * 512 + n],
                                                  pbig[:, bank * 512:bank * 512 + n], AF.Copy),
                                     reads=[PB[bank]], writes=[B_ssb[sd]])
                            b.op("dve", lambda E, L=L, sd=sd:
                                 E.reduce_max(C_mx[sd], s_sb[sd][:, 0:L], axis=AX.X),
                                 reads=[B_ssb[sd]], writes=[B_colside[sd]])
                            b.op("dve", lambda E, L=L, sd=sd:
                                 E.tensor_tensor(s_sb[sd][:, L - 128:L], s_sb[sd][:, L - 128:L],
                                                 maskb[:], OP.add),
                                 reads=[c_mask], writes=[B_ssb[sd]])
                            b.op("dve", lambda E, sd=sd:
                                 E.tensor_scalar(C_nb[sd], C_mx[sd], -ASCALE, None, OP.mult),
                                 reads=[B_colside[sd]], writes=[B_colside[sd]])
                            b.op("act", lambda E, L=L, sd=sd:
                                 E.activation(p_sb[sd][:, 0:L], s_sb[sd][:, 0:L], AF.Exp,
                                              bias=C_nb[sd], scale=ASCALE, accum_out=C_sum[sd]),
                                 reads=[B_ssb[sd], B_colside[sd]],
                                 writes=[B_psb[sd], B_colside[sd]])
                            b.op("dve", lambda E, sd=sd: E.reciprocal(C_rs[sd], C_sum[sd]),
                                 reads=[B_colside[sd]], writes=[B_colside[sd]])
                            b.op("dve", lambda E, L=L, sd=sd:
                                 E.tensor_single_scalar(p_sb[sd][:, 0:L], p_sb[sd][:, 0:L],
                                                        C_rs[sd], OP.mult),
                                 reads=[B_colside[sd]], writes=[B_psb[sd]])
                            for j0 in range(0, qi + 1, 4):
                                jn = min(4, qi + 1 - j0)
                                fns = [lambda E, j=j0 + jj, jj=jj, sd=sd:
                                       E.transpose(p4b[:, sd * 512 + jj * 128:sd * 512 + (jj + 1) * 128],
                                                   p_sb[sd][:, j * 128:(j + 1) * 128], ident[:])
                                       for jj in range(jn)]
                                b.group("pe", fns, reads=[B_psb[sd], c_ident], writes=[P4])
                                b.op("dve", lambda E, j0=j0, jn=jn, sd=sd:
                                     E.tensor_copy(pT[sd][:, j0 * 128:(j0 + jn) * 128],
                                                   p4b[:, sd * 512:sd * 512 + jn * 128]),
                                     reads=[P4], writes=[B_pT[sd]])
                            zp, Bzp = (p5, P5) if sd == 0 else (p6, P6)
                            fns = []
                            for j in range(qi + 1):
                                lhs = v_tm[:, j * 256 + hh * 128:j * 256 + hh * 128 + 128]
                                rhs = pT[sd][:, j * 128:(j + 1) * 128]
                                fns.append(lambda E, lhs=lhs, rhs=rhs, zp=zp, qs=qs,
                                           st=(j == 0), sp=(j == qi):
                                           E.matmul(zp[:, qs * 128:qs * 128 + 128], lhs, rhs,
                                                    start=st, stop=sp))
                            b.group("pe", fns, reads=[B_vtm, B_pT[sd]], writes=[Bzp])
                    for sd, h_ in enumerate(heads):
                        zp, Bzp = (p5, P5) if sd == 0 else (p6, P6)
                        b.op("dve", lambda E, zp=zp, h_=h_,
                             d=z_all[:, h_ * 2048 + band * 512:h_ * 2048 + (band + 1) * 512]:
                             E.tensor_copy(d, zp[:, :]), reads=[Bzp], writes=[B_z[seq]])
                emit_dequant(40)

            if not cfg["do_wo"]:
                continue
            wsl_ap = [arD16[:, 8192 + sl_ * 2048:8192 + (sl_ + 1) * 2048] for sl_ in range(2)]
            B_wsl = [nb("D"), nb("D")]
            ao = [arB[:, i * 512:(i + 1) * 512] for i in range(16)]
            ao_bufs = [nb("B") for _ in range(16)]
            xm = [arC[:, i * 512:(i + 1) * 512] for i in range(16)]
            xm_bufs = [nb("C") for _ in range(16)]
            stg2 = [arB16[:, i * 512:(i + 1) * 512] for i in range(16)]
            stg2_bufs = [nb("B") for _ in range(16)]
            B_bo = nb()
            for tg in range(4):
                t0 = tg * 512
                for dt in range(16):
                    wsl = wsl_ap[dt % 2]
                    b.dma("pool", lambda E, dt=dt, d=wsl:
                          E.dma_start(out=d, in_=wo_in[dt, :, :]), writes=[B_wsl[dt % 2]])
                    b.dma("sp", lambda E, dt=dt:
                          E.dma_start(out=C_bo, in_=bo_in[dt:dt + 1, :].rearrange("a p -> p a"),
                                      allow_slow_non_contiguous=True), writes=[B_bo])
                    ps, Bp = (p5, P5) if dt % 2 == 0 else (p6, P6)
                    fns = []
                    for hc in range(16):
                        lhs = wsl[:, hc * 128:(hc + 1) * 128]
                        rhs = z_all[:, hc * 2048 + t0:hc * 2048 + t0 + 512]
                        fns.append(lambda E, lhs=lhs, rhs=rhs, ps=ps, st=(hc == 0), sp=(hc == 15):
                                   E.matmul(ps[:, :], lhs, rhs, start=st, stop=sp))
                    b.group("pe", fns, reads=[B_wsl[dt % 2], B_z[seq]], writes=[Bp])
                    b.op("dve", lambda E, ps=ps, d=ao[dt]:
                         E.tensor_single_scalar(d, ps[:, :], C_bo, OP.add),
                         reads=[Bp, B_bo], writes=[ao_bufs[dt]])
                ln_stats(None, wide=[(arB[:, i * 1024:(i + 1) * 1024],
                                      [ao_bufs[2 * i], ao_bufs[2 * i + 1]]) for i in range(8)])
                for c in range(16):
                    b.dma("sp", lambda E, c=c, t0=t0, seq=seq, d=xm[c]:
                          E.dma_start(out=d, in_=x_in[seq, c, :, t0:t0 + 512]),
                          writes=[xm_bufs[c]])
                bc6 = bass.AP(p6, 0, [[512, 128], [0, 4], [1, 512]])
                bc5 = bass.AP(p5, 0, [[512, 128], [0, 4], [1, 512]])
                for i in range(4):
                    a4 = arB[:, i * 2048:(i + 1) * 2048].rearrange("p (c t) -> p c t", c=4)
                    x4 = arC[:, i * 2048:(i + 1) * 2048].rearrange("p (c t) -> p c t", c=4)
                    rda = ao_bufs[4 * i:4 * i + 4]
                    rdx = xm_bufs[4 * i:4 * i + 4]
                    b.op("dve", lambda E, a4=a4, bc6=bc6: E.tensor_tensor(a4, a4, bc6, OP.mult),
                         reads=[P6], writes=rda)
                    b.op("dve", lambda E, a4=a4, bc5=bc5: E.tensor_tensor(a4, a4, bc5, OP.subtract),
                         reads=[P5], writes=rda)
                    b.op("dve", lambda E, a4=a4, x4=x4: E.tensor_tensor(x4, x4, a4, OP.add),
                         reads=rda, writes=rdx)
                    o_ap = bass.AP(xmid_s.tensor, (seq * 16 + 4 * i) * 128 * S + t0,
                                   [[S, 128], [128 * S, 4], [1, 512]])
                    b.dma("act", lambda E, x4=x4, o_ap=o_ap: E.dma_start(out=o_ap, in_=x4),
                          reads=rdx, writes=[B_xmid[(seq, tg)]])
                ln_stats(None, wide=[(arC[:, i * 1024:(i + 1) * 1024],
                                      [xm_bufs[2 * i], xm_bufs[2 * i + 1]]) for i in range(8)])
                for i in range(4):
                    x4 = arC[:, i * 2048:(i + 1) * 2048].rearrange("p (c t) -> p c t", c=4)
                    d4 = arB16[:, i * 2048:(i + 1) * 2048].rearrange("p (c t) -> p c t", c=4)
                    rdx = xm_bufs[4 * i:4 * i + 4]
                    wrs = stg2_bufs[4 * i:4 * i + 4]
                    b.op("dve", lambda E, x4=x4, d4=d4, bc6=bc6: E.tensor_tensor(d4, x4, bc6, OP.mult),
                         reads=rdx + [P6], writes=wrs)
                    b.op("dve", lambda E, d4=d4, bc5=bc5: E.tensor_tensor(d4, d4, bc5, OP.subtract),
                         reads=[P5], writes=wrs)
                    o_ap = bass.AP(xn2_s.tensor, (seq * 16 + 4 * i) * 128 * S + t0,
                                   [[S, 128], [128 * S, 4], [1, 512]])
                    b.dma("act", lambda E, d4=d4, o_ap=o_ap: E.dma_start(out=o_ap, in_=d4),
                          reads=wrs, writes=[B_xn2[(seq, tg)]])
                emit_dequant(10)

    if cfg["do_ffn"]:
        emit_dequant(len(deq_tasks))
        wdst = arD16[:, 16384:18432]
        xmst = arD[:, 9216:10240]
        outst = arD[:, 10240:11264]
        gel = arD[:, 11264:12288]
        B_aux = [nb(), nb()]
        xnw = arB16
        B_xnw = nb("B")
        inter = arC16
        inter_bufs = {(ip, nt, th): nb("C") for ip in range(2) for nt in range(8) for th in range(2)}
        gsl_ap = [arD16[:, s_ * 4096:(s_ + 1) * 4096] for s_ in range(2)]
        usl_ap = [arD16[:, 8192 + s_ * 4096:8192 + (s_ + 1) * 4096] for s_ in range(2)]
        B_gsl = [nb("D") for _ in range(2)]
        B_usl = [nb("D") for _ in range(2)]
        B_wdst = [nb("D"), nb("D")]
        B_xmst = [nb("D"), nb("D")]
        B_outst = [nb("D"), nb("D")]
        B_gel = nb("D")
        for g in cfg["ffn_groups"]:
            seq, half = g // 2, g % 2
            t0 = half * 1024
            b.dma("sp", lambda E, seq=seq, t0=t0:
                  E.dma_start(out=xnw[:, 0:16384].rearrange("p (c t) -> p c t", c=16),
                              in_=bass.AP(xn2_s.tensor, seq * 16 * 128 * S + t0,
                                          [[S, 128], [128 * S, 16], [1, 1024]])),
                  reads=[B_xn2[(seq, 2 * half)], B_xn2[(seq, 2 * half + 1)]],
                  writes=[B_xnw])
            acc = arA
            acc_bufs = [nb("A") for _ in range(16)]
            for sl in range(64):
                sup, sli = sl // 4, sl % 4
                gs_ = sl % 2
                gsl = gsl_ap[gs_]
                usl = usl_ap[gs_]
                b.dma("sp", lambda E, sl=sl, d=gsl:
                      E.dma_start(out=d, in_=bass.AP(wg_s.tensor, sl * 128 * 16 * 256,
                                                     [[16 * 256, 128], [1, 16 * 256]])),
                      reads=[B_wg[sl]], writes=[B_gsl[gs_]])
                b.dma("sp", lambda E, sl=sl, d=usl:
                      E.dma_start(out=d, in_=bass.AP(wu_s.tensor, sl * 128 * 16 * 256,
                                                     [[16 * 256, 128], [1, 16 * 256]])),
                      reads=[B_wu[sl]], writes=[B_usl[gs_]])
                ip = sup % 2
                for par in (0, 1):
                    j = 2 * sl + par
                    nt = j % 8
                    for th in range(2):
                        tof = th * 512
                        gps = pbig[:, th * 512:th * 512 + 512]
                        ups = pbig[:, 1024 + th * 512:1024 + th * 512 + 512]
                        Bg = PB[th]
                        Bu = PB[2 + th]
                        for wbase, ps_, bps, bw in ((gs_ * 4096, gps, Bg, B_gsl[gs_]),
                                                    (8192 + gs_ * 4096, ups, Bu, B_usl[gs_])):
                            fns = []
                            for c in range(16):
                                lhs = bass.AP(arD16, wbase + c * 256 + par,
                                              [[24576, 128], [2, 128]])
                                rhs = xnw[:, c * 1024 + tof:c * 1024 + tof + 512]
                                fns.append(lambda E, lhs=lhs, rhs=rhs, ps_=ps_,
                                           st=(c == 0), sp=(c == 15):
                                           E.matmul(ps_, lhs, rhs, start=st, stop=sp))
                            b.group("pe", fns, reads=[bw, B_xnw], writes=[bps])
                    gfull = pbig[:, 0:1024]
                    ufull = pbig[:, 1024:2048]
                    it = inter[:, ip * 8192 + nt * 1024:ip * 8192 + nt * 1024 + 1024]
                    Bits = [inter_bufs[(ip, nt, 0)], inter_bufs[(ip, nt, 1)]]
                    gbuf = aux8[:, 0:1024]
                    ubuf = aux8[:, 1024:2048]
                    tA = gel[:, 0:1024]
                    b.op("dve", lambda E, g_=gfull, gbuf=gbuf: E.tensor_copy(gbuf, g_),
                         reads=[PB[0], PB[1]], writes=[B_aux[0]])
                    b.op("act", lambda E, u=ufull, ubuf=ubuf: E.activation(ubuf, u, AF.Copy),
                         reads=[PB[2], PB[3]], writes=[B_aux[1]])
                    b.op("act", lambda E, gbuf=gbuf, tA=tA: E.activation(tA, gbuf, AF.Square),
                         reads=[B_aux[0]], writes=[B_gel])
                    b.op("act", lambda E, tA=tA:
                         E.activation(tA, tA, AF.Copy, scale=C0 * C1, bias=C0),
                         reads=[B_gel], writes=[B_gel])
                    b.op("pool", lambda E, tA=tA, gbuf=gbuf: E.tensor_tensor(tA, tA, gbuf, OP.mult),
                         reads=[B_aux[0]], writes=[B_gel])
                    b.op("pool", lambda E, gbuf=gbuf, ubuf=ubuf:
                         E.tensor_tensor(ubuf, gbuf, ubuf, OP.mult),
                         reads=[B_aux[0]], writes=[B_aux[1]])
                    b.op("act", lambda E, tA=tA: E.activation(tA, tA, AF.Tanh),
                         reads=[B_gel], writes=[B_gel])
                    b.op("act", lambda E, tA=tA: E.activation(tA, tA, AF.Copy, scale=0.5, bias=0.5),
                         reads=[B_gel], writes=[B_gel])
                    b.op("dve", lambda E, tA=tA, u=ubuf, it=it: E.tensor_tensor(it, tA, u, OP.mult),
                         reads=[B_gel, B_aux[1]], writes=Bits)
                if sli == 3 and sup % 2 == 1:
                    for dt in range(16):
                        for s2 in (sup - 1, sup):
                            wds = wdst[:, ((2 * dt + s2) % 2) * 1024:((2 * dt + s2) % 2) * 1024 + 1024]
                            Bw = B_wdst[(2 * dt + s2) % 2]
                            b.dma("sp", lambda E, s2=s2, dt=dt, d=wds:
                                  E.dma_start(out=d, in_=bass.AP(wd_s.tensor,
                                                                 (s2 * 16 + dt) * 128 * 8 * 128,
                                                                 [[8 * 128, 128], [1, 8 * 128]])),
                                  reads=[B_wd[s2]], writes=[Bw])
                            for th in range(2):
                                dps, Bdps = (p5, P5) if (dt + th) % 2 == 0 else (p6, P6)
                                fns = []
                                rd = [Bw]
                                ip2 = s2 % 2
                                for nt in range(8):
                                    lhs = wds[:, nt * 128:(nt + 1) * 128]
                                    rhs = inter[:, ip2 * 8192 + nt * 1024 + th * 512:
                                                ip2 * 8192 + nt * 1024 + th * 512 + 512]
                                    rd.append(inter_bufs[(ip2, nt, th)])
                                    fns.append(lambda E, lhs=lhs, rhs=rhs, dps=dps,
                                               st=(nt == 0 and s2 == sup - 1),
                                               sp=(nt == 7 and s2 == sup):
                                               E.matmul(dps[:, :], lhs, rhs, start=st, stop=sp))
                                b.group("pe", fns, reads=rd, writes=[Bdps])
                                if s2 == sup:
                                    a = acc[:, dt * 1024 + th * 512:dt * 1024 + th * 512 + 512]
                                    if sup == 1:
                                        b.op("dve", lambda E, a=a, dps=dps:
                                             E.tensor_copy(a, dps[:, :]),
                                             reads=[Bdps], writes=[acc_bufs[dt]])
                                    else:
                                        b.op("dve", lambda E, a=a, dps=dps:
                                             E.tensor_tensor(a, a, dps[:, :], OP.add),
                                             reads=[Bdps], writes=[acc_bufs[dt]])
            for th in range(2):
                tof = th * 512
                wide = [(bass.AP(arA, 2 * i * 1024 + tof, [[16384, 128], [1024, 2], [1, 512]]),
                         [acc_bufs[2 * i], acc_bufs[2 * i + 1]]) for i in range(8)]
                ln_stats(None, wide=wide)
                bc6 = bass.AP(p6, 0, [[512, 128], [0, 2], [1, 512]])
                bc5 = bass.AP(p5, 0, [[512, 128], [0, 2], [1, 512]])
                for i in range(8):
                    a2 = bass.AP(arA, 2 * i * 1024 + tof, [[16384, 128], [1024, 2], [1, 512]])
                    sl2 = i % 2
                    xm2 = xmst[:, 0:1024] if sl2 == 0 else outst[:, 0:1024]
                    Bxm = B_xmst[0] if sl2 == 0 else B_outst[0]
                    rda = [acc_bufs[2 * i], acc_bufs[2 * i + 1]]
                    i_ap = bass.AP(xmid_s.tensor, (seq * 16 + 2 * i) * 128 * S + t0 + tof,
                                   [[S, 128], [128 * S, 2], [1, 512]])
                    b.dma("sp", lambda E, xm2=xm2, i_ap=i_ap:
                          E.dma_start(out=xm2.rearrange("p (c t) -> p c t", c=2), in_=i_ap),
                          reads=[B_xmid[(seq, 2 * half + th)]], writes=[Bxm])
                    b.op("dve", lambda E, a2=a2, bc6=bc6: E.tensor_tensor(a2, a2, bc6, OP.mult),
                         reads=[P6], writes=rda)
                    b.op("dve", lambda E, a2=a2, bc5=bc5: E.tensor_tensor(a2, a2, bc5, OP.subtract),
                         reads=[P5], writes=rda)
                    b.op("dve", lambda E, a2=a2, xm2=xm2: E.tensor_tensor(
                        a2, a2, xm2.rearrange("p (c t) -> p c t", c=2), OP.add),
                         reads=[Bxm], writes=rda)
                    o_ap = bass.AP(out_d.tensor, (seq * 16 + 2 * i) * 128 * S + t0 + tof,
                                   [[S, 128], [128 * S, 2], [1, 512]])
                    b.dma("act", lambda E, a2=a2, o_ap=o_ap: E.dma_start(out=o_ap, in_=a2),
                          reads=rda, writes=[B_out])

    allb = Buf()
    allb.w = dict(b.cnt)
    for name in cfg.get("dump_sbuf", []):
        ap = {"q_sb": arC16[:, 0:2048], "k_sb": arC16[:, 2048:4096],
              "v_tm": arB16[:, 8192:12288], "p_sb": arD16[:, 0:2048],
              "pT": arD16[:, 4096:6144], "z_all": arA16[:, 0:32768],
              "xw": arB16[:, 0:8192], "acc": arA[:, :], "xnw": arB16[:, 0:16384],
              "rows": rows[:, :], "gelx": arD[:, 9216:12288],
              "inter": arC16[:, 0:16384]}[name]
        shp = list(ap.shape)
        dt_ = {2: BF16, 4: F32}[4 if name in ("acc", "rows", "gelx") else 2]
        dd = DT("dbg_" + name, shp, dt_, kind="ExternalOutput").ap()
        b.dma("sp", lambda E, ap=ap, dd=dd: E.dma_start(out=dd[:, :], in_=ap), reads=[allb])
    for name, src, dst in dbg:
        flat_src = bass.AP(src.tensor, 0,
                           [[src.shape[-1], int(np.prod(src.shape)) // src.shape[-1]],
                            [1, src.shape[-1]]])
        b.dma("sp", lambda E, s=flat_src, d=dst: E.dma_start(out=d, in_=s), reads=[allb])

    b.finalize()
    return nc


def _pi():
    pi = np.zeros(2048, np.int64)
    for c in range(8):
        for par in range(2):
            m = 2 * c + par
            pi[m * 128:(m + 1) * 128] = 256 * c + 2 * np.arange(128) + par
    return pi


def prep_inputs(L0_attn_x, L0_attn_W_Q, L0_attn_W_K, L0_attn_W_V, L0_attn_W_O,
                L0_attn_b_Q, L0_attn_b_K, L0_attn_b_V, L0_attn_b_O,
                L0_ffn_gate_W_packed, L0_ffn_gate_scales, L0_ffn_up_W_packed,
                L0_ffn_up_scales, L0_ffn_down_W_packed, L0_ffn_down_scales):
    pi = _pi()
    x = np.ascontiguousarray(
        np.asarray(L0_attn_x).transpose(0, 2, 1)[:, pi, :]).reshape(16, 16, 128, 2048)

    def head_img(w):
        wp = np.asarray(w)[:, pi, :]
        return np.ascontiguousarray(
            wp.reshape(16, 16, 128, 128).transpose(0, 2, 1, 3).reshape(16, 128, 2048))

    wq = head_img(L0_attn_W_Q)
    wk = head_img(L0_attn_W_K)
    wvp = np.asarray(L0_attn_W_V)[:, pi, :]
    wv = np.ascontiguousarray(
        wvp.transpose(1, 0, 2).reshape(16, 128, 2048)
        .reshape(16, 128, 8, 256).transpose(2, 1, 0, 3).reshape(8, 128, 4096))
    wo = np.asarray(L0_attn_W_O).reshape(2048, 2048)[:, pi]
    wo2 = np.ascontiguousarray(
        wo.reshape(16, 128, 16, 128).transpose(2, 1, 0, 3).reshape(16, 128, 2048))
    bq = np.ascontiguousarray(np.asarray(L0_attn_b_Q, dtype=np.float32))
    bk = np.ascontiguousarray(np.asarray(L0_attn_b_K, dtype=np.float32))
    bvg = np.ascontiguousarray(np.asarray(L0_attn_b_V, dtype=np.float32).reshape(1, 2048))
    bo = np.ascontiguousarray(np.asarray(L0_attn_b_O, dtype=np.float32)[pi].reshape(16, 128))

    def u8(a):
        return np.asarray(a).astype(np.uint8)

    pg = np.ascontiguousarray(u8(L0_ffn_gate_W_packed).T)
    pu = np.ascontiguousarray(u8(L0_ffn_up_W_packed).T)
    pd = np.ascontiguousarray(u8(L0_ffn_down_W_packed).T[:, pi])
    sg = np.ascontiguousarray(np.asarray(L0_ffn_gate_scales, dtype=np.float32).T)
    su = np.ascontiguousarray(np.asarray(L0_ffn_up_scales, dtype=np.float32).T)
    sd = np.ascontiguousarray(np.asarray(L0_ffn_down_scales, dtype=np.float32).T[:, pi])
    idn = np.eye(128, dtype=np.float32)
    msk = np.where(np.arange(128)[None, :] > np.arange(128)[:, None],
                   np.float32(NEG), np.float32(0.0)).astype(np.float32)
    shared = dict(wq=wq, wk=wk, wv=wv, wo=wo2, bq=bq, bk=bk, bvg=bvg, bo=bo,
                  pg=pg, pu=pu, pd=pd, sg=sg, su=su, sd=sd, idn=idn, msk=msk)
    in_maps = []
    for core in range(8):
        m = dict(shared)
        m["x"] = np.ascontiguousarray(x[2 * core:2 * core + 2])
        in_maps.append(m)
    return in_maps, pi


def kernel(**inputs):
    in_maps, pi = prep_inputs(**inputs)
    nc = build()
    res = run_bass_kernel_spmd(nc, in_maps, core_ids=list(range(8)))
    inv = np.argsort(pi)
    out = np.empty((16, 2048, 2048), np.float32)
    for core in range(8):
        o = np.asarray(res.results[core]["out"]).reshape(2, 2048, 2048)
        out[2 * core:2 * core + 2] = o[:, inv, :].transpose(0, 2, 1)
    return out


if __name__ == "__main__":
    n = build()
    print("built ok; instructions:", n.m.n_instructions() if hasattr(n, "m") else "?")


# version 09
# speedup vs baseline: 1.0241x; 1.0241x over previous
import numpy as np
import concourse.bass as bass
import concourse.mybir as mybir
from concourse.bass_utils import run_bass_kernel_spmd

F32 = mybir.dt.float32
BF16 = mybir.dt.bfloat16
U8 = mybir.dt.uint8
U16 = mybir.dt.uint16
AF = mybir.ActivationFunctionType
OP = mybir.AluOpType
AX = mybir.AxisListType

D = 2048
S = 2048
NSEQ = 2
MLP = 16384
EPS = 1e-5
ASCALE = 1.0 / float(np.sqrt(np.float32(128.0)))
C0 = 0.7978845608028654
C1 = 0.044715
NEG = -100000.0

CFG = {
    "seqs": [0, 1],
    "heads": list(range(16)),
    "ffn_groups": [0, 1, 2, 3],
    "do_attn": True,
    "do_wo": True,
    "do_ffn": True,
    "deq_upfront": False,
    "debug_outs": [],
}


def _merge(dst, src):
    for k, v in src.items():
        if dst.get(k, -1) < v:
            dst[k] = v


class Buf:
    __slots__ = ("w", "r")

    def __init__(self):
        self.w = {}
        self.r = {}


class Bld:
    ENG = ("pe", "dve", "act", "pool", "sp")

    NROT = 16

    def __init__(self, nc):
        self.nc = nc
        self.q = {e: [] for e in self.ENG}
        self.sem = {}
        names = ["Epe", "Edve", "Eact", "Epool"]
        for qn in ("Qsp", "Qact", "Qpool"):
            names += ["%s_%d" % (qn, i) for i in range(self.NROT)]
        for s in names:
            self.sem[s] = nc.alloc_semaphore(s)
        self.cnt = {s: 0 for s in self.sem}
        self.dma_i = {"Qsp": 0, "Qact": 0, "Qpool": 0}
        self.wm = {e: {} for e in self.ENG}

    def _waits(self, eng, dep):
        for s in sorted(dep):
            v = dep[s]
            if self.wm[eng].get(s, 0) < v:
                so = self.sem[s]
                self.q[eng].append(lambda E, so=so, v=v: E.wait_ge(so, v))
                self.wm[eng][s] = v

    def _dep(self, reads, writes):
        dep = {}
        for b in reads:
            _merge(dep, b.w)
        for b in writes:
            _merge(dep, b.w)
            _merge(dep, b.r)
        return dep

    def _fin(self, h, reads, writes):
        for b in reads:
            _merge(b.r, h)
        for b in writes:
            _merge(b.w, h)
            b.r = {}

    def op(self, eng, fn, reads=(), writes=()):
        self._waits(eng, self._dep(reads, writes))
        s = "E" + eng
        self.cnt[s] += 1
        so = self.sem[s]
        self.q[eng].append(lambda E, fn=fn, so=so: fn(E).then_inc(so, 1))
        h = {s: self.cnt[s]}
        self._fin(h, reads, writes)
        return h

    def group(self, eng, fns, reads=(), writes=()):
        self._waits(eng, self._dep(reads, writes))
        s = "E" + eng
        self.cnt[s] += 1
        so = self.sem[s]
        for fn in fns[:-1]:
            self.q[eng].append(lambda E, fn=fn: fn(E))
        self.q[eng].append(lambda E, fn=fns[-1], so=so: fn(E).then_inc(so, 1))
        h = {s: self.cnt[s]}
        self._fin(h, reads, writes)
        return h

    def dma(self, qeng, fn, reads=(), writes=()):
        dep = self._dep(reads, writes)
        qn = "Q" + qeng
        i = self.dma_i[qn]
        self.dma_i[qn] = i + 1
        s = "%s_%d" % (qn, i % self.NROT)
        if self.cnt[s] > 0:
            _merge(dep, {s: self.cnt[s]})
        self._waits(qeng, dep)
        self.cnt[s] += 16
        so = self.sem[s]
        self.q[qeng].append(lambda E, fn=fn, so=so: fn(E).then_inc(so, 16))
        h = {s: self.cnt[s]}
        self._fin(h, reads, writes)
        return h

    def finalize(self):
        for s in sorted(self.cnt):
            c = self.cnt[s]
            if c > 0:
                so = self.sem[s]
                self.q["sp"].append(lambda E, so=so, v=c: E.wait_ge(so, v))
        with self.nc.Block() as block:
            block.tensor(lambda E: ([f(E) for f in self.q["pe"]], None)[1])
            block.vector(lambda E: ([f(E) for f in self.q["dve"]], None)[1])
            block.scalar(lambda E: ([f(E) for f in self.q["act"]], None)[1])
            block.gpsimd(lambda E: ([f(E) for f in self.q["pool"]], None)[1])
            block.sync(lambda E: ([f(E) for f in self.q["sp"]], None)[1])


def build(cfg=None):
    if cfg is None:
        cfg = CFG
    nc = bass.Bass("TRN2", target_bir_lowering=False)
    DT = nc.dram_tensor
    x_in = DT("x", [NSEQ, 16, 128, S], F32, kind="ExternalInput").ap()
    wq_in = DT("wq", [16, 128, 2048], F32, kind="ExternalInput").ap()
    wk_in = DT("wk", [16, 128, 2048], F32, kind="ExternalInput").ap()
    wv_in = DT("wv", [8, 128, 4096], F32, kind="ExternalInput").ap()
    wo_in = DT("wo", [16, 128, 2048], F32, kind="ExternalInput").ap()
    bq_in = DT("bq", [16, 128], F32, kind="ExternalInput").ap()
    bk_in = DT("bk", [16, 128], F32, kind="ExternalInput").ap()
    bvg_in = DT("bvg", [1, 2048], F32, kind="ExternalInput").ap()
    bo_in = DT("bo", [16, 128], F32, kind="ExternalInput").ap()
    pg_in = DT("pg", [1024, MLP], U8, kind="ExternalInput").ap()
    pu_in = DT("pu", [1024, MLP], U8, kind="ExternalInput").ap()
    pd_in = DT("pd", [8192, 2048], U8, kind="ExternalInput").ap()
    sg_in = DT("sg", [64, MLP], F32, kind="ExternalInput").ap()
    su_in = DT("su", [64, MLP], F32, kind="ExternalInput").ap()
    sd_in = DT("sd", [512, 2048], F32, kind="ExternalInput").ap()
    idn_in = DT("idn", [128, 128], F32, kind="ExternalInput").ap()
    msk_in = DT("msk", [128, 128], F32, kind="ExternalInput").ap()

    xln1_s = DT("xln1_s", [NSEQ, 16, 128, S], BF16, kind="Internal").ap()
    xmid_s = DT("xmid_s", [NSEQ, 16, 128, S], F32, kind="Internal").ap()
    xn2_s = DT("xn2_s", [NSEQ, 16, 128, S], BF16, kind="Internal").ap()
    wg_s = DT("wg_s", [64, 128, 16, 256], BF16, kind="Internal").ap()
    wu_s = DT("wu_s", [64, 128, 16, 256], BF16, kind="Internal").ap()
    wd_s = DT("wd_s", [16, 16, 128, 8, 128], BF16, kind="Internal").ap()
    out_d = DT("out", [NSEQ, 16, 128, S], F32, kind="ExternalOutput").ap()

    dbg = []
    for name in cfg["debug_outs"]:
        src = {"xln1_s": xln1_s, "xmid_s": xmid_s, "xn2_s": xn2_s,
               "wg_s": wg_s, "wu_s": wu_s, "wd_s": wd_s}[name]
        n = int(np.prod(src.shape))
        dbg.append((name, src, DT("dbg_" + name, [n // src.shape[-1], src.shape[-1]],
                                  src.dtype, kind="ExternalOutput").ap()))

    SB = nc.alloc_sbuf_tensor
    arA = SB("arA", [128, 16384], F32)
    arB = SB("arB", [128, 8192], F32)
    arC = SB("arC", [128, 8192], F32)
    arD = SB("arD", [128, 12288], F32)
    arA16 = arA.bitcast(BF16)
    arB16 = arB.bitcast(BF16)
    arC16 = arC.bitcast(BF16)
    arD16 = arD.bitcast(BF16)
    arD8 = arD.bitcast(U8)
    arDu16 = arD.bitcast(U16)

    ident = SB("ident", [128, 128], BF16)
    mask = SB("mask", [128, 128], F32)
    ones_cb = SB("ones_cb", [128, 1], BF16)
    ones_r = SB("ones_r", [1, 128], F32)
    ones_rb = SB("ones_rb", [1, 128], BF16)
    rows = SB("rows", [1, 2560], F32)
    cols = SB("cols", [128, 12], F32)
    bvg_sb = SB("bvg_sb", [1, 2048], BF16)
    maskb = SB("maskb", [128, 128], BF16)
    aux8 = SB("aux8", [128, 2048], F32)
    x2t = SB("x2t", [128, 2048], BF16)
    xbt = SB("xbt", [128, 2048], BF16)

    pbig = nc.alloc_psum_tensor("pbig", [128, 2048], F32)
    p4b = nc.alloc_psum_tensor("p4b", [128, 1024], BF16)
    p5 = nc.alloc_psum_tensor("p5", [128, 512], F32)
    p6 = nc.alloc_psum_tensor("p6", [128, 512], F32)
    p7 = nc.alloc_psum_tensor("p7", [128, 512], F32)

    b = Bld(nc)
    AR = {"A": [], "B": [], "C": [], "D": []}

    def nb(ar=None, lo=0, hi=1 << 30):
        bf_ = Buf()
        if ar is not None:
            for (l2, h2, b2) in AR[ar]:
                if l2 < hi and lo < h2:
                    _merge(bf_.w, b2.w)
                    _merge(bf_.w, b2.r)
            AR[ar].append((lo, hi, bf_))
        return bf_

    PB = [nb() for _ in range(4)]
    P4, P5, P6, P7 = nb(), nb(), nb(), nb()
    B_rows, B_cols = nb(), nb()
    B_x2t = [nb(), nb()]
    B_xbt = [nb(), nb()]

    R_mean = rows[0:1, 0:512]
    R_msq = rows[0:1, 512:1024]
    R_var = rows[0:1, 1024:1536]
    R_rstd = rows[0:1, 1536:2048]
    R_nmr = rows[0:1, 2048:2560]
    R_tmp = rows[0:1, 512:1024]
    C_mx = [cols[:, 0:1], cols[:, 4:5]]
    C_nb = [cols[:, 1:2], cols[:, 5:6]]
    C_sum = [cols[:, 2:3], cols[:, 6:7]]
    C_rs = [cols[:, 3:4], cols[:, 7:8]]
    C_bq = cols[:, 8:9]
    C_bk = cols[:, 9:10]
    C_bo = cols[:, 10:11]

    c_ident = nb(); c_mask = nb(); c_ones = nb(); c_bvg = nb()
    b.dma("pool", lambda E: E.dma_start(out=ident[:], in_=idn_in[:, :]), writes=[c_ident])
    b.dma("sp", lambda E: E.dma_start(out=mask[:], in_=msk_in[:, :]), writes=[c_mask])
    b.op("dve", lambda E: E.tensor_copy(maskb[:], mask[:]), reads=[c_mask], writes=[c_mask])
    b.dma("pool", lambda E: E.dma_start(out=bvg_sb[:], in_=bvg_in[:, :]), writes=[c_bvg])
    b.op("pool", lambda E: E.memset(ones_cb[:], 1.0), writes=[c_ones])
    b.op("pool", lambda E: E.memset(ones_r[:], 1.0), writes=[c_ones])
    b.op("pool", lambda E: E.memset(ones_rb[:], 1.0), writes=[c_ones])

    def ln_stats(tiles, wide=None):
        for i in range(0, 16, 2):
            sl = (i // 2) % 2
            xb = xbt[:, sl * 1024:sl * 1024 + 1024]
            x2 = x2t[:, sl * 1024:sl * 1024 + 1024]
            if wide is not None:
                wap, wbufs = wide[i // 2]
                b.op("dve", lambda E, tl=wap, xb=xb: E.tensor_copy(xb, tl),
                     reads=wbufs, writes=[B_xbt[sl]])
                b.op("act", lambda E, tl=wap, x2=x2: E.activation(x2, tl, AF.Square),
                     reads=wbufs, writes=[B_x2t[sl]])
            else:
                for k in range(2):
                    tl, tb = tiles[i + k]
                    b.op("dve", lambda E, tl=tl, d=xb[:, k * 512:k * 512 + 512]:
                         E.tensor_copy(d, tl), reads=[tb], writes=[B_xbt[sl]])
                    b.op("act", lambda E, tl=tl, d=x2[:, k * 512:k * 512 + 512]:
                         E.activation(d, tl, AF.Square), reads=[tb], writes=[B_x2t[sl]])
            st, sp = (i == 0), (i == 14)
            for k in range(2):
                stk = st and k == 0
                spk = sp and k == 1
                b.group("pe", [lambda E, xb=xb[:, k * 512:k * 512 + 512], st=stk, sp=spk:
                               E.matmul(p7[0:1, :], ones_cb[:], xb, start=st, stop=sp)],
                        reads=[B_xbt[sl], c_ones], writes=[P7])
                b.group("pe", [lambda E, x2=x2[:, k * 512:k * 512 + 512], st=stk, sp=spk:
                               E.matmul(p5[0:1, :], ones_cb[:], x2, start=st, stop=sp)],
                        reads=[B_x2t[sl], c_ones], writes=[P5])
        b.op("dve", lambda E: E.tensor_scalar(R_mean, p7[0:1, :], 1.0 / D, None, OP.mult),
             reads=[P7], writes=[B_rows])
        b.op("dve", lambda E: E.tensor_scalar(R_msq, p5[0:1, :], 1.0 / D, None, OP.mult),
             reads=[P5], writes=[B_rows])
        b.op("dve", lambda E: E.tensor_mul(R_var, R_mean, R_mean), reads=[B_rows], writes=[B_rows])
        b.op("dve", lambda E: E.scalar_tensor_tensor(R_var, R_msq, EPS, R_var, OP.add, OP.subtract),
             reads=[B_rows], writes=[B_rows])
        b.op("act", lambda E: E.activation(R_tmp, R_var, AF.Ln), reads=[B_rows], writes=[B_rows])
        b.op("act", lambda E: E.activation(R_rstd, R_tmp, AF.Exp, scale=-0.5),
             reads=[B_rows], writes=[B_rows])
        b.op("dve", lambda E: E.tensor_mul(R_nmr, R_mean, R_rstd), reads=[B_rows], writes=[B_rows])
        b.group("pe", [lambda E: E.matmul(p6[:, :], ones_r[:], R_rstd, start=True, stop=True)],
                reads=[B_rows, c_ones], writes=[P6])
        b.group("pe", [lambda E: E.matmul(p5[:, :], ones_r[:], R_nmr, start=True, stop=True)],
                reads=[B_rows, c_ones], writes=[P5])

    B_xln1 = {s: nb() for s in range(NSEQ)}
    for seq in cfg["seqs"]:
        xw = [arA[:, i * 512:(i + 1) * 512] for i in range(16)]
        xw_bufs = [nb("A") for _ in range(16)]
        stg = [arA16[:, 16384 + i * 512:16384 + (i + 1) * 512] for i in range(16)]
        stg_bufs = [nb("A") for _ in range(16)]
        for tg in range(4):
            t0 = tg * 512
            for c in range(16):
                b.dma("sp", lambda E, c=c, t0=t0, seq=seq, d=xw[c]:
                      E.dma_start(out=d, in_=x_in[seq, c, :, t0:t0 + 512]),
                      writes=[xw_bufs[c]])
            ln_stats(None, wide=[(arA[:, i * 1024:(i + 1) * 1024],
                                  [xw_bufs[2 * i], xw_bufs[2 * i + 1]]) for i in range(8)])
            bc6 = bass.AP(p6, 0, [[512, 128], [0, 4], [1, 512]])
            bc5 = bass.AP(p5, 0, [[512, 128], [0, 4], [1, 512]])
            for i in range(4):
                src = arA[:, i * 2048:(i + 1) * 2048]
                d = arA16[:, 16384 + i * 2048:16384 + (i + 1) * 2048]
                rd = xw_bufs[4 * i:4 * i + 4]
                wr = stg_bufs[4 * i:4 * i + 4]
                b.op("dve", lambda E, s_=src, d=d, bc6=bc6: E.tensor_tensor(
                    d.rearrange("p (c t) -> p c t", c=4), s_.rearrange("p (c t) -> p c t", c=4),
                    bc6, OP.mult), reads=rd + [P6], writes=wr)
                b.op("dve", lambda E, d=d, bc5=bc5: E.tensor_tensor(
                    d.rearrange("p (c t) -> p c t", c=4), d.rearrange("p (c t) -> p c t", c=4),
                    bc5, OP.subtract), reads=[P5], writes=wr)
                o_ap = bass.AP(xln1_s.tensor, (seq * 16 + 4 * i) * 128 * S + t0,
                               [[S, 128], [128 * S, 4], [1, 512]])
                b.dma("act", lambda E, d=d, o_ap=o_ap:
                      E.dma_start(out=o_ap, in_=d.rearrange("p (c t) -> p c t", c=4)),
                      reads=wr, writes=[B_xln1[seq]])

    deq_tasks = []
    if cfg["do_ffn"]:
        for kpc in range(8):
            for ns in range(32):
                deq_tasks.append(("g", kpc, ns))
        for kpc in range(8):
            for ns in range(32):
                deq_tasks.append(("u", kpc, ns))
        for npc in range(64):
            for ds in range(4):
                deq_tasks.append(("d", npc, ds))
    B_wg = {i: nb() for i in range(64)}
    B_wu = {i: nb() for i in range(64)}
    B_wd = {i: nb() for i in range(16)}

    dq_pk = [arD8[:, 24576 + i * 512:24576 + (i + 1) * 512] for i in range(2)]
    dq_sc = [arD[:, 6400 + i * 512:6400 + (i + 1) * 512] for i in range(2)]
    DQ = {}
    for si in range(2):
        base = 29696 + si * 9216
        DQ[si] = {
            "n": arD8[:, base:base + 512],
            "m": arD8[:, base + 512:base + 1024],
            "out": arD16[:, (base + 1024) // 2:(base + 1024) // 2 + 512],
            "t1": arDu16[:, (base + 2048) // 2:(base + 2048) // 2 + 512],
            "b0": arDu16[:, (base + 3072) // 2:(base + 3072) // 2 + 512],
            "bits": arDu16[:, (base + 4096) // 2:(base + 4096) // 2 + 512],
            "g0": arD[:, (base + 5120) // 4:(base + 5120) // 4 + 512],
        }
    B_dq_pk = [nb("D"), nb("D")]
    B_dq_sc = [nb("D"), nb("D")]
    B_dq = [{k: nb("D") for k in ("n", "m", "t1", "b0", "bits", "g0", "out")}
            for _ in range(2)]
    dq_state = {"i": 0, "slot": 0}

    def emit_dequant(n_tiles):
        for _ in range(n_tiles):
            i = dq_state["i"]
            if i >= len(deq_tasks):
                return
            dq_state["i"] += 1
            kind, a0, a1 = deq_tasks[i]
            slot = dq_state["slot"]
            dq_state["slot"] ^= 1
            pk = dq_pk[slot]
            sc = dq_sc[slot]
            bpk = B_dq_pk[slot]
            bsc = B_dq_sc[slot]
            if kind in ("g", "u"):
                src = pg_in if kind == "g" else pu_in
                ssrc = sg_in if kind == "g" else su_in
                b.dma("sp", lambda E, src=src, a0=a0, a1=a1, pk=pk:
                      E.dma_start(out=pk, in_=src[a0 * 128:(a0 + 1) * 128,
                                                  a1 * 512:(a1 + 1) * 512]),
                      writes=[bpk])
                rep = bass.AP(ssrc.tensor, 8 * a0 * MLP + a1 * 512,
                              [[MLP, 8], [0, 16], [1, 512]])
                b.dma("sp", lambda E, rep=rep, sc=sc: E.dma_start(out=sc, in_=rep),
                      writes=[bsc])
            else:
                npc, ds = a0, a1
                b.dma("sp", lambda E, npc=npc, ds=ds, pk=pk:
                      E.dma_start(out=pk, in_=pd_in[npc * 128:(npc + 1) * 128,
                                                    ds * 512:(ds + 1) * 512]),
                      writes=[bpk])
                rep = bass.AP(sd_in.tensor, 8 * npc * 2048 + ds * 512,
                              [[2048, 8], [0, 16], [1, 512]])
                b.dma("sp", lambda E, rep=rep, sc=sc: E.dma_start(out=sc, in_=rep),
                      writes=[bsc])
            q = DQ[slot]
            Bq = B_dq[slot]
            for par in (0, 1):
                if par == 0:
                    b.op("dve", lambda E, pk=pk, n=q["n"]:
                         E.tensor_single_scalar(n, pk, 15, OP.bitwise_and),
                         reads=[bpk], writes=[Bq["n"]])
                else:
                    b.op("dve", lambda E, pk=pk, n=q["n"]:
                         E.tensor_single_scalar(n, pk, 4, OP.logical_shift_right),
                         reads=[bpk], writes=[Bq["n"]])
                b.op("dve", lambda E, n=q["n"], t1=q["t1"]:
                     E.tensor_scalar(t1, n, 8, 0x7E00, OP.is_ge, OP.mult),
                     reads=[Bq["n"]], writes=[Bq["t1"]])
                b.op("dve", lambda E, n=q["n"], b0=q["b0"]:
                     E.tensor_scalar(b0, n, 64, 0x3F00, OP.mult, OP.add),
                     reads=[Bq["n"]], writes=[Bq["b0"]])
                b.op("dve", lambda E, b0=q["b0"], t1=q["t1"], bi=q["bits"]:
                     E.tensor_tensor(bi, b0, t1, OP.add),
                     reads=[Bq["t1"], Bq["b0"]], writes=[Bq["bits"]])
                b.op("dve", lambda E, n=q["n"], m=q["m"]:
                     E.tensor_single_scalar(m, n, 7, OP.bitwise_and),
                     reads=[Bq["n"]], writes=[Bq["m"]])
                b.op("dve", lambda E, m=q["m"], g0=q["g0"]:
                     E.tensor_scalar(g0, m, 2.0 / 3.0, 1.0, OP.mult, OP.min),
                     reads=[Bq["m"]], writes=[Bq["g0"]])
                b.op("pool", lambda E, g0=q["g0"], sc=sc:
                     E.tensor_tensor(g0, g0, sc, OP.mult),
                     reads=[bsc], writes=[Bq["g0"]])
                b.op("pool", lambda E, g0=q["g0"], bi=q["bits"], ot=q["out"]:
                     E.tensor_tensor(ot, g0, bi.bitcast(BF16), OP.mult),
                     reads=[Bq["g0"], Bq["bits"]], writes=[Bq["out"]])
                ot = q["out"]
                if kind in ("g", "u"):
                    tgt = wg_s if kind == "g" else wu_s
                    Bt = B_wg if kind == "g" else B_wu
                    m_ = 2 * a0 + par
                    o_ap = bass.AP(tgt.tensor, (2 * a1) * 128 * 16 * 256 + m_ * 256,
                                   [[16 * 256, 128], [128 * 16 * 256, 2], [1, 256]])
                    b.dma("act", lambda E, o_ap=o_ap, ot=ot:
                          E.dma_start(out=o_ap, in_=ot.rearrange("p (w n) -> p w n", w=2)),
                          reads=[Bq["out"]], writes=[Bt[2 * a1], Bt[2 * a1 + 1]])
                else:
                    npc, ds = a0, a1
                    j = 2 * npc + par
                    sup, nt = j // 8, j % 8
                    o_ap = bass.AP(wd_s.tensor,
                                   (sup * 16 + ds * 4) * 128 * 8 * 128 + nt * 128,
                                   [[8 * 128, 128], [128 * 8 * 128, 4], [1, 128]])
                    b.dma("act", lambda E, o_ap=o_ap, ot=ot:
                          E.dma_start(out=o_ap, in_=ot.rearrange("p (t d) -> p t d", t=4)),
                          reads=[Bq["out"]], writes=[B_wd[sup]])

    if cfg.get("deq_upfront"):
        emit_dequant(len(deq_tasks))

    B_z = {}
    B_xmid = {(s, tg): nb() for s in range(NSEQ) for tg in range(4)}
    B_xn2 = {(s, tg): nb() for s in range(NSEQ) for tg in range(4)}
    B_out = nb()

    if cfg["do_attn"]:
        for seq in cfg["seqs"]:
            z_all = arA16
            B_z[seq] = nb("A")
            B_xw = nb("B")
            B_vtm = nb("B")
            B_ssb = [nb("B"), nb("B")]
            B_qk = [[nb("C"), nb("C")], [nb("C"), nb("C")]]
            B_wvst = nb("C")
            B_wq = nb("C")
            B_wk = nb("C")
            B_psb = [nb("D"), nb("D")]
            B_pT = [nb("D"), nb("D")]
            B_bqk = nb()
            B_colside = [nb(), nb()]
            xw = arB16[:, 0:8192]
            v_tm = arB16[:, 8192:12288]
            s_sb = [arB16[:, 12288 + sd * 2048:12288 + (sd + 1) * 2048] for sd in range(2)]
            qk_sb = [[arC16[:, (2 * sd) * 2048:(2 * sd) * 2048 + 2048],
                      arC16[:, (2 * sd + 1) * 2048:(2 * sd + 1) * 2048 + 2048]]
                     for sd in range(2)]
            wvst = arC16[:, 8192:12288]
            wqst = arC16[:, 12288:14336]
            wkst = arC16[:, 14336:16384]
            p_sb = [arD16[:, sd * 2048:(sd + 1) * 2048] for sd in range(2)]
            pT = [arD16[:, 4096 + sd * 2048:4096 + (sd + 1) * 2048] for sd in range(2)]
            for pair in range(8):
                heads = [h for h in cfg["heads"] if h // 2 == pair]
                if not heads:
                    continue
                b.dma("pool", lambda E, pair=pair, d=wvst:
                      E.dma_start(out=d, in_=wv_in[pair, :, :]), writes=[B_wvst])
                for ts_ in range(4):
                    t0 = ts_ * 512
                    b.dma("sp", lambda E, seq=seq, t0=t0, d=xw:
                          E.dma_start(out=d.rearrange("p (c t) -> p c t", c=16),
                                      in_=bass.AP(xln1_s.tensor, seq * 16 * 128 * S + t0,
                                                  [[S, 128], [128 * S, 16], [1, 512]])),
                          reads=[B_xln1[seq]], writes=[B_xw])
                    for tt in range(4):
                        ps, Bp = (p7, P7) if tt % 2 == 0 else (p6, P6)
                        fns = []
                        for c in range(16):
                            lhs = xw[:, c * 512 + tt * 128:c * 512 + tt * 128 + 128]
                            rhs = wvst[:, c * 256:(c + 1) * 256]
                            fns.append(lambda E, lhs=lhs, rhs=rhs, ps=ps, st=(c == 0):
                                       E.matmul(ps[:, 0:256], lhs, rhs, start=st, stop=False))
                        fns.append(lambda E, ps=ps, pair=pair:
                                   E.matmul(ps[:, 0:256], ones_rb[:],
                                            bvg_sb[0:1, pair * 256:(pair + 1) * 256],
                                            start=False, stop=True))
                        b.group("pe", fns, reads=[B_xw, B_wvst, c_bvg, c_ones], writes=[Bp])
                        gt = ts_ * 4 + tt
                        b.op("dve", lambda E, ps=ps, d=v_tm[:, gt * 256:(gt + 1) * 256]:
                             E.tensor_copy(d, ps[:, 0:256]), reads=[Bp], writes=[B_vtm])
                for sd, h_ in enumerate(heads):
                    b.dma("pool", lambda E, h_=h_, d=wqst:
                          E.dma_start(out=d, in_=wq_in[h_, :, :]), writes=[B_wq])
                    b.dma("pool", lambda E, h_=h_, d=wkst:
                          E.dma_start(out=d, in_=wk_in[h_, :, :]), writes=[B_wk])
                    b.dma("sp", lambda E, h_=h_:
                          E.dma_start(out=C_bq, in_=bq_in[h_:h_ + 1, :].rearrange("a p -> p a"),
                                      allow_slow_non_contiguous=True), writes=[B_bqk])
                    b.dma("sp", lambda E, h_=h_:
                          E.dma_start(out=C_bk, in_=bk_in[h_:h_ + 1, :].rearrange("a p -> p a"),
                                      allow_slow_non_contiguous=True), writes=[B_bqk])
                    for ts_ in range(4):
                        t0 = ts_ * 512
                        b.dma("sp", lambda E, seq=seq, t0=t0, d=xw:
                              E.dma_start(out=d.rearrange("p (c t) -> p c t", c=16),
                                          in_=bass.AP(xln1_s.tensor, seq * 16 * 128 * S + t0,
                                                      [[S, 128], [128 * S, 16], [1, 512]])),
                              reads=[B_xln1[seq]], writes=[B_xw])
                        for which, bcol, ps, Bp in ((0, C_bq, p7, P7), (1, C_bk, p6, P6)):
                            wst = wqst if which == 0 else wkst
                            bwst = B_wq if which == 0 else B_wk
                            fns = []
                            for c in range(16):
                                lhs = wst[:, c * 128:(c + 1) * 128]
                                rhs = xw[:, c * 512:(c + 1) * 512]
                                fns.append(lambda E, lhs=lhs, rhs=rhs, ps=ps,
                                           st=(c == 0), sp=(c == 15):
                                           E.matmul(ps[:, :], lhs, rhs, start=st, stop=sp))
                            b.group("pe", fns, reads=[B_xw, bwst], writes=[Bp])
                            b.op("dve", lambda E, ps=ps, d=qk_sb[sd][which][:, t0:t0 + 512], bc=bcol:
                                 E.tensor_single_scalar(d, ps[:, :], bc, OP.add),
                                 reads=[Bp, B_bqk], writes=[B_qk[sd][which]])
                for band in range(4):
                    for qs in range(4):
                        qi = band * 4 + qs
                        L = (qi + 1) * 128
                        nbank = (L + 511) // 512
                        for sd, h_ in enumerate(heads):
                            for j in range(nbank):
                                n = min(512, L - j * 512)
                                bank = 2 * sd + (j % 2)
                                b.group("pe", [lambda E, qi=qi, j=j, n=n, bank=bank, sd=sd:
                                               E.matmul(pbig[:, bank * 512:bank * 512 + n],
                                                        qk_sb[sd][0][:, qi * 128:qi * 128 + 128],
                                                        qk_sb[sd][1][:, j * 512:j * 512 + n],
                                                        start=True, stop=True)],
                                        reads=[B_qk[sd][0], B_qk[sd][1]], writes=[PB[bank]])
                                b.op("act", lambda E, j=j, n=n, bank=bank, sd=sd:
                                     E.activation(s_sb[sd][:, j * 512:j * 512 + n],
                                                  pbig[:, bank * 512:bank * 512 + n], AF.Copy),
                                     reads=[PB[bank]], writes=[B_ssb[sd]])
                            b.op("dve", lambda E, L=L, sd=sd:
                                 E.reduce_max(C_mx[sd], s_sb[sd][:, 0:L], axis=AX.X),
                                 reads=[B_ssb[sd]], writes=[B_colside[sd]])
                            b.op("dve", lambda E, L=L, sd=sd:
                                 E.tensor_tensor(s_sb[sd][:, L - 128:L], s_sb[sd][:, L - 128:L],
                                                 maskb[:], OP.add),
                                 reads=[c_mask], writes=[B_ssb[sd]])
                            b.op("dve", lambda E, sd=sd:
                                 E.tensor_scalar(C_nb[sd], C_mx[sd], -ASCALE, None, OP.mult),
                                 reads=[B_colside[sd]], writes=[B_colside[sd]])
                            b.op("act", lambda E, L=L, sd=sd:
                                 E.activation(p_sb[sd][:, 0:L], s_sb[sd][:, 0:L], AF.Exp,
                                              bias=C_nb[sd], scale=ASCALE, accum_out=C_sum[sd]),
                                 reads=[B_ssb[sd], B_colside[sd]],
                                 writes=[B_psb[sd], B_colside[sd]])
                            b.op("dve", lambda E, sd=sd: E.reciprocal(C_rs[sd], C_sum[sd]),
                                 reads=[B_colside[sd]], writes=[B_colside[sd]])
                            b.op("dve", lambda E, L=L, sd=sd:
                                 E.tensor_single_scalar(p_sb[sd][:, 0:L], p_sb[sd][:, 0:L],
                                                        C_rs[sd], OP.mult),
                                 reads=[B_colside[sd]], writes=[B_psb[sd]])
                        for sd, h_ in enumerate(heads):
                            hh = h_ % 2
                            for j0 in range(0, qi + 1, 4):
                                jn = min(4, qi + 1 - j0)
                                fns = [lambda E, j=j0 + jj, jj=jj, sd=sd:
                                       E.transpose(p4b[:, sd * 512 + jj * 128:sd * 512 + (jj + 1) * 128],
                                                   p_sb[sd][:, j * 128:(j + 1) * 128], ident[:])
                                       for jj in range(jn)]
                                b.group("pe", fns, reads=[B_psb[sd], c_ident], writes=[P4])
                                b.op("dve", lambda E, j0=j0, jn=jn, sd=sd:
                                     E.tensor_copy(pT[sd][:, j0 * 128:(j0 + jn) * 128],
                                                   p4b[:, sd * 512:sd * 512 + jn * 128]),
                                     reads=[P4], writes=[B_pT[sd]])
                            zp, Bzp = (p5, P5) if sd == 0 else (p6, P6)
                            fns = []
                            for j in range(qi + 1):
                                lhs = v_tm[:, j * 256 + hh * 128:j * 256 + hh * 128 + 128]
                                rhs = pT[sd][:, j * 128:(j + 1) * 128]
                                fns.append(lambda E, lhs=lhs, rhs=rhs, zp=zp, qs=qs,
                                           st=(j == 0), sp=(j == qi):
                                           E.matmul(zp[:, qs * 128:qs * 128 + 128], lhs, rhs,
                                                    start=st, stop=sp))
                            b.group("pe", fns, reads=[B_vtm, B_pT[sd]], writes=[Bzp])
                    for sd, h_ in enumerate(heads):
                        zp, Bzp = (p5, P5) if sd == 0 else (p6, P6)
                        b.op("dve", lambda E, zp=zp, h_=h_,
                             d=z_all[:, h_ * 2048 + band * 512:h_ * 2048 + (band + 1) * 512]:
                             E.tensor_copy(d, zp[:, :]), reads=[Bzp], writes=[B_z[seq]])
                emit_dequant(40)

            if not cfg["do_wo"]:
                continue
            wsl_ap = [arD16[:, 8192 + sl_ * 2048:8192 + (sl_ + 1) * 2048] for sl_ in range(2)]
            B_wsl = [nb("D"), nb("D")]
            ao = [arB[:, i * 512:(i + 1) * 512] for i in range(16)]
            ao_bufs = [nb("B") for _ in range(16)]
            xm = [arC[:, i * 512:(i + 1) * 512] for i in range(16)]
            xm_bufs = [nb("C") for _ in range(16)]
            stg2 = [arB16[:, i * 512:(i + 1) * 512] for i in range(16)]
            stg2_bufs = [nb("B") for _ in range(16)]
            B_bo = nb()
            for tg in range(4):
                t0 = tg * 512
                for dt in range(16):
                    wsl = wsl_ap[dt % 2]
                    b.dma("pool", lambda E, dt=dt, d=wsl:
                          E.dma_start(out=d, in_=wo_in[dt, :, :]), writes=[B_wsl[dt % 2]])
                    b.dma("sp", lambda E, dt=dt:
                          E.dma_start(out=C_bo, in_=bo_in[dt:dt + 1, :].rearrange("a p -> p a"),
                                      allow_slow_non_contiguous=True), writes=[B_bo])
                    ps, Bp = (p5, P5) if dt % 2 == 0 else (p6, P6)
                    fns = []
                    for hc in range(16):
                        lhs = wsl[:, hc * 128:(hc + 1) * 128]
                        rhs = z_all[:, hc * 2048 + t0:hc * 2048 + t0 + 512]
                        fns.append(lambda E, lhs=lhs, rhs=rhs, ps=ps, st=(hc == 0), sp=(hc == 15):
                                   E.matmul(ps[:, :], lhs, rhs, start=st, stop=sp))
                    b.group("pe", fns, reads=[B_wsl[dt % 2], B_z[seq]], writes=[Bp])
                    b.op("dve", lambda E, ps=ps, d=ao[dt]:
                         E.tensor_single_scalar(d, ps[:, :], C_bo, OP.add),
                         reads=[Bp, B_bo], writes=[ao_bufs[dt]])
                ln_stats(None, wide=[(arB[:, i * 1024:(i + 1) * 1024],
                                      [ao_bufs[2 * i], ao_bufs[2 * i + 1]]) for i in range(8)])
                for c in range(16):
                    b.dma("sp", lambda E, c=c, t0=t0, seq=seq, d=xm[c]:
                          E.dma_start(out=d, in_=x_in[seq, c, :, t0:t0 + 512]),
                          writes=[xm_bufs[c]])
                bc6 = bass.AP(p6, 0, [[512, 128], [0, 4], [1, 512]])
                bc5 = bass.AP(p5, 0, [[512, 128], [0, 4], [1, 512]])
                for i in range(4):
                    a4 = arB[:, i * 2048:(i + 1) * 2048].rearrange("p (c t) -> p c t", c=4)
                    x4 = arC[:, i * 2048:(i + 1) * 2048].rearrange("p (c t) -> p c t", c=4)
                    rda = ao_bufs[4 * i:4 * i + 4]
                    rdx = xm_bufs[4 * i:4 * i + 4]
                    b.op("dve", lambda E, a4=a4, bc6=bc6: E.tensor_tensor(a4, a4, bc6, OP.mult),
                         reads=[P6], writes=rda)
                    b.op("dve", lambda E, a4=a4, bc5=bc5: E.tensor_tensor(a4, a4, bc5, OP.subtract),
                         reads=[P5], writes=rda)
                    b.op("dve", lambda E, a4=a4, x4=x4: E.tensor_tensor(x4, x4, a4, OP.add),
                         reads=rda, writes=rdx)
                    o_ap = bass.AP(xmid_s.tensor, (seq * 16 + 4 * i) * 128 * S + t0,
                                   [[S, 128], [128 * S, 4], [1, 512]])
                    b.dma("act", lambda E, x4=x4, o_ap=o_ap: E.dma_start(out=o_ap, in_=x4),
                          reads=rdx, writes=[B_xmid[(seq, tg)]])
                ln_stats(None, wide=[(arC[:, i * 1024:(i + 1) * 1024],
                                      [xm_bufs[2 * i], xm_bufs[2 * i + 1]]) for i in range(8)])
                for i in range(4):
                    x4 = arC[:, i * 2048:(i + 1) * 2048].rearrange("p (c t) -> p c t", c=4)
                    d4 = arB16[:, i * 2048:(i + 1) * 2048].rearrange("p (c t) -> p c t", c=4)
                    rdx = xm_bufs[4 * i:4 * i + 4]
                    wrs = stg2_bufs[4 * i:4 * i + 4]
                    b.op("dve", lambda E, x4=x4, d4=d4, bc6=bc6: E.tensor_tensor(d4, x4, bc6, OP.mult),
                         reads=rdx + [P6], writes=wrs)
                    b.op("dve", lambda E, d4=d4, bc5=bc5: E.tensor_tensor(d4, d4, bc5, OP.subtract),
                         reads=[P5], writes=wrs)
                    o_ap = bass.AP(xn2_s.tensor, (seq * 16 + 4 * i) * 128 * S + t0,
                                   [[S, 128], [128 * S, 4], [1, 512]])
                    b.dma("act", lambda E, d4=d4, o_ap=o_ap: E.dma_start(out=o_ap, in_=d4),
                          reads=wrs, writes=[B_xn2[(seq, tg)]])
                emit_dequant(10)

    if cfg["do_ffn"]:
        emit_dequant(len(deq_tasks))
        wdst = arD16[:, 16384:18432]
        xmst = arD[:, 9216:10240]
        outst = arD[:, 10240:11264]
        gel = arD[:, 11264:12288]
        B_aux = [nb(), nb()]
        xnw = arB16
        B_xnw = nb("B")
        inter = arC16
        inter_bufs = {(ip, nt, th): nb("C") for ip in range(2) for nt in range(8) for th in range(2)}
        gsl_ap = [arD16[:, s_ * 4096:(s_ + 1) * 4096] for s_ in range(2)]
        usl_ap = [arD16[:, 8192 + s_ * 4096:8192 + (s_ + 1) * 4096] for s_ in range(2)]
        B_gsl = [nb("D") for _ in range(2)]
        B_usl = [nb("D") for _ in range(2)]
        B_wdst = [nb("D"), nb("D")]
        B_xmst = [nb("D"), nb("D")]
        B_outst = [nb("D"), nb("D")]
        B_gel = nb("D")
        for g in cfg["ffn_groups"]:
            seq, half = g // 2, g % 2
            t0 = half * 1024
            b.dma("sp", lambda E, seq=seq, t0=t0:
                  E.dma_start(out=xnw[:, 0:16384].rearrange("p (c t) -> p c t", c=16),
                              in_=bass.AP(xn2_s.tensor, seq * 16 * 128 * S + t0,
                                          [[S, 128], [128 * S, 16], [1, 1024]])),
                  reads=[B_xn2[(seq, 2 * half)], B_xn2[(seq, 2 * half + 1)]],
                  writes=[B_xnw])
            acc = arA
            acc_bufs = [nb("A") for _ in range(16)]
            for sl in range(64):
                sup, sli = sl // 4, sl % 4
                gs_ = sl % 2
                gsl = gsl_ap[gs_]
                usl = usl_ap[gs_]
                b.dma("sp", lambda E, sl=sl, d=gsl:
                      E.dma_start(out=d, in_=bass.AP(wg_s.tensor, sl * 128 * 16 * 256,
                                                     [[16 * 256, 128], [1, 16 * 256]])),
                      reads=[B_wg[sl]], writes=[B_gsl[gs_]])
                b.dma("sp", lambda E, sl=sl, d=usl:
                      E.dma_start(out=d, in_=bass.AP(wu_s.tensor, sl * 128 * 16 * 256,
                                                     [[16 * 256, 128], [1, 16 * 256]])),
                      reads=[B_wu[sl]], writes=[B_usl[gs_]])
                ip = sup % 2
                for par in (0, 1):
                    j = 2 * sl + par
                    nt = j % 8
                    for th in range(2):
                        tof = th * 512
                        gps = pbig[:, th * 512:th * 512 + 512]
                        ups = pbig[:, 1024 + th * 512:1024 + th * 512 + 512]
                        Bg = PB[th]
                        Bu = PB[2 + th]
                        for wbase, ps_, bps, bw in ((gs_ * 4096, gps, Bg, B_gsl[gs_]),
                                                    (8192 + gs_ * 4096, ups, Bu, B_usl[gs_])):
                            fns = []
                            for c in range(16):
                                lhs = bass.AP(arD16, wbase + c * 256 + par,
                                              [[24576, 128], [2, 128]])
                                rhs = xnw[:, c * 1024 + tof:c * 1024 + tof + 512]
                                fns.append(lambda E, lhs=lhs, rhs=rhs, ps_=ps_,
                                           st=(c == 0), sp=(c == 15):
                                           E.matmul(ps_, lhs, rhs, start=st, stop=sp))
                            b.group("pe", fns, reads=[bw, B_xnw], writes=[bps])
                    gfull = pbig[:, 0:1024]
                    ufull = pbig[:, 1024:2048]
                    it = inter[:, ip * 8192 + nt * 1024:ip * 8192 + nt * 1024 + 1024]
                    Bits = [inter_bufs[(ip, nt, 0)], inter_bufs[(ip, nt, 1)]]
                    gbuf = aux8[:, 0:1024]
                    ubuf = aux8[:, 1024:2048]
                    tA = gel[:, 0:1024]
                    b.op("dve", lambda E, g_=gfull, gbuf=gbuf: E.tensor_copy(gbuf, g_),
                         reads=[PB[0], PB[1]], writes=[B_aux[0]])
                    b.op("act", lambda E, u=ufull, ubuf=ubuf: E.activation(ubuf, u, AF.Copy),
                         reads=[PB[2], PB[3]], writes=[B_aux[1]])
                    b.op("act", lambda E, gbuf=gbuf, tA=tA: E.activation(tA, gbuf, AF.Square),
                         reads=[B_aux[0]], writes=[B_gel])
                    b.op("act", lambda E, tA=tA:
                         E.activation(tA, tA, AF.Copy, scale=C0 * C1, bias=C0),
                         reads=[B_gel], writes=[B_gel])
                    b.op("pool", lambda E, tA=tA, gbuf=gbuf: E.tensor_tensor(tA, tA, gbuf, OP.mult),
                         reads=[B_aux[0]], writes=[B_gel])
                    b.op("pool", lambda E, gbuf=gbuf, ubuf=ubuf:
                         E.tensor_tensor(ubuf, gbuf, ubuf, OP.mult),
                         reads=[B_aux[0]], writes=[B_aux[1]])
                    b.op("act", lambda E, tA=tA: E.activation(tA, tA, AF.Tanh),
                         reads=[B_gel], writes=[B_gel])
                    b.op("act", lambda E, tA=tA: E.activation(tA, tA, AF.Copy, scale=0.5, bias=0.5),
                         reads=[B_gel], writes=[B_gel])
                    b.op("dve", lambda E, tA=tA, u=ubuf, it=it: E.tensor_tensor(it, tA, u, OP.mult),
                         reads=[B_gel, B_aux[1]], writes=Bits)
                if sli == 3 and sup % 2 == 1:
                    for dt in range(16):
                        for s2 in (sup - 1, sup):
                            wds = wdst[:, ((2 * dt + s2) % 2) * 1024:((2 * dt + s2) % 2) * 1024 + 1024]
                            Bw = B_wdst[(2 * dt + s2) % 2]
                            b.dma("sp", lambda E, s2=s2, dt=dt, d=wds:
                                  E.dma_start(out=d, in_=bass.AP(wd_s.tensor,
                                                                 (s2 * 16 + dt) * 128 * 8 * 128,
                                                                 [[8 * 128, 128], [1, 8 * 128]])),
                                  reads=[B_wd[s2]], writes=[Bw])
                            for th in range(2):
                                dps, Bdps = (p5, P5) if (dt + th) % 2 == 0 else (p6, P6)
                                fns = []
                                rd = [Bw]
                                ip2 = s2 % 2
                                for nt in range(8):
                                    lhs = wds[:, nt * 128:(nt + 1) * 128]
                                    rhs = inter[:, ip2 * 8192 + nt * 1024 + th * 512:
                                                ip2 * 8192 + nt * 1024 + th * 512 + 512]
                                    rd.append(inter_bufs[(ip2, nt, th)])
                                    fns.append(lambda E, lhs=lhs, rhs=rhs, dps=dps,
                                               st=(nt == 0 and s2 == sup - 1),
                                               sp=(nt == 7 and s2 == sup):
                                               E.matmul(dps[:, :], lhs, rhs, start=st, stop=sp))
                                b.group("pe", fns, reads=rd, writes=[Bdps])
                                if s2 == sup:
                                    a = acc[:, dt * 1024 + th * 512:dt * 1024 + th * 512 + 512]
                                    if sup == 1:
                                        b.op("dve", lambda E, a=a, dps=dps:
                                             E.tensor_copy(a, dps[:, :]),
                                             reads=[Bdps], writes=[acc_bufs[dt]])
                                    else:
                                        b.op("dve", lambda E, a=a, dps=dps:
                                             E.tensor_tensor(a, a, dps[:, :], OP.add),
                                             reads=[Bdps], writes=[acc_bufs[dt]])
            for th in range(2):
                tof = th * 512
                wide = [(bass.AP(arA, 2 * i * 1024 + tof, [[16384, 128], [1024, 2], [1, 512]]),
                         [acc_bufs[2 * i], acc_bufs[2 * i + 1]]) for i in range(8)]
                ln_stats(None, wide=wide)
                bc6 = bass.AP(p6, 0, [[512, 128], [0, 2], [1, 512]])
                bc5 = bass.AP(p5, 0, [[512, 128], [0, 2], [1, 512]])
                for i in range(8):
                    a2 = bass.AP(arA, 2 * i * 1024 + tof, [[16384, 128], [1024, 2], [1, 512]])
                    sl2 = i % 2
                    xm2 = xmst[:, 0:1024] if sl2 == 0 else outst[:, 0:1024]
                    Bxm = B_xmst[0] if sl2 == 0 else B_outst[0]
                    rda = [acc_bufs[2 * i], acc_bufs[2 * i + 1]]
                    i_ap = bass.AP(xmid_s.tensor, (seq * 16 + 2 * i) * 128 * S + t0 + tof,
                                   [[S, 128], [128 * S, 2], [1, 512]])
                    b.dma("sp", lambda E, xm2=xm2, i_ap=i_ap:
                          E.dma_start(out=xm2.rearrange("p (c t) -> p c t", c=2), in_=i_ap),
                          reads=[B_xmid[(seq, 2 * half + th)]], writes=[Bxm])
                    b.op("dve", lambda E, a2=a2, bc6=bc6: E.tensor_tensor(a2, a2, bc6, OP.mult),
                         reads=[P6], writes=rda)
                    b.op("dve", lambda E, a2=a2, bc5=bc5: E.tensor_tensor(a2, a2, bc5, OP.subtract),
                         reads=[P5], writes=rda)
                    b.op("dve", lambda E, a2=a2, xm2=xm2: E.tensor_tensor(
                        a2, a2, xm2.rearrange("p (c t) -> p c t", c=2), OP.add),
                         reads=[Bxm], writes=rda)
                    o_ap = bass.AP(out_d.tensor, (seq * 16 + 2 * i) * 128 * S + t0 + tof,
                                   [[S, 128], [128 * S, 2], [1, 512]])
                    b.dma("act", lambda E, a2=a2, o_ap=o_ap: E.dma_start(out=o_ap, in_=a2),
                          reads=rda, writes=[B_out])

    allb = Buf()
    allb.w = dict(b.cnt)
    for name in cfg.get("dump_sbuf", []):
        ap = {"q_sb": arC16[:, 0:2048], "k_sb": arC16[:, 2048:4096],
              "v_tm": arB16[:, 8192:12288], "p_sb": arD16[:, 0:2048],
              "pT": arD16[:, 4096:6144], "z_all": arA16[:, 0:32768],
              "xw": arB16[:, 0:8192], "acc": arA[:, :], "xnw": arB16[:, 0:16384],
              "rows": rows[:, :], "gelx": arD[:, 9216:12288],
              "inter": arC16[:, 0:16384]}[name]
        shp = list(ap.shape)
        dt_ = {2: BF16, 4: F32}[4 if name in ("acc", "rows", "gelx") else 2]
        dd = DT("dbg_" + name, shp, dt_, kind="ExternalOutput").ap()
        b.dma("sp", lambda E, ap=ap, dd=dd: E.dma_start(out=dd[:, :], in_=ap), reads=[allb])
    for name, src, dst in dbg:
        flat_src = bass.AP(src.tensor, 0,
                           [[src.shape[-1], int(np.prod(src.shape)) // src.shape[-1]],
                            [1, src.shape[-1]]])
        b.dma("sp", lambda E, s=flat_src, d=dst: E.dma_start(out=d, in_=s), reads=[allb])

    b.finalize()
    return nc


def _pi():
    pi = np.zeros(2048, np.int64)
    for c in range(8):
        for par in range(2):
            m = 2 * c + par
            pi[m * 128:(m + 1) * 128] = 256 * c + 2 * np.arange(128) + par
    return pi


def prep_inputs(L0_attn_x, L0_attn_W_Q, L0_attn_W_K, L0_attn_W_V, L0_attn_W_O,
                L0_attn_b_Q, L0_attn_b_K, L0_attn_b_V, L0_attn_b_O,
                L0_ffn_gate_W_packed, L0_ffn_gate_scales, L0_ffn_up_W_packed,
                L0_ffn_up_scales, L0_ffn_down_W_packed, L0_ffn_down_scales):
    pi = _pi()
    x = np.ascontiguousarray(
        np.asarray(L0_attn_x).transpose(0, 2, 1)[:, pi, :]).reshape(16, 16, 128, 2048)

    def head_img(w):
        wp = np.asarray(w)[:, pi, :]
        return np.ascontiguousarray(
            wp.reshape(16, 16, 128, 128).transpose(0, 2, 1, 3).reshape(16, 128, 2048))

    wq = head_img(L0_attn_W_Q)
    wk = head_img(L0_attn_W_K)
    wvp = np.asarray(L0_attn_W_V)[:, pi, :]
    wv = np.ascontiguousarray(
        wvp.transpose(1, 0, 2).reshape(16, 128, 2048)
        .reshape(16, 128, 8, 256).transpose(2, 1, 0, 3).reshape(8, 128, 4096))
    wo = np.asarray(L0_attn_W_O).reshape(2048, 2048)[:, pi]
    wo2 = np.ascontiguousarray(
        wo.reshape(16, 128, 16, 128).transpose(2, 1, 0, 3).reshape(16, 128, 2048))
    bq = np.ascontiguousarray(np.asarray(L0_attn_b_Q, dtype=np.float32))
    bk = np.ascontiguousarray(np.asarray(L0_attn_b_K, dtype=np.float32))
    bvg = np.ascontiguousarray(np.asarray(L0_attn_b_V, dtype=np.float32).reshape(1, 2048))
    bo = np.ascontiguousarray(np.asarray(L0_attn_b_O, dtype=np.float32)[pi].reshape(16, 128))

    def u8(a):
        return np.asarray(a).astype(np.uint8)

    pg = np.ascontiguousarray(u8(L0_ffn_gate_W_packed).T)
    pu = np.ascontiguousarray(u8(L0_ffn_up_W_packed).T)
    pd = np.ascontiguousarray(u8(L0_ffn_down_W_packed).T[:, pi])
    sg = np.ascontiguousarray(np.asarray(L0_ffn_gate_scales, dtype=np.float32).T)
    su = np.ascontiguousarray(np.asarray(L0_ffn_up_scales, dtype=np.float32).T)
    sd = np.ascontiguousarray(np.asarray(L0_ffn_down_scales, dtype=np.float32).T[:, pi])
    idn = np.eye(128, dtype=np.float32)
    msk = np.where(np.arange(128)[None, :] > np.arange(128)[:, None],
                   np.float32(NEG), np.float32(0.0)).astype(np.float32)
    shared = dict(wq=wq, wk=wk, wv=wv, wo=wo2, bq=bq, bk=bk, bvg=bvg, bo=bo,
                  pg=pg, pu=pu, pd=pd, sg=sg, su=su, sd=sd, idn=idn, msk=msk)
    in_maps = []
    for core in range(8):
        m = dict(shared)
        m["x"] = np.ascontiguousarray(x[2 * core:2 * core + 2])
        in_maps.append(m)
    return in_maps, pi


def kernel(**inputs):
    in_maps, pi = prep_inputs(**inputs)
    nc = build()
    res = run_bass_kernel_spmd(nc, in_maps, core_ids=list(range(8)))
    inv = np.argsort(pi)
    out = np.empty((16, 2048, 2048), np.float32)
    for core in range(8):
        o = np.asarray(res.results[core]["out"]).reshape(2, 2048, 2048)
        out[2 * core:2 * core + 2] = o[:, inv, :].transpose(0, 2, 1)
    return out


if __name__ == "__main__":
    n = build()
    print("built ok; instructions:", n.m.n_instructions() if hasattr(n, "m") else "?")


# version 10
# speedup vs baseline: 1.1764x; 1.1488x over previous
import numpy as np
import concourse.bass as bass
import concourse.mybir as mybir
from concourse.bass_utils import run_bass_kernel_spmd

F32 = mybir.dt.float32
BF16 = mybir.dt.bfloat16
U8 = mybir.dt.uint8
U16 = mybir.dt.uint16
AF = mybir.ActivationFunctionType
OP = mybir.AluOpType
AX = mybir.AxisListType

D = 2048
S = 2048
NSEQ = 2
MLP = 16384
EPS = 1e-5
ASCALE = 1.0 / float(np.sqrt(np.float32(128.0)))
C0 = 0.7978845608028654
C1 = 0.044715
NEG = -100000.0

CFG = {
    "seqs": [0, 1],
    "heads": list(range(16)),
    "ffn_groups": [0, 1, 2, 3],
    "do_attn": True,
    "do_wo": True,
    "do_ffn": True,
    "deq_upfront": False,
    "debug_outs": [],
}


def _merge(dst, src):
    for k, v in src.items():
        if dst.get(k, -1) < v:
            dst[k] = v


class Buf:
    __slots__ = ("w", "r")

    def __init__(self):
        self.w = {}
        self.r = {}


class Bld:
    ENG = ("pe", "dve", "act", "pool", "sp")

    NROT = 16

    def __init__(self, nc):
        self.nc = nc
        self.q = {e: [] for e in self.ENG}
        self.sem = {}
        names = ["Epe", "Edve", "Eact", "Epool"]
        for qn in ("Qsp", "Qact", "Qpool"):
            names += ["%s_%d" % (qn, i) for i in range(self.NROT)]
        for s in names:
            self.sem[s] = nc.alloc_semaphore(s)
        self.cnt = {s: 0 for s in self.sem}
        self.dma_i = {"Qsp": 0, "Qact": 0, "Qpool": 0}
        self.wm = {e: {} for e in self.ENG}

    def _waits(self, eng, dep):
        for s in sorted(dep):
            v = dep[s]
            if self.wm[eng].get(s, 0) < v:
                so = self.sem[s]
                self.q[eng].append(lambda E, so=so, v=v: E.wait_ge(so, v))
                self.wm[eng][s] = v

    def _dep(self, reads, writes):
        dep = {}
        for b in reads:
            _merge(dep, b.w)
        for b in writes:
            _merge(dep, b.w)
            _merge(dep, b.r)
        return dep

    def _fin(self, h, reads, writes):
        for b in reads:
            _merge(b.r, h)
        for b in writes:
            _merge(b.w, h)
            b.r = {}

    def op(self, eng, fn, reads=(), writes=()):
        self._waits(eng, self._dep(reads, writes))
        s = "E" + eng
        self.cnt[s] += 1
        so = self.sem[s]
        self.q[eng].append(lambda E, fn=fn, so=so: fn(E).then_inc(so, 1))
        h = {s: self.cnt[s]}
        self._fin(h, reads, writes)
        return h

    def group(self, eng, fns, reads=(), writes=()):
        self._waits(eng, self._dep(reads, writes))
        s = "E" + eng
        self.cnt[s] += 1
        so = self.sem[s]
        for fn in fns[:-1]:
            self.q[eng].append(lambda E, fn=fn: fn(E))
        self.q[eng].append(lambda E, fn=fns[-1], so=so: fn(E).then_inc(so, 1))
        h = {s: self.cnt[s]}
        self._fin(h, reads, writes)
        return h

    def dma(self, qeng, fn, reads=(), writes=()):
        dep = self._dep(reads, writes)
        qn = "Q" + qeng
        i = self.dma_i[qn]
        self.dma_i[qn] = i + 1
        s = "%s_%d" % (qn, i % self.NROT)
        if self.cnt[s] > 0:
            _merge(dep, {s: self.cnt[s]})
        self._waits(qeng, dep)
        self.cnt[s] += 16
        so = self.sem[s]
        self.q[qeng].append(lambda E, fn=fn, so=so: fn(E).then_inc(so, 16))
        h = {s: self.cnt[s]}
        self._fin(h, reads, writes)
        return h

    def finalize(self):
        for s in sorted(self.cnt):
            c = self.cnt[s]
            if c > 0:
                so = self.sem[s]
                self.q["sp"].append(lambda E, so=so, v=c: E.wait_ge(so, v))
        with self.nc.Block() as block:
            block.tensor(lambda E: ([f(E) for f in self.q["pe"]], None)[1])
            block.vector(lambda E: ([f(E) for f in self.q["dve"]], None)[1])
            block.scalar(lambda E: ([f(E) for f in self.q["act"]], None)[1])
            block.gpsimd(lambda E: ([f(E) for f in self.q["pool"]], None)[1])
            block.sync(lambda E: ([f(E) for f in self.q["sp"]], None)[1])


def build(cfg=None):
    if cfg is None:
        cfg = CFG
    nc = bass.Bass("TRN2", target_bir_lowering=False)
    DT = nc.dram_tensor
    x_in = DT("x", [NSEQ, 16, 128, S], F32, kind="ExternalInput").ap()
    wq_in = DT("wq", [16, 128, 2048], F32, kind="ExternalInput").ap()
    wk_in = DT("wk", [16, 128, 2048], F32, kind="ExternalInput").ap()
    wv_in = DT("wv", [8, 128, 4096], F32, kind="ExternalInput").ap()
    wo_in = DT("wo", [16, 128, 2048], F32, kind="ExternalInput").ap()
    bq_in = DT("bq", [16, 128], F32, kind="ExternalInput").ap()
    bk_in = DT("bk", [16, 128], F32, kind="ExternalInput").ap()
    bvg_in = DT("bvg", [1, 2048], F32, kind="ExternalInput").ap()
    bo_in = DT("bo", [16, 128], F32, kind="ExternalInput").ap()
    pg_in = DT("pg", [1024, MLP], U8, kind="ExternalInput").ap()
    pu_in = DT("pu", [1024, MLP], U8, kind="ExternalInput").ap()
    pd_in = DT("pd", [8192, 2048], U8, kind="ExternalInput").ap()
    sg_in = DT("sg", [64, MLP], F32, kind="ExternalInput").ap()
    su_in = DT("su", [64, MLP], F32, kind="ExternalInput").ap()
    sd_in = DT("sd", [512, 2048], F32, kind="ExternalInput").ap()
    idn_in = DT("idn", [128, 128], F32, kind="ExternalInput").ap()
    msk_in = DT("msk", [128, 128], F32, kind="ExternalInput").ap()

    xln1_s = DT("xln1_s", [NSEQ, 16, 128, S], BF16, kind="Internal").ap()
    xmid_s = DT("xmid_s", [NSEQ, 16, 128, S], F32, kind="Internal").ap()
    xn2_s = DT("xn2_s", [NSEQ, 16, 128, S], BF16, kind="Internal").ap()
    wg_s = DT("wg_s", [64, 128, 16, 256], BF16, kind="Internal").ap()
    wu_s = DT("wu_s", [64, 128, 16, 256], BF16, kind="Internal").ap()
    wd_s = DT("wd_s", [16, 16, 128, 8, 128], BF16, kind="Internal").ap()
    out_d = DT("out", [NSEQ, 16, 128, S], F32, kind="ExternalOutput").ap()

    dbg = []
    for name in cfg["debug_outs"]:
        src = {"xln1_s": xln1_s, "xmid_s": xmid_s, "xn2_s": xn2_s,
               "wg_s": wg_s, "wu_s": wu_s, "wd_s": wd_s}[name]
        n = int(np.prod(src.shape))
        dbg.append((name, src, DT("dbg_" + name, [n // src.shape[-1], src.shape[-1]],
                                  src.dtype, kind="ExternalOutput").ap()))

    SB = nc.alloc_sbuf_tensor
    arA = SB("arA", [128, 16384], F32)
    arB = SB("arB", [128, 8192], F32)
    arC = SB("arC", [128, 8192], F32)
    arD = SB("arD", [128, 12288], F32)
    arA16 = arA.bitcast(BF16)
    arB16 = arB.bitcast(BF16)
    arC16 = arC.bitcast(BF16)
    arD16 = arD.bitcast(BF16)
    arD8 = arD.bitcast(U8)
    arDu16 = arD.bitcast(U16)

    ident = SB("ident", [128, 128], BF16)
    mask = SB("mask", [128, 128], F32)
    ones_cb = SB("ones_cb", [128, 1], BF16)
    ones_r = SB("ones_r", [1, 128], F32)
    ones_rb = SB("ones_rb", [1, 128], BF16)
    rows = SB("rows", [1, 2560], F32)
    cols = SB("cols", [128, 12], F32)
    bvg_sb = SB("bvg_sb", [1, 2048], BF16)
    maskb = SB("maskb", [128, 128], BF16)
    aux8 = SB("aux8", [128, 2048], F32)
    x2t = SB("x2t", [128, 2048], BF16)
    xbt = SB("xbt", [128, 2048], BF16)

    pbig = nc.alloc_psum_tensor("pbig", [128, 2048], F32)
    p4b = nc.alloc_psum_tensor("p4b", [128, 1024], BF16)
    p5 = nc.alloc_psum_tensor("p5", [128, 512], F32)
    p6 = nc.alloc_psum_tensor("p6", [128, 512], F32)
    p7 = nc.alloc_psum_tensor("p7", [128, 512], F32)

    b = Bld(nc)
    AR = {"A": [], "B": [], "C": [], "D": []}

    def nb(ar=None, lo=0, hi=1 << 30):
        bf_ = Buf()
        if ar is not None:
            for (l2, h2, b2) in AR[ar]:
                if l2 < hi and lo < h2:
                    _merge(bf_.w, b2.w)
                    _merge(bf_.w, b2.r)
            AR[ar].append((lo, hi, bf_))
        return bf_

    PB = [nb() for _ in range(4)]
    P4, P5, P6, P7 = nb(), nb(), nb(), nb()
    B_rows, B_cols = nb(), nb()
    B_x2t = [nb(), nb()]
    B_xbt = [nb(), nb()]

    R_mean = rows[0:1, 0:512]
    R_msq = rows[0:1, 512:1024]
    R_var = rows[0:1, 1024:1536]
    R_rstd = rows[0:1, 1536:2048]
    R_nmr = rows[0:1, 2048:2560]
    R_tmp = rows[0:1, 512:1024]
    C_mx = [cols[:, 0:1], cols[:, 4:5]]
    C_nb = [cols[:, 1:2], cols[:, 5:6]]
    C_sum = [cols[:, 2:3], cols[:, 6:7]]
    C_rs = [cols[:, 3:4], cols[:, 7:8]]
    C_bq = cols[:, 8:9]
    C_bk = cols[:, 9:10]
    C_bo = cols[:, 10:11]

    c_ident = nb(); c_mask = nb(); c_ones = nb(); c_bvg = nb()
    b.dma("pool", lambda E: E.dma_start(out=ident[:], in_=idn_in[:, :]), writes=[c_ident])
    b.dma("sp", lambda E: E.dma_start(out=mask[:], in_=msk_in[:, :]), writes=[c_mask])
    b.op("dve", lambda E: E.tensor_copy(maskb[:], mask[:]), reads=[c_mask], writes=[c_mask])
    b.dma("pool", lambda E: E.dma_start(out=bvg_sb[:], in_=bvg_in[:, :]), writes=[c_bvg])
    b.op("pool", lambda E: E.memset(ones_cb[:], 1.0), writes=[c_ones])
    b.op("pool", lambda E: E.memset(ones_r[:], 1.0), writes=[c_ones])
    b.op("pool", lambda E: E.memset(ones_rb[:], 1.0), writes=[c_ones])

    def ln_stats(tiles, wide=None):
        for i in range(0, 16, 2):
            sl = (i // 2) % 2
            xb = xbt[:, sl * 1024:sl * 1024 + 1024]
            x2 = x2t[:, sl * 1024:sl * 1024 + 1024]
            if wide is not None:
                wap, wbufs = wide[i // 2]
                b.op("dve", lambda E, tl=wap, xb=xb: E.tensor_copy(xb, tl),
                     reads=wbufs, writes=[B_xbt[sl]])
                b.op("act", lambda E, tl=wap, x2=x2: E.activation(x2, tl, AF.Square),
                     reads=wbufs, writes=[B_x2t[sl]])
            else:
                for k in range(2):
                    tl, tb = tiles[i + k]
                    b.op("dve", lambda E, tl=tl, d=xb[:, k * 512:k * 512 + 512]:
                         E.tensor_copy(d, tl), reads=[tb], writes=[B_xbt[sl]])
                    b.op("act", lambda E, tl=tl, d=x2[:, k * 512:k * 512 + 512]:
                         E.activation(d, tl, AF.Square), reads=[tb], writes=[B_x2t[sl]])
            st, sp = (i == 0), (i == 14)
            for k in range(2):
                stk = st and k == 0
                spk = sp and k == 1
                b.group("pe", [lambda E, xb=xb[:, k * 512:k * 512 + 512], st=stk, sp=spk:
                               E.matmul(p7[0:1, :], ones_cb[:], xb, start=st, stop=sp)],
                        reads=[B_xbt[sl], c_ones], writes=[P7])
                b.group("pe", [lambda E, x2=x2[:, k * 512:k * 512 + 512], st=stk, sp=spk:
                               E.matmul(p5[0:1, :], ones_cb[:], x2, start=st, stop=sp)],
                        reads=[B_x2t[sl], c_ones], writes=[P5])
        b.op("dve", lambda E: E.tensor_scalar(R_mean, p7[0:1, :], 1.0 / D, None, OP.mult),
             reads=[P7], writes=[B_rows])
        b.op("dve", lambda E: E.tensor_scalar(R_msq, p5[0:1, :], 1.0 / D, None, OP.mult),
             reads=[P5], writes=[B_rows])
        b.op("dve", lambda E: E.tensor_mul(R_var, R_mean, R_mean), reads=[B_rows], writes=[B_rows])
        b.op("dve", lambda E: E.scalar_tensor_tensor(R_var, R_msq, EPS, R_var, OP.add, OP.subtract),
             reads=[B_rows], writes=[B_rows])
        b.op("act", lambda E: E.activation(R_tmp, R_var, AF.Ln), reads=[B_rows], writes=[B_rows])
        b.op("act", lambda E: E.activation(R_rstd, R_tmp, AF.Exp, scale=-0.5),
             reads=[B_rows], writes=[B_rows])
        b.op("dve", lambda E: E.tensor_mul(R_nmr, R_mean, R_rstd), reads=[B_rows], writes=[B_rows])
        b.group("pe", [lambda E: E.matmul(p6[:, :], ones_r[:], R_rstd, start=True, stop=True)],
                reads=[B_rows, c_ones], writes=[P6])
        b.group("pe", [lambda E: E.matmul(p5[:, :], ones_r[:], R_nmr, start=True, stop=True)],
                reads=[B_rows, c_ones], writes=[P5])

    B_xln1 = {s: nb() for s in range(NSEQ)}
    for seq in cfg["seqs"]:
        xw = [arA[:, i * 512:(i + 1) * 512] for i in range(16)]
        xw_bufs = [nb("A") for _ in range(16)]
        stg = [arA16[:, 16384 + i * 512:16384 + (i + 1) * 512] for i in range(16)]
        stg_bufs = [nb("A") for _ in range(16)]
        for tg in range(4):
            t0 = tg * 512
            for c in range(16):
                b.dma("sp", lambda E, c=c, t0=t0, seq=seq, d=xw[c]:
                      E.dma_start(out=d, in_=x_in[seq, c, :, t0:t0 + 512]),
                      writes=[xw_bufs[c]])
            ln_stats(None, wide=[(arA[:, i * 1024:(i + 1) * 1024],
                                  [xw_bufs[2 * i], xw_bufs[2 * i + 1]]) for i in range(8)])
            bc6 = bass.AP(p6, 0, [[512, 128], [0, 4], [1, 512]])
            bc5 = bass.AP(p5, 0, [[512, 128], [0, 4], [1, 512]])
            for i in range(4):
                src = arA[:, i * 2048:(i + 1) * 2048]
                d = arA16[:, 16384 + i * 2048:16384 + (i + 1) * 2048]
                rd = xw_bufs[4 * i:4 * i + 4]
                wr = stg_bufs[4 * i:4 * i + 4]
                b.op("dve", lambda E, s_=src, d=d, bc6=bc6: E.tensor_tensor(
                    d.rearrange("p (c t) -> p c t", c=4), s_.rearrange("p (c t) -> p c t", c=4),
                    bc6, OP.mult), reads=rd + [P6], writes=wr)
                b.op("dve", lambda E, d=d, bc5=bc5: E.tensor_tensor(
                    d.rearrange("p (c t) -> p c t", c=4), d.rearrange("p (c t) -> p c t", c=4),
                    bc5, OP.subtract), reads=[P5], writes=wr)
                o_ap = bass.AP(xln1_s.tensor, (seq * 16 + 4 * i) * 128 * S + t0,
                               [[S, 128], [128 * S, 4], [1, 512]])
                b.dma("act", lambda E, d=d, o_ap=o_ap:
                      E.dma_start(out=o_ap, in_=d.rearrange("p (c t) -> p c t", c=4)),
                      reads=wr, writes=[B_xln1[seq]])

    deq_tasks = []
    if cfg["do_ffn"]:
        for kpc in range(8):
            for ns in range(16):
                deq_tasks.append(("g", kpc, ns))
        for kpc in range(8):
            for ns in range(16):
                deq_tasks.append(("u", kpc, ns))
        for npc in range(64):
            for ds in range(2):
                deq_tasks.append(("d", npc, ds))
    B_wg = {i: nb() for i in range(64)}
    B_wu = {i: nb() for i in range(64)}
    B_wd = {i: nb() for i in range(16)}

    dq_pk = [arD8[:, 24576 + i * 1024:24576 + (i + 1) * 1024] for i in range(2)]
    dq_sc = [arD[:, 6656 + i * 1024:6656 + (i + 1) * 1024] for i in range(2)]
    dqb = 34816
    DQ = {
        "pw": arDu16[:, dqb // 2:dqb // 2 + 1024],
        "n": arDu16[:, (dqb + 2048) // 2:(dqb + 2048) // 2 + 1024],
        "m": arDu16[:, (dqb + 4096) // 2:(dqb + 4096) // 2 + 1024],
        "t1": arDu16[:, (dqb + 6144) // 2:(dqb + 6144) // 2 + 1024],
        "b0": arDu16[:, (dqb + 8192) // 2:(dqb + 8192) // 2 + 1024],
        "bits": arDu16[:, (dqb + 8192) // 2:(dqb + 8192) // 2 + 1024],
        "g0": arD[:, (dqb + 10240) // 4:(dqb + 10240) // 4 + 1024],
        "out": arD16[:, (dqb + 6144) // 2:(dqb + 6144) // 2 + 1024],
    }
    B_dq_pk = [nb("D"), nb("D")]
    B_dq_sc = [nb("D"), nb("D")]
    B_dq = {k: nb("D") for k in ("pw", "n", "m", "t1", "b0", "g0")}
    B_dq["bits"] = B_dq["b0"]
    B_dq["out"] = B_dq["t1"]
    dq_state = {"i": 0, "slot": 0}

    def emit_dequant(n_tiles):
        for _ in range(n_tiles):
            i = dq_state["i"]
            if i >= len(deq_tasks):
                return
            dq_state["i"] += 1
            kind, a0, a1 = deq_tasks[i]
            slot = dq_state["slot"]
            dq_state["slot"] ^= 1
            pk = dq_pk[slot]
            sc = dq_sc[slot]
            bpk = B_dq_pk[slot]
            bsc = B_dq_sc[slot]
            if kind in ("g", "u"):
                src = pg_in if kind == "g" else pu_in
                ssrc = sg_in if kind == "g" else su_in
                b.dma("sp", lambda E, src=src, a0=a0, a1=a1, pk=pk:
                      E.dma_start(out=pk, in_=src[a0 * 128:(a0 + 1) * 128,
                                                  a1 * 1024:(a1 + 1) * 1024]),
                      writes=[bpk])
                rep = bass.AP(ssrc.tensor, 8 * a0 * MLP + a1 * 1024,
                              [[MLP, 8], [0, 16], [1, 1024]])
                b.dma("sp", lambda E, rep=rep, sc=sc: E.dma_start(out=sc, in_=rep),
                      writes=[bsc])
            else:
                npc, ds = a0, a1
                b.dma("sp", lambda E, npc=npc, ds=ds, pk=pk:
                      E.dma_start(out=pk, in_=pd_in[npc * 128:(npc + 1) * 128,
                                                    ds * 1024:(ds + 1) * 1024]),
                      writes=[bpk])
                rep = bass.AP(sd_in.tensor, 8 * npc * 2048 + ds * 1024,
                              [[2048, 8], [0, 16], [1, 1024]])
                b.dma("sp", lambda E, rep=rep, sc=sc: E.dma_start(out=sc, in_=rep),
                      writes=[bsc])
            q = DQ
            Bq = B_dq
            b.op("dve", lambda E, pk=pk, pw=q["pw"]: E.tensor_copy(pw, pk),
                 reads=[bpk], writes=[Bq["pw"]])
            for par in (0, 1):
                if par == 0:
                    b.op("dve", lambda E, pw=q["pw"], n=q["n"]:
                         E.tensor_single_scalar(n, pw, 15, OP.bitwise_and),
                         reads=[Bq["pw"]], writes=[Bq["n"]])
                else:
                    b.op("dve", lambda E, pw=q["pw"], n=q["n"]:
                         E.tensor_single_scalar(n, pw, 4, OP.logical_shift_right),
                         reads=[Bq["pw"]], writes=[Bq["n"]])
                b.op("dve", lambda E, n=q["n"], t1=q["t1"]:
                     E.tensor_scalar(t1, n, 8, 0x7E00, OP.is_ge, OP.mult),
                     reads=[Bq["n"]], writes=[Bq["t1"]])
                b.op("dve", lambda E, n=q["n"], b0=q["b0"]:
                     E.tensor_scalar(b0, n, 64, 0x3F00, OP.mult, OP.add),
                     reads=[Bq["n"]], writes=[Bq["b0"]])
                b.op("dve", lambda E, b0=q["b0"], t1=q["t1"], bi=q["bits"]:
                     E.tensor_tensor(bi, b0, t1, OP.add),
                     reads=[Bq["t1"]], writes=[Bq["bits"]])
                b.op("dve", lambda E, n=q["n"], m=q["m"]:
                     E.tensor_single_scalar(m, n, 7, OP.bitwise_and),
                     reads=[Bq["n"]], writes=[Bq["m"]])
                b.op("dve", lambda E, m=q["m"], g0=q["g0"]:
                     E.tensor_scalar(g0, m, 2.0 / 3.0, 1.0, OP.mult, OP.min),
                     reads=[Bq["m"]], writes=[Bq["g0"]])
                b.op("pool", lambda E, g0=q["g0"], sc=sc:
                     E.tensor_tensor(g0, g0, sc, OP.mult),
                     reads=[bsc], writes=[Bq["g0"]])
                b.op("pool", lambda E, g0=q["g0"], bi=q["bits"], ot=q["out"]:
                     E.tensor_tensor(ot, g0, bi.bitcast(BF16), OP.mult),
                     reads=[Bq["g0"], Bq["bits"]], writes=[Bq["out"]])
                ot = q["out"]
                if kind in ("g", "u"):
                    tgt = wg_s if kind == "g" else wu_s
                    Bt = B_wg if kind == "g" else B_wu
                    m_ = 2 * a0 + par
                    o_ap = bass.AP(tgt.tensor, (4 * a1) * 128 * 16 * 256 + m_ * 256,
                                   [[16 * 256, 128], [128 * 16 * 256, 4], [1, 256]])
                    b.dma("act", lambda E, o_ap=o_ap, ot=ot:
                          E.dma_start(out=o_ap, in_=ot.rearrange("p (w n) -> p w n", w=4)),
                          reads=[Bq["out"]],
                          writes=[Bt[4 * a1 + w] for w in range(4)])
                else:
                    npc, ds = a0, a1
                    j = 2 * npc + par
                    sup, nt = j // 8, j % 8
                    o_ap = bass.AP(wd_s.tensor,
                                   (sup * 16 + ds * 8) * 128 * 8 * 128 + nt * 128,
                                   [[8 * 128, 128], [128 * 8 * 128, 8], [1, 128]])
                    b.dma("act", lambda E, o_ap=o_ap, ot=ot:
                          E.dma_start(out=o_ap, in_=ot.rearrange("p (t d) -> p t d", t=8)),
                          reads=[Bq["out"]], writes=[B_wd[sup]])

    B_z = {}
    B_xmid = {(s, tg): nb() for s in range(NSEQ) for tg in range(4)}
    B_xn2 = {(s, tg): nb() for s in range(NSEQ) for tg in range(4)}
    B_out = nb()

    if cfg["do_attn"]:
        for seq in cfg["seqs"]:
            z_all = arA16
            B_z[seq] = nb("A")
            B_xw = nb("B")
            B_vtm = nb("B")
            B_ssb = [nb("B"), nb("B")]
            B_qk = [[nb("C"), nb("C")], [nb("C"), nb("C")]]
            B_wvst = nb("C")
            B_wq = nb("C")
            B_wk = nb("C")
            B_psb = [nb("D"), nb("D")]
            B_pT = [nb("D"), nb("D")]
            B_bqk = nb()
            B_colside = [nb(), nb()]
            xw = arB16[:, 0:8192]
            v_tm = arB16[:, 8192:12288]
            s_sb = [arB16[:, 12288 + sd * 2048:12288 + (sd + 1) * 2048] for sd in range(2)]
            qk_sb = [[arC16[:, (2 * sd) * 2048:(2 * sd) * 2048 + 2048],
                      arC16[:, (2 * sd + 1) * 2048:(2 * sd + 1) * 2048 + 2048]]
                     for sd in range(2)]
            wvst = arC16[:, 8192:12288]
            wqst = arC16[:, 12288:14336]
            wkst = arC16[:, 14336:16384]
            p_sb = [arD16[:, sd * 2048:(sd + 1) * 2048] for sd in range(2)]
            pT = [arD16[:, 4096 + sd * 2048:4096 + (sd + 1) * 2048] for sd in range(2)]
            for pair in range(8):
                heads = [h for h in cfg["heads"] if h // 2 == pair]
                if not heads:
                    continue
                b.dma("pool", lambda E, pair=pair, d=wvst:
                      E.dma_start(out=d, in_=wv_in[pair, :, :]), writes=[B_wvst])
                for ts_ in range(4):
                    t0 = ts_ * 512
                    b.dma("sp", lambda E, seq=seq, t0=t0, d=xw:
                          E.dma_start(out=d.rearrange("p (c t) -> p c t", c=16),
                                      in_=bass.AP(xln1_s.tensor, seq * 16 * 128 * S + t0,
                                                  [[S, 128], [128 * S, 16], [1, 512]])),
                          reads=[B_xln1[seq]], writes=[B_xw])
                    for tt in range(4):
                        ps, Bp = (p7, P7) if tt % 2 == 0 else (p6, P6)
                        fns = []
                        for c in range(16):
                            lhs = xw[:, c * 512 + tt * 128:c * 512 + tt * 128 + 128]
                            rhs = wvst[:, c * 256:(c + 1) * 256]
                            fns.append(lambda E, lhs=lhs, rhs=rhs, ps=ps, st=(c == 0):
                                       E.matmul(ps[:, 0:256], lhs, rhs, start=st, stop=False))
                        fns.append(lambda E, ps=ps, pair=pair:
                                   E.matmul(ps[:, 0:256], ones_rb[:],
                                            bvg_sb[0:1, pair * 256:(pair + 1) * 256],
                                            start=False, stop=True))
                        b.group("pe", fns, reads=[B_xw, B_wvst, c_bvg, c_ones], writes=[Bp])
                        gt = ts_ * 4 + tt
                        b.op("dve", lambda E, ps=ps, d=v_tm[:, gt * 256:(gt + 1) * 256]:
                             E.tensor_copy(d, ps[:, 0:256]), reads=[Bp], writes=[B_vtm])
                for sd, h_ in enumerate(heads):
                    b.dma("pool", lambda E, h_=h_, d=wqst:
                          E.dma_start(out=d, in_=wq_in[h_, :, :]), writes=[B_wq])
                    b.dma("pool", lambda E, h_=h_, d=wkst:
                          E.dma_start(out=d, in_=wk_in[h_, :, :]), writes=[B_wk])
                    b.dma("sp", lambda E, h_=h_:
                          E.dma_start(out=C_bq, in_=bq_in[h_:h_ + 1, :].rearrange("a p -> p a"),
                                      allow_slow_non_contiguous=True), writes=[B_bqk])
                    b.dma("sp", lambda E, h_=h_:
                          E.dma_start(out=C_bk, in_=bk_in[h_:h_ + 1, :].rearrange("a p -> p a"),
                                      allow_slow_non_contiguous=True), writes=[B_bqk])
                    for ts_ in range(4):
                        t0 = ts_ * 512
                        b.dma("sp", lambda E, seq=seq, t0=t0, d=xw:
                              E.dma_start(out=d.rearrange("p (c t) -> p c t", c=16),
                                          in_=bass.AP(xln1_s.tensor, seq * 16 * 128 * S + t0,
                                                      [[S, 128], [128 * S, 16], [1, 512]])),
                              reads=[B_xln1[seq]], writes=[B_xw])
                        for which, bcol, ps, Bp in ((0, C_bq, p7, P7), (1, C_bk, p6, P6)):
                            wst = wqst if which == 0 else wkst
                            bwst = B_wq if which == 0 else B_wk
                            fns = []
                            for c in range(16):
                                lhs = wst[:, c * 128:(c + 1) * 128]
                                rhs = xw[:, c * 512:(c + 1) * 512]
                                fns.append(lambda E, lhs=lhs, rhs=rhs, ps=ps,
                                           st=(c == 0), sp=(c == 15):
                                           E.matmul(ps[:, :], lhs, rhs, start=st, stop=sp))
                            b.group("pe", fns, reads=[B_xw, bwst], writes=[Bp])
                            b.op("dve", lambda E, ps=ps, d=qk_sb[sd][which][:, t0:t0 + 512], bc=bcol:
                                 E.tensor_single_scalar(d, ps[:, :], bc, OP.add),
                                 reads=[Bp, B_bqk], writes=[B_qk[sd][which]])
                for band in range(4):
                    for qs in range(4):
                        qi = band * 4 + qs
                        L = (qi + 1) * 128
                        nbank = (L + 511) // 512
                        for sd, h_ in enumerate(heads):
                            for j in range(nbank):
                                n = min(512, L - j * 512)
                                bank = 2 * sd + (j % 2)
                                b.group("pe", [lambda E, qi=qi, j=j, n=n, bank=bank, sd=sd:
                                               E.matmul(pbig[:, bank * 512:bank * 512 + n],
                                                        qk_sb[sd][0][:, qi * 128:qi * 128 + 128],
                                                        qk_sb[sd][1][:, j * 512:j * 512 + n],
                                                        start=True, stop=True)],
                                        reads=[B_qk[sd][0], B_qk[sd][1]], writes=[PB[bank]])
                                b.op("act", lambda E, j=j, n=n, bank=bank, sd=sd:
                                     E.activation(s_sb[sd][:, j * 512:j * 512 + n],
                                                  pbig[:, bank * 512:bank * 512 + n], AF.Copy),
                                     reads=[PB[bank]], writes=[B_ssb[sd]])
                            b.op("dve", lambda E, L=L, sd=sd:
                                 E.reduce_max(C_mx[sd], s_sb[sd][:, 0:L], axis=AX.X),
                                 reads=[B_ssb[sd]], writes=[B_colside[sd]])
                            b.op("dve", lambda E, L=L, sd=sd:
                                 E.tensor_tensor(s_sb[sd][:, L - 128:L], s_sb[sd][:, L - 128:L],
                                                 maskb[:], OP.add),
                                 reads=[c_mask], writes=[B_ssb[sd]])
                            b.op("dve", lambda E, sd=sd:
                                 E.tensor_scalar(C_nb[sd], C_mx[sd], -ASCALE, None, OP.mult),
                                 reads=[B_colside[sd]], writes=[B_colside[sd]])
                            b.op("act", lambda E, L=L, sd=sd:
                                 E.activation(p_sb[sd][:, 0:L], s_sb[sd][:, 0:L], AF.Exp,
                                              bias=C_nb[sd], scale=ASCALE, accum_out=C_sum[sd]),
                                 reads=[B_ssb[sd], B_colside[sd]],
                                 writes=[B_psb[sd], B_colside[sd]])
                            b.op("dve", lambda E, sd=sd: E.reciprocal(C_rs[sd], C_sum[sd]),
                                 reads=[B_colside[sd]], writes=[B_colside[sd]])
                            b.op("dve", lambda E, L=L, sd=sd:
                                 E.tensor_single_scalar(p_sb[sd][:, 0:L], p_sb[sd][:, 0:L],
                                                        C_rs[sd], OP.mult),
                                 reads=[B_colside[sd]], writes=[B_psb[sd]])
                        emit_dequant(1)
                        for sd, h_ in enumerate(heads):
                            hh = h_ % 2
                            for j0 in range(0, qi + 1, 4):
                                jn = min(4, qi + 1 - j0)
                                fns = [lambda E, j=j0 + jj, jj=jj, sd=sd:
                                       E.transpose(p4b[:, sd * 512 + jj * 128:sd * 512 + (jj + 1) * 128],
                                                   p_sb[sd][:, j * 128:(j + 1) * 128], ident[:])
                                       for jj in range(jn)]
                                b.group("pe", fns, reads=[B_psb[sd], c_ident], writes=[P4])
                                b.op("dve", lambda E, j0=j0, jn=jn, sd=sd:
                                     E.tensor_copy(pT[sd][:, j0 * 128:(j0 + jn) * 128],
                                                   p4b[:, sd * 512:sd * 512 + jn * 128]),
                                     reads=[P4], writes=[B_pT[sd]])
                            zp, Bzp = (p5, P5) if sd == 0 else (p6, P6)
                            fns = []
                            for j in range(qi + 1):
                                lhs = v_tm[:, j * 256 + hh * 128:j * 256 + hh * 128 + 128]
                                rhs = pT[sd][:, j * 128:(j + 1) * 128]
                                fns.append(lambda E, lhs=lhs, rhs=rhs, zp=zp, qs=qs,
                                           st=(j == 0), sp=(j == qi):
                                           E.matmul(zp[:, qs * 128:qs * 128 + 128], lhs, rhs,
                                                    start=st, stop=sp))
                            b.group("pe", fns, reads=[B_vtm, B_pT[sd]], writes=[Bzp])
                    for sd, h_ in enumerate(heads):
                        zp, Bzp = (p5, P5) if sd == 0 else (p6, P6)
                        b.op("dve", lambda E, zp=zp, h_=h_,
                             d=z_all[:, h_ * 2048 + band * 512:h_ * 2048 + (band + 1) * 512]:
                             E.tensor_copy(d, zp[:, :]), reads=[Bzp], writes=[B_z[seq]])

            if not cfg["do_wo"]:
                continue
            wsl_ap = [arD16[:, 8192 + sl_ * 2048:8192 + (sl_ + 1) * 2048] for sl_ in range(2)]
            B_wsl = [nb("D"), nb("D")]
            ao = [arB[:, i * 512:(i + 1) * 512] for i in range(16)]
            ao_bufs = [nb("B") for _ in range(16)]
            xm = [arC[:, i * 512:(i + 1) * 512] for i in range(16)]
            xm_bufs = [nb("C") for _ in range(16)]
            stg2 = [arB16[:, i * 512:(i + 1) * 512] for i in range(16)]
            stg2_bufs = [nb("B") for _ in range(16)]
            B_bo = nb()
            for tg in range(4):
                t0 = tg * 512
                for dt in range(16):
                    wsl = wsl_ap[dt % 2]
                    b.dma("pool", lambda E, dt=dt, d=wsl:
                          E.dma_start(out=d, in_=wo_in[dt, :, :]), writes=[B_wsl[dt % 2]])
                    b.dma("sp", lambda E, dt=dt:
                          E.dma_start(out=C_bo, in_=bo_in[dt:dt + 1, :].rearrange("a p -> p a"),
                                      allow_slow_non_contiguous=True), writes=[B_bo])
                    ps, Bp = (p5, P5) if dt % 2 == 0 else (p6, P6)
                    fns = []
                    for hc in range(16):
                        lhs = wsl[:, hc * 128:(hc + 1) * 128]
                        rhs = z_all[:, hc * 2048 + t0:hc * 2048 + t0 + 512]
                        fns.append(lambda E, lhs=lhs, rhs=rhs, ps=ps, st=(hc == 0), sp=(hc == 15):
                                   E.matmul(ps[:, :], lhs, rhs, start=st, stop=sp))
                    b.group("pe", fns, reads=[B_wsl[dt % 2], B_z[seq]], writes=[Bp])
                    b.op("dve", lambda E, ps=ps, d=ao[dt]:
                         E.tensor_single_scalar(d, ps[:, :], C_bo, OP.add),
                         reads=[Bp, B_bo], writes=[ao_bufs[dt]])
                ln_stats(None, wide=[(arB[:, i * 1024:(i + 1) * 1024],
                                      [ao_bufs[2 * i], ao_bufs[2 * i + 1]]) for i in range(8)])
                for c in range(16):
                    b.dma("sp", lambda E, c=c, t0=t0, seq=seq, d=xm[c]:
                          E.dma_start(out=d, in_=x_in[seq, c, :, t0:t0 + 512]),
                          writes=[xm_bufs[c]])
                bc6 = bass.AP(p6, 0, [[512, 128], [0, 4], [1, 512]])
                bc5 = bass.AP(p5, 0, [[512, 128], [0, 4], [1, 512]])
                for i in range(4):
                    a4 = arB[:, i * 2048:(i + 1) * 2048].rearrange("p (c t) -> p c t", c=4)
                    x4 = arC[:, i * 2048:(i + 1) * 2048].rearrange("p (c t) -> p c t", c=4)
                    rda = ao_bufs[4 * i:4 * i + 4]
                    rdx = xm_bufs[4 * i:4 * i + 4]
                    b.op("dve", lambda E, a4=a4, bc6=bc6: E.tensor_tensor(a4, a4, bc6, OP.mult),
                         reads=[P6], writes=rda)
                    b.op("dve", lambda E, a4=a4, bc5=bc5: E.tensor_tensor(a4, a4, bc5, OP.subtract),
                         reads=[P5], writes=rda)
                    b.op("dve", lambda E, a4=a4, x4=x4: E.tensor_tensor(x4, x4, a4, OP.add),
                         reads=rda, writes=rdx)
                    o_ap = bass.AP(xmid_s.tensor, (seq * 16 + 4 * i) * 128 * S + t0,
                                   [[S, 128], [128 * S, 4], [1, 512]])
                    b.dma("act", lambda E, x4=x4, o_ap=o_ap: E.dma_start(out=o_ap, in_=x4),
                          reads=rdx, writes=[B_xmid[(seq, tg)]])
                ln_stats(None, wide=[(arC[:, i * 1024:(i + 1) * 1024],
                                      [xm_bufs[2 * i], xm_bufs[2 * i + 1]]) for i in range(8)])
                for i in range(4):
                    x4 = arC[:, i * 2048:(i + 1) * 2048].rearrange("p (c t) -> p c t", c=4)
                    d4 = arB16[:, i * 2048:(i + 1) * 2048].rearrange("p (c t) -> p c t", c=4)
                    rdx = xm_bufs[4 * i:4 * i + 4]
                    wrs = stg2_bufs[4 * i:4 * i + 4]
                    b.op("dve", lambda E, x4=x4, d4=d4, bc6=bc6: E.tensor_tensor(d4, x4, bc6, OP.mult),
                         reads=rdx + [P6], writes=wrs)
                    b.op("dve", lambda E, d4=d4, bc5=bc5: E.tensor_tensor(d4, d4, bc5, OP.subtract),
                         reads=[P5], writes=wrs)
                    o_ap = bass.AP(xn2_s.tensor, (seq * 16 + 4 * i) * 128 * S + t0,
                                   [[S, 128], [128 * S, 4], [1, 512]])
                    b.dma("act", lambda E, d4=d4, o_ap=o_ap: E.dma_start(out=o_ap, in_=d4),
                          reads=wrs, writes=[B_xn2[(seq, tg)]])
                emit_dequant(4)

    if cfg["do_ffn"]:
        emit_dequant(len(deq_tasks))
        wdst = arD16[:, 16384:18432]
        xmst = arD[:, 9216:10240]
        outst = arD[:, 10240:11264]
        gel = arD[:, 11264:12288]
        B_aux = [nb(), nb()]
        xnw = arB16
        B_xnw = nb("B")
        inter = arC16
        inter_bufs = {(ip, nt, th): nb("C") for ip in range(2) for nt in range(8) for th in range(2)}
        gsl_ap = [arD16[:, s_ * 4096:(s_ + 1) * 4096] for s_ in range(2)]
        usl_ap = [arD16[:, 8192 + s_ * 4096:8192 + (s_ + 1) * 4096] for s_ in range(2)]
        B_gsl = [nb("D") for _ in range(2)]
        B_usl = [nb("D") for _ in range(2)]
        B_wdst = [nb("D"), nb("D")]
        B_xmst = [nb("D"), nb("D")]
        B_outst = [nb("D"), nb("D")]
        B_gel = nb("D")
        for g in cfg["ffn_groups"]:
            seq, half = g // 2, g % 2
            t0 = half * 1024
            b.dma("sp", lambda E, seq=seq, t0=t0:
                  E.dma_start(out=xnw[:, 0:16384].rearrange("p (c t) -> p c t", c=16),
                              in_=bass.AP(xn2_s.tensor, seq * 16 * 128 * S + t0,
                                          [[S, 128], [128 * S, 16], [1, 1024]])),
                  reads=[B_xn2[(seq, 2 * half)], B_xn2[(seq, 2 * half + 1)]],
                  writes=[B_xnw])
            acc = arA
            acc_bufs = [nb("A") for _ in range(16)]
            for sl in range(64):
                sup, sli = sl // 4, sl % 4
                gs_ = sl % 2
                gsl = gsl_ap[gs_]
                usl = usl_ap[gs_]
                b.dma("sp", lambda E, sl=sl, d=gsl:
                      E.dma_start(out=d, in_=bass.AP(wg_s.tensor, sl * 128 * 16 * 256,
                                                     [[16 * 256, 128], [1, 16 * 256]])),
                      reads=[B_wg[sl]], writes=[B_gsl[gs_]])
                b.dma("sp", lambda E, sl=sl, d=usl:
                      E.dma_start(out=d, in_=bass.AP(wu_s.tensor, sl * 128 * 16 * 256,
                                                     [[16 * 256, 128], [1, 16 * 256]])),
                      reads=[B_wu[sl]], writes=[B_usl[gs_]])
                ip = sup % 2
                for par in (0, 1):
                    j = 2 * sl + par
                    nt = j % 8
                    for th in range(2):
                        tof = th * 512
                        gps = pbig[:, th * 512:th * 512 + 512]
                        ups = pbig[:, 1024 + th * 512:1024 + th * 512 + 512]
                        Bg = PB[th]
                        Bu = PB[2 + th]
                        for wbase, ps_, bps, bw in ((gs_ * 4096, gps, Bg, B_gsl[gs_]),
                                                    (8192 + gs_ * 4096, ups, Bu, B_usl[gs_])):
                            fns = []
                            for c in range(16):
                                lhs = bass.AP(arD16, wbase + c * 256 + par,
                                              [[24576, 128], [2, 128]])
                                rhs = xnw[:, c * 1024 + tof:c * 1024 + tof + 512]
                                fns.append(lambda E, lhs=lhs, rhs=rhs, ps_=ps_,
                                           st=(c == 0), sp=(c == 15):
                                           E.matmul(ps_, lhs, rhs, start=st, stop=sp))
                            b.group("pe", fns, reads=[bw, B_xnw], writes=[bps])
                    gfull = pbig[:, 0:1024]
                    ufull = pbig[:, 1024:2048]
                    it = inter[:, ip * 8192 + nt * 1024:ip * 8192 + nt * 1024 + 1024]
                    Bits = [inter_bufs[(ip, nt, 0)], inter_bufs[(ip, nt, 1)]]
                    gbuf = aux8[:, 0:1024]
                    ubuf = aux8[:, 1024:2048]
                    tA = gel[:, 0:1024]
                    b.op("dve", lambda E, g_=gfull, gbuf=gbuf: E.tensor_copy(gbuf, g_),
                         reads=[PB[0], PB[1]], writes=[B_aux[0]])
                    b.op("act", lambda E, u=ufull, ubuf=ubuf: E.activation(ubuf, u, AF.Copy),
                         reads=[PB[2], PB[3]], writes=[B_aux[1]])
                    b.op("act", lambda E, gbuf=gbuf, tA=tA: E.activation(tA, gbuf, AF.Square),
                         reads=[B_aux[0]], writes=[B_gel])
                    b.op("act", lambda E, tA=tA:
                         E.activation(tA, tA, AF.Copy, scale=C0 * C1, bias=C0),
                         reads=[B_gel], writes=[B_gel])
                    b.op("pool", lambda E, tA=tA, gbuf=gbuf: E.tensor_tensor(tA, tA, gbuf, OP.mult),
                         reads=[B_aux[0]], writes=[B_gel])
                    b.op("pool", lambda E, gbuf=gbuf, ubuf=ubuf:
                         E.tensor_tensor(ubuf, gbuf, ubuf, OP.mult),
                         reads=[B_aux[0]], writes=[B_aux[1]])
                    b.op("act", lambda E, tA=tA: E.activation(tA, tA, AF.Tanh),
                         reads=[B_gel], writes=[B_gel])
                    b.op("act", lambda E, tA=tA: E.activation(tA, tA, AF.Copy, scale=0.5, bias=0.5),
                         reads=[B_gel], writes=[B_gel])
                    b.op("dve", lambda E, tA=tA, u=ubuf, it=it: E.tensor_tensor(it, tA, u, OP.mult),
                         reads=[B_gel, B_aux[1]], writes=Bits)
                if sli == 3 and sup % 2 == 1:
                    for dt in range(16):
                        for s2 in (sup - 1, sup):
                            wds = wdst[:, ((2 * dt + s2) % 2) * 1024:((2 * dt + s2) % 2) * 1024 + 1024]
                            Bw = B_wdst[(2 * dt + s2) % 2]
                            b.dma("sp", lambda E, s2=s2, dt=dt, d=wds:
                                  E.dma_start(out=d, in_=bass.AP(wd_s.tensor,
                                                                 (s2 * 16 + dt) * 128 * 8 * 128,
                                                                 [[8 * 128, 128], [1, 8 * 128]])),
                                  reads=[B_wd[s2]], writes=[Bw])
                            for th in range(2):
                                dps, Bdps = (p5, P5) if (dt + th) % 2 == 0 else (p6, P6)
                                fns = []
                                rd = [Bw]
                                ip2 = s2 % 2
                                for nt in range(8):
                                    lhs = wds[:, nt * 128:(nt + 1) * 128]
                                    rhs = inter[:, ip2 * 8192 + nt * 1024 + th * 512:
                                                ip2 * 8192 + nt * 1024 + th * 512 + 512]
                                    rd.append(inter_bufs[(ip2, nt, th)])
                                    fns.append(lambda E, lhs=lhs, rhs=rhs, dps=dps,
                                               st=(nt == 0 and s2 == sup - 1),
                                               sp=(nt == 7 and s2 == sup):
                                               E.matmul(dps[:, :], lhs, rhs, start=st, stop=sp))
                                b.group("pe", fns, reads=rd, writes=[Bdps])
                                if s2 == sup:
                                    a = acc[:, dt * 1024 + th * 512:dt * 1024 + th * 512 + 512]
                                    if sup == 1:
                                        b.op("dve", lambda E, a=a, dps=dps:
                                             E.tensor_copy(a, dps[:, :]),
                                             reads=[Bdps], writes=[acc_bufs[dt]])
                                    else:
                                        b.op("dve", lambda E, a=a, dps=dps:
                                             E.tensor_tensor(a, a, dps[:, :], OP.add),
                                             reads=[Bdps], writes=[acc_bufs[dt]])
            for th in range(2):
                tof = th * 512
                wide = [(bass.AP(arA, 2 * i * 1024 + tof, [[16384, 128], [1024, 2], [1, 512]]),
                         [acc_bufs[2 * i], acc_bufs[2 * i + 1]]) for i in range(8)]
                ln_stats(None, wide=wide)
                bc6 = bass.AP(p6, 0, [[512, 128], [0, 2], [1, 512]])
                bc5 = bass.AP(p5, 0, [[512, 128], [0, 2], [1, 512]])
                for i in range(8):
                    a2 = bass.AP(arA, 2 * i * 1024 + tof, [[16384, 128], [1024, 2], [1, 512]])
                    sl2 = i % 2
                    xm2 = xmst[:, 0:1024] if sl2 == 0 else outst[:, 0:1024]
                    Bxm = B_xmst[0] if sl2 == 0 else B_outst[0]
                    rda = [acc_bufs[2 * i], acc_bufs[2 * i + 1]]
                    i_ap = bass.AP(xmid_s.tensor, (seq * 16 + 2 * i) * 128 * S + t0 + tof,
                                   [[S, 128], [128 * S, 2], [1, 512]])
                    b.dma("sp", lambda E, xm2=xm2, i_ap=i_ap:
                          E.dma_start(out=xm2.rearrange("p (c t) -> p c t", c=2), in_=i_ap),
                          reads=[B_xmid[(seq, 2 * half + th)]], writes=[Bxm])
                    b.op("dve", lambda E, a2=a2, bc6=bc6: E.tensor_tensor(a2, a2, bc6, OP.mult),
                         reads=[P6], writes=rda)
                    b.op("dve", lambda E, a2=a2, bc5=bc5: E.tensor_tensor(a2, a2, bc5, OP.subtract),
                         reads=[P5], writes=rda)
                    b.op("dve", lambda E, a2=a2, xm2=xm2: E.tensor_tensor(
                        a2, a2, xm2.rearrange("p (c t) -> p c t", c=2), OP.add),
                         reads=[Bxm], writes=rda)
                    o_ap = bass.AP(out_d.tensor, (seq * 16 + 2 * i) * 128 * S + t0 + tof,
                                   [[S, 128], [128 * S, 2], [1, 512]])
                    b.dma("act", lambda E, a2=a2, o_ap=o_ap: E.dma_start(out=o_ap, in_=a2),
                          reads=rda, writes=[B_out])

    allb = Buf()
    allb.w = dict(b.cnt)
    for name in cfg.get("dump_sbuf", []):
        ap = {"q_sb": arC16[:, 0:2048], "k_sb": arC16[:, 2048:4096],
              "v_tm": arB16[:, 8192:12288], "p_sb": arD16[:, 0:2048],
              "pT": arD16[:, 4096:6144], "z_all": arA16[:, 0:32768],
              "xw": arB16[:, 0:8192], "acc": arA[:, :], "xnw": arB16[:, 0:16384],
              "rows": rows[:, :], "gelx": arD[:, 9216:12288],
              "inter": arC16[:, 0:16384]}[name]
        shp = list(ap.shape)
        dt_ = {2: BF16, 4: F32}[4 if name in ("acc", "rows", "gelx") else 2]
        dd = DT("dbg_" + name, shp, dt_, kind="ExternalOutput").ap()
        b.dma("sp", lambda E, ap=ap, dd=dd: E.dma_start(out=dd[:, :], in_=ap), reads=[allb])
    for name, src, dst in dbg:
        flat_src = bass.AP(src.tensor, 0,
                           [[src.shape[-1], int(np.prod(src.shape)) // src.shape[-1]],
                            [1, src.shape[-1]]])
        b.dma("sp", lambda E, s=flat_src, d=dst: E.dma_start(out=d, in_=s), reads=[allb])

    b.finalize()
    return nc


def _pi():
    pi = np.zeros(2048, np.int64)
    for c in range(8):
        for par in range(2):
            m = 2 * c + par
            pi[m * 128:(m + 1) * 128] = 256 * c + 2 * np.arange(128) + par
    return pi


def prep_inputs(L0_attn_x, L0_attn_W_Q, L0_attn_W_K, L0_attn_W_V, L0_attn_W_O,
                L0_attn_b_Q, L0_attn_b_K, L0_attn_b_V, L0_attn_b_O,
                L0_ffn_gate_W_packed, L0_ffn_gate_scales, L0_ffn_up_W_packed,
                L0_ffn_up_scales, L0_ffn_down_W_packed, L0_ffn_down_scales):
    pi = _pi()
    x = np.ascontiguousarray(
        np.asarray(L0_attn_x).transpose(0, 2, 1)[:, pi, :]).reshape(16, 16, 128, 2048)

    def head_img(w):
        wp = np.asarray(w)[:, pi, :]
        return np.ascontiguousarray(
            wp.reshape(16, 16, 128, 128).transpose(0, 2, 1, 3).reshape(16, 128, 2048))

    wq = head_img(L0_attn_W_Q)
    wk = head_img(L0_attn_W_K)
    wvp = np.asarray(L0_attn_W_V)[:, pi, :]
    wv = np.ascontiguousarray(
        wvp.transpose(1, 0, 2).reshape(16, 128, 2048)
        .reshape(16, 128, 8, 256).transpose(2, 1, 0, 3).reshape(8, 128, 4096))
    wo = np.asarray(L0_attn_W_O).reshape(2048, 2048)[:, pi]
    wo2 = np.ascontiguousarray(
        wo.reshape(16, 128, 16, 128).transpose(2, 1, 0, 3).reshape(16, 128, 2048))
    bq = np.ascontiguousarray(np.asarray(L0_attn_b_Q, dtype=np.float32))
    bk = np.ascontiguousarray(np.asarray(L0_attn_b_K, dtype=np.float32))
    bvg = np.ascontiguousarray(np.asarray(L0_attn_b_V, dtype=np.float32).reshape(1, 2048))
    bo = np.ascontiguousarray(np.asarray(L0_attn_b_O, dtype=np.float32)[pi].reshape(16, 128))

    def u8(a):
        return np.asarray(a).astype(np.uint8)

    pg = np.ascontiguousarray(u8(L0_ffn_gate_W_packed).T)
    pu = np.ascontiguousarray(u8(L0_ffn_up_W_packed).T)
    pd = np.ascontiguousarray(u8(L0_ffn_down_W_packed).T[:, pi])
    sg = np.ascontiguousarray(np.asarray(L0_ffn_gate_scales, dtype=np.float32).T)
    su = np.ascontiguousarray(np.asarray(L0_ffn_up_scales, dtype=np.float32).T)
    sd = np.ascontiguousarray(np.asarray(L0_ffn_down_scales, dtype=np.float32).T[:, pi])
    idn = np.eye(128, dtype=np.float32)
    msk = np.where(np.arange(128)[None, :] > np.arange(128)[:, None],
                   np.float32(NEG), np.float32(0.0)).astype(np.float32)
    shared = dict(wq=wq, wk=wk, wv=wv, wo=wo2, bq=bq, bk=bk, bvg=bvg, bo=bo,
                  pg=pg, pu=pu, pd=pd, sg=sg, su=su, sd=sd, idn=idn, msk=msk)
    in_maps = []
    for core in range(8):
        m = dict(shared)
        m["x"] = np.ascontiguousarray(x[2 * core:2 * core + 2])
        in_maps.append(m)
    return in_maps, pi


def kernel(**inputs):
    in_maps, pi = prep_inputs(**inputs)
    nc = build()
    res = run_bass_kernel_spmd(nc, in_maps, core_ids=list(range(8)))
    inv = np.argsort(pi)
    out = np.empty((16, 2048, 2048), np.float32)
    for core in range(8):
        o = np.asarray(res.results[core]["out"]).reshape(2, 2048, 2048)
        out[2 * core:2 * core + 2] = o[:, inv, :].transpose(0, 2, 1)
    return out


if __name__ == "__main__":
    n = build()
    print("built ok; instructions:", n.m.n_instructions() if hasattr(n, "m") else "?")


# version 11
# speedup vs baseline: 1.2508x; 1.0633x over previous
import numpy as np
import concourse.bass as bass
import concourse.mybir as mybir
from concourse.bass_utils import run_bass_kernel_spmd

F32 = mybir.dt.float32
BF16 = mybir.dt.bfloat16
U8 = mybir.dt.uint8
U16 = mybir.dt.uint16
AF = mybir.ActivationFunctionType
OP = mybir.AluOpType
AX = mybir.AxisListType

D = 2048
S = 2048
NSEQ = 2
MLP = 16384
EPS = 1e-5
ASCALE = 1.0 / float(np.sqrt(np.float32(128.0)))
C0 = 0.7978845608028654
C1 = 0.044715
NEG = -100000.0

CFG = {
    "seqs": [0, 1],
    "heads": list(range(16)),
    "ffn_groups": [0, 1, 2, 3],
    "do_attn": True,
    "do_wo": True,
    "do_ffn": True,
    "deq_upfront": False,
    "debug_outs": [],
}


def _merge(dst, src):
    for k, v in src.items():
        if dst.get(k, -1) < v:
            dst[k] = v


class Buf:
    __slots__ = ("w", "r")

    def __init__(self):
        self.w = {}
        self.r = {}


class Bld:
    ENG = ("pe", "dve", "act", "pool", "sp")

    NROT = 16

    def __init__(self, nc):
        self.nc = nc
        self.q = {e: [] for e in self.ENG}
        self.sem = {}
        names = ["Epe", "Edve", "Eact", "Epool"]
        for qn in ("Qsp", "Qact", "Qpool"):
            names += ["%s_%d" % (qn, i) for i in range(self.NROT)]
        for s in names:
            self.sem[s] = nc.alloc_semaphore(s)
        self.cnt = {s: 0 for s in self.sem}
        self.dma_i = {"Qsp": 0, "Qact": 0, "Qpool": 0}
        self.wm = {e: {} for e in self.ENG}

    def _waits(self, eng, dep):
        for s in sorted(dep):
            v = dep[s]
            if self.wm[eng].get(s, 0) < v:
                so = self.sem[s]
                self.q[eng].append(lambda E, so=so, v=v: E.wait_ge(so, v))
                self.wm[eng][s] = v

    def _dep(self, reads, writes):
        dep = {}
        for b in reads:
            _merge(dep, b.w)
        for b in writes:
            _merge(dep, b.w)
            _merge(dep, b.r)
        return dep

    def _fin(self, h, reads, writes):
        for b in reads:
            _merge(b.r, h)
        for b in writes:
            _merge(b.w, h)
            b.r = {}

    def op(self, eng, fn, reads=(), writes=()):
        self._waits(eng, self._dep(reads, writes))
        s = "E" + eng
        self.cnt[s] += 1
        so = self.sem[s]
        self.q[eng].append(lambda E, fn=fn, so=so: fn(E).then_inc(so, 1))
        h = {s: self.cnt[s]}
        self._fin(h, reads, writes)
        return h

    def group(self, eng, fns, reads=(), writes=()):
        self._waits(eng, self._dep(reads, writes))
        s = "E" + eng
        self.cnt[s] += 1
        so = self.sem[s]
        for fn in fns[:-1]:
            self.q[eng].append(lambda E, fn=fn: fn(E))
        self.q[eng].append(lambda E, fn=fns[-1], so=so: fn(E).then_inc(so, 1))
        h = {s: self.cnt[s]}
        self._fin(h, reads, writes)
        return h

    def dma(self, qeng, fn, reads=(), writes=()):
        dep = self._dep(reads, writes)
        qn = "Q" + qeng
        i = self.dma_i[qn]
        self.dma_i[qn] = i + 1
        s = "%s_%d" % (qn, i % self.NROT)
        if self.cnt[s] > 0:
            _merge(dep, {s: self.cnt[s]})
        self._waits(qeng, dep)
        self.cnt[s] += 16
        so = self.sem[s]
        self.q[qeng].append(lambda E, fn=fn, so=so: fn(E).then_inc(so, 16))
        h = {s: self.cnt[s]}
        self._fin(h, reads, writes)
        return h

    def finalize(self):
        for s in sorted(self.cnt):
            c = self.cnt[s]
            if c > 0:
                so = self.sem[s]
                self.q["sp"].append(lambda E, so=so, v=c: E.wait_ge(so, v))
        with self.nc.Block() as block:
            block.tensor(lambda E: ([f(E) for f in self.q["pe"]], None)[1])
            block.vector(lambda E: ([f(E) for f in self.q["dve"]], None)[1])
            block.scalar(lambda E: ([f(E) for f in self.q["act"]], None)[1])
            block.gpsimd(lambda E: ([f(E) for f in self.q["pool"]], None)[1])
            block.sync(lambda E: ([f(E) for f in self.q["sp"]], None)[1])


def build(cfg=None):
    if cfg is None:
        cfg = CFG
    nc = bass.Bass("TRN2", target_bir_lowering=False)
    DT = nc.dram_tensor
    x_in = DT("x", [NSEQ, 16, 128, S], F32, kind="ExternalInput").ap()
    wq_in = DT("wq", [16, 128, 2048], F32, kind="ExternalInput").ap()
    wk_in = DT("wk", [16, 128, 2048], F32, kind="ExternalInput").ap()
    wv_in = DT("wv", [8, 128, 4096], F32, kind="ExternalInput").ap()
    wo_in = DT("wo", [16, 128, 2048], F32, kind="ExternalInput").ap()
    bqk_in = DT("bqk", [8, 128, 4], F32, kind="ExternalInput").ap()
    bvg_in = DT("bvg", [1, 2048], F32, kind="ExternalInput").ap()
    bo_in = DT("bo", [16, 128], F32, kind="ExternalInput").ap()
    pg_in = DT("pg", [1024, MLP], U8, kind="ExternalInput").ap()
    pu_in = DT("pu", [1024, MLP], U8, kind="ExternalInput").ap()
    pd_in = DT("pd", [8192, 2048], U8, kind="ExternalInput").ap()
    sg_in = DT("sg", [64, MLP], F32, kind="ExternalInput").ap()
    su_in = DT("su", [64, MLP], F32, kind="ExternalInput").ap()
    sd_in = DT("sd", [512, 2048], F32, kind="ExternalInput").ap()
    idn_in = DT("idn", [128, 128], F32, kind="ExternalInput").ap()
    msk_in = DT("msk", [128, 128], F32, kind="ExternalInput").ap()

    xln1_s = DT("xln1_s", [NSEQ, 16, 128, S], BF16, kind="Internal").ap()
    xmid_s = DT("xmid_s", [NSEQ, 16, 128, S], F32, kind="Internal").ap()
    xn2_s = DT("xn2_s", [NSEQ, 16, 128, S], BF16, kind="Internal").ap()
    wqb_s = DT("wqb_s", [16, 128, 2048], BF16, kind="Internal").ap()
    wkb_s = DT("wkb_s", [16, 128, 2048], BF16, kind="Internal").ap()
    wvb_s = DT("wvb_s", [8, 128, 4096], BF16, kind="Internal").ap()
    wob_s = DT("wob_s", [16, 128, 2048], BF16, kind="Internal").ap()
    wg_s = DT("wg_s", [64, 128, 16, 256], BF16, kind="Internal").ap()
    wu_s = DT("wu_s", [64, 128, 16, 256], BF16, kind="Internal").ap()
    wd_s = DT("wd_s", [16, 16, 128, 8, 128], BF16, kind="Internal").ap()
    out_d = DT("out", [NSEQ, 16, 128, S], F32, kind="ExternalOutput").ap()

    dbg = []
    for name in cfg["debug_outs"]:
        src = {"xln1_s": xln1_s, "xmid_s": xmid_s, "xn2_s": xn2_s,
               "wg_s": wg_s, "wu_s": wu_s, "wd_s": wd_s}[name]
        n = int(np.prod(src.shape))
        dbg.append((name, src, DT("dbg_" + name, [n // src.shape[-1], src.shape[-1]],
                                  src.dtype, kind="ExternalOutput").ap()))

    SB = nc.alloc_sbuf_tensor
    arA = SB("arA", [128, 16384], F32)
    arB = SB("arB", [128, 8192], F32)
    arC = SB("arC", [128, 8192], F32)
    arD = SB("arD", [128, 12288], F32)
    arA16 = arA.bitcast(BF16)
    arB16 = arB.bitcast(BF16)
    arC16 = arC.bitcast(BF16)
    arD16 = arD.bitcast(BF16)
    arD8 = arD.bitcast(U8)
    arDu16 = arD.bitcast(U16)

    ident = SB("ident", [128, 128], BF16)
    mask = SB("mask", [128, 128], F32)
    ones_cb = SB("ones_cb", [128, 1], BF16)
    ones_r = SB("ones_r", [1, 128], F32)
    ones_rb = SB("ones_rb", [1, 128], BF16)
    rows = SB("rows", [1, 2560], F32)
    cols = SB("cols", [128, 12], F32)
    bvg_sb = SB("bvg_sb", [1, 2048], BF16)
    maskb = SB("maskb", [128, 128], BF16)
    aux8 = SB("aux8", [128, 2048], F32)
    x2t = SB("x2t", [128, 2048], BF16)
    xbt = SB("xbt", [128, 2048], BF16)

    pbig = nc.alloc_psum_tensor("pbig", [128, 2048], F32)
    p4b = nc.alloc_psum_tensor("p4b", [128, 1024], BF16)
    p5 = nc.alloc_psum_tensor("p5", [128, 512], F32)
    p6 = nc.alloc_psum_tensor("p6", [128, 512], F32)
    p7 = nc.alloc_psum_tensor("p7", [128, 512], F32)

    b = Bld(nc)
    AR = {"A": [], "B": [], "C": [], "D": []}

    def nb(ar=None, lo=0, hi=1 << 30):
        bf_ = Buf()
        if ar is not None:
            for (l2, h2, b2) in AR[ar]:
                if l2 < hi and lo < h2:
                    _merge(bf_.w, b2.w)
                    _merge(bf_.w, b2.r)
            AR[ar].append((lo, hi, bf_))
        return bf_

    PB = [nb() for _ in range(4)]
    P4, P5, P6, P7 = nb(), nb(), nb(), nb()
    B_rows, B_cols = nb(), nb()
    B_x2t = [nb(), nb()]
    B_xbt = [nb(), nb()]

    R_mean = rows[0:1, 0:512]
    R_msq = rows[0:1, 512:1024]
    R_var = rows[0:1, 1024:1536]
    R_rstd = rows[0:1, 1536:2048]
    R_nmr = rows[0:1, 2048:2560]
    R_tmp = rows[0:1, 512:1024]
    C_mx = [cols[:, 0:1], cols[:, 4:5]]
    C_nb = [cols[:, 1:2], cols[:, 5:6]]
    C_sum = [cols[:, 2:3], cols[:, 6:7]]
    C_rs = [cols[:, 3:4], cols[:, 7:8]]
    C_bq = cols[:, 8:9]
    C_bk = cols[:, 9:10]
    C_bo = cols[:, 10:11]

    c_ident = nb(); c_mask = nb(); c_ones = nb(); c_bvg = nb()
    b.dma("pool", lambda E: E.dma_start(out=ident[:], in_=idn_in[:, :]), writes=[c_ident])
    b.dma("sp", lambda E: E.dma_start(out=mask[:], in_=msk_in[:, :]), writes=[c_mask])
    b.op("dve", lambda E: E.tensor_copy(maskb[:], mask[:]), reads=[c_mask], writes=[c_mask])
    b.dma("pool", lambda E: E.dma_start(out=bvg_sb[:], in_=bvg_in[:, :]), writes=[c_bvg])
    b.op("pool", lambda E: E.memset(ones_cb[:], 1.0), writes=[c_ones])
    b.op("pool", lambda E: E.memset(ones_r[:], 1.0), writes=[c_ones])
    b.op("pool", lambda E: E.memset(ones_rb[:], 1.0), writes=[c_ones])

    def ln_stats(tiles, wide=None):
        for i in range(0, 16, 2):
            sl = (i // 2) % 2
            xb = xbt[:, sl * 1024:sl * 1024 + 1024]
            x2 = x2t[:, sl * 1024:sl * 1024 + 1024]
            if wide is not None:
                wap, wbufs = wide[i // 2]
                b.op("dve", lambda E, tl=wap, xb=xb: E.tensor_copy(xb, tl),
                     reads=wbufs, writes=[B_xbt[sl]])
                b.op("act", lambda E, tl=wap, x2=x2: E.activation(x2, tl, AF.Square),
                     reads=wbufs, writes=[B_x2t[sl]])
            else:
                for k in range(2):
                    tl, tb = tiles[i + k]
                    b.op("dve", lambda E, tl=tl, d=xb[:, k * 512:k * 512 + 512]:
                         E.tensor_copy(d, tl), reads=[tb], writes=[B_xbt[sl]])
                    b.op("act", lambda E, tl=tl, d=x2[:, k * 512:k * 512 + 512]:
                         E.activation(d, tl, AF.Square), reads=[tb], writes=[B_x2t[sl]])
            st, sp = (i == 0), (i == 14)
            for k in range(2):
                stk = st and k == 0
                spk = sp and k == 1
                b.group("pe", [lambda E, xb=xb[:, k * 512:k * 512 + 512], st=stk, sp=spk:
                               E.matmul(p7[0:1, :], ones_cb[:], xb, start=st, stop=sp)],
                        reads=[B_xbt[sl], c_ones], writes=[P7])
                b.group("pe", [lambda E, x2=x2[:, k * 512:k * 512 + 512], st=stk, sp=spk:
                               E.matmul(p5[0:1, :], ones_cb[:], x2, start=st, stop=sp)],
                        reads=[B_x2t[sl], c_ones], writes=[P5])
        b.op("dve", lambda E: E.tensor_scalar(R_mean, p7[0:1, :], 1.0 / D, None, OP.mult),
             reads=[P7], writes=[B_rows])
        b.op("dve", lambda E: E.tensor_scalar(R_msq, p5[0:1, :], 1.0 / D, None, OP.mult),
             reads=[P5], writes=[B_rows])
        b.op("dve", lambda E: E.tensor_mul(R_var, R_mean, R_mean), reads=[B_rows], writes=[B_rows])
        b.op("dve", lambda E: E.scalar_tensor_tensor(R_var, R_msq, EPS, R_var, OP.add, OP.subtract),
             reads=[B_rows], writes=[B_rows])
        b.op("act", lambda E: E.activation(R_tmp, R_var, AF.Ln), reads=[B_rows], writes=[B_rows])
        b.op("act", lambda E: E.activation(R_rstd, R_tmp, AF.Exp, scale=-0.5),
             reads=[B_rows], writes=[B_rows])
        b.op("dve", lambda E: E.tensor_mul(R_nmr, R_mean, R_rstd), reads=[B_rows], writes=[B_rows])
        b.group("pe", [lambda E: E.matmul(p6[:, :], ones_r[:], R_rstd, start=True, stop=True)],
                reads=[B_rows, c_ones], writes=[P6])
        b.group("pe", [lambda E: E.matmul(p5[:, :], ones_r[:], R_nmr, start=True, stop=True)],
                reads=[B_rows, c_ones], writes=[P5])

    B_wqb = nb(); B_wkb = nb(); B_wvb = nb(); B_wob = nb()
    cv_in = [nb("A"), nb("A")]
    cv_out = [nb("A"), nb("A")]
    cv_jobs = []
    for h in range(16):
        cv_jobs.append((wq_in[h, :, :], wqb_s[h, :, :], 2048, B_wqb))
        cv_jobs.append((wk_in[h, :, :], wkb_s[h, :, :], 2048, B_wkb))
        cv_jobs.append((wo_in[h, :, :], wob_s[h, :, :], 2048, B_wob))
    for g_ in range(8):
        cv_jobs.append((wv_in[g_, :, :], wvb_s[g_, :, :], 4096, B_wvb))
    for ji, (src, dst, w_, bd) in enumerate(cv_jobs):
        sl = ji % 2
        fa = arA[:, sl * 4096:sl * 4096 + w_]
        fb = arA16[:, 16384 + sl * 4096:16384 + sl * 4096 + w_]
        b.dma("sp", lambda E, src=src, fa=fa: E.dma_start(out=fa, in_=src),
              writes=[cv_in[sl]])
        b.op("dve", lambda E, fa=fa, fb=fb: E.tensor_copy(fb, fa),
             reads=[cv_in[sl]], writes=[cv_out[sl]])
        b.dma("act", lambda E, dst=dst, fb=fb: E.dma_start(out=dst, in_=fb),
              reads=[cv_out[sl]], writes=[bd])

    B_xln1 = {s: nb() for s in range(NSEQ)}
    for seq in cfg["seqs"]:
        xw = [arA[:, i * 512:(i + 1) * 512] for i in range(16)]
        xw_bufs = [nb("A") for _ in range(16)]
        stg = [arA16[:, 16384 + i * 512:16384 + (i + 1) * 512] for i in range(16)]
        stg_bufs = [nb("A") for _ in range(16)]
        for tg in range(4):
            t0 = tg * 512
            for c in range(16):
                b.dma("sp", lambda E, c=c, t0=t0, seq=seq, d=xw[c]:
                      E.dma_start(out=d, in_=x_in[seq, c, :, t0:t0 + 512]),
                      writes=[xw_bufs[c]])
            ln_stats(None, wide=[(arA[:, i * 1024:(i + 1) * 1024],
                                  [xw_bufs[2 * i], xw_bufs[2 * i + 1]]) for i in range(8)])
            bc6 = bass.AP(p6, 0, [[512, 128], [0, 4], [1, 512]])
            bc5 = bass.AP(p5, 0, [[512, 128], [0, 4], [1, 512]])
            for i in range(4):
                src = arA[:, i * 2048:(i + 1) * 2048]
                d = arA16[:, 16384 + i * 2048:16384 + (i + 1) * 2048]
                rd = xw_bufs[4 * i:4 * i + 4]
                wr = stg_bufs[4 * i:4 * i + 4]
                b.op("dve", lambda E, s_=src, d=d, bc6=bc6: E.tensor_tensor(
                    d.rearrange("p (c t) -> p c t", c=4), s_.rearrange("p (c t) -> p c t", c=4),
                    bc6, OP.mult), reads=rd + [P6], writes=wr)
                b.op("dve", lambda E, d=d, bc5=bc5: E.tensor_tensor(
                    d.rearrange("p (c t) -> p c t", c=4), d.rearrange("p (c t) -> p c t", c=4),
                    bc5, OP.subtract), reads=[P5], writes=wr)
                o_ap = bass.AP(xln1_s.tensor, (seq * 16 + 4 * i) * 128 * S + t0,
                               [[S, 128], [128 * S, 4], [1, 512]])
                b.dma("act", lambda E, d=d, o_ap=o_ap:
                      E.dma_start(out=o_ap, in_=d.rearrange("p (c t) -> p c t", c=4)),
                      reads=wr, writes=[B_xln1[seq]])

    deq_tasks = []
    if cfg["do_ffn"]:
        for kpc in range(8):
            for ns in range(16):
                deq_tasks.append(("g", kpc, ns))
        for kpc in range(8):
            for ns in range(16):
                deq_tasks.append(("u", kpc, ns))
        for npc in range(64):
            for ds in range(2):
                deq_tasks.append(("d", npc, ds))
    B_wg = {i: nb() for i in range(64)}
    B_wu = {i: nb() for i in range(64)}
    B_wd = {i: nb() for i in range(16)}

    dq_pk = [arD8[:, 24576 + i * 1024:24576 + (i + 1) * 1024] for i in range(2)]
    dq_sc = [arD[:, 6656 + i * 1024:6656 + (i + 1) * 1024] for i in range(2)]
    dqb = 34816
    DQ = {
        "pw": arDu16[:, dqb // 2:dqb // 2 + 1024],
        "n": arDu16[:, (dqb + 2048) // 2:(dqb + 2048) // 2 + 1024],
        "m": arDu16[:, (dqb + 4096) // 2:(dqb + 4096) // 2 + 1024],
        "t1": arDu16[:, (dqb + 6144) // 2:(dqb + 6144) // 2 + 1024],
        "b0": arDu16[:, (dqb + 8192) // 2:(dqb + 8192) // 2 + 1024],
        "bits": arDu16[:, (dqb + 8192) // 2:(dqb + 8192) // 2 + 1024],
        "g0": arD[:, (dqb + 10240) // 4:(dqb + 10240) // 4 + 1024],
        "out": arD16[:, (dqb + 6144) // 2:(dqb + 6144) // 2 + 1024],
    }
    B_dq_pk = [nb("D"), nb("D")]
    B_dq_sc = [nb("D"), nb("D")]
    B_dq = {k: nb("D") for k in ("pw", "n", "m", "t1", "b0", "g0")}
    B_dq["bits"] = B_dq["b0"]
    B_dq["out"] = B_dq["t1"]
    dq_state = {"i": 0, "slot": 0}

    def emit_dequant(n_tiles):
        for _ in range(n_tiles):
            i = dq_state["i"]
            if i >= len(deq_tasks):
                return
            dq_state["i"] += 1
            kind, a0, a1 = deq_tasks[i]
            slot = dq_state["slot"]
            dq_state["slot"] ^= 1
            pk = dq_pk[slot]
            sc = dq_sc[slot]
            bpk = B_dq_pk[slot]
            bsc = B_dq_sc[slot]
            if kind in ("g", "u"):
                src = pg_in if kind == "g" else pu_in
                ssrc = sg_in if kind == "g" else su_in
                b.dma("sp", lambda E, src=src, a0=a0, a1=a1, pk=pk:
                      E.dma_start(out=pk, in_=src[a0 * 128:(a0 + 1) * 128,
                                                  a1 * 1024:(a1 + 1) * 1024]),
                      writes=[bpk])
                rep = bass.AP(ssrc.tensor, 8 * a0 * MLP + a1 * 1024,
                              [[MLP, 8], [0, 16], [1, 1024]])
                b.dma("sp", lambda E, rep=rep, sc=sc: E.dma_start(out=sc, in_=rep),
                      writes=[bsc])
            else:
                npc, ds = a0, a1
                b.dma("sp", lambda E, npc=npc, ds=ds, pk=pk:
                      E.dma_start(out=pk, in_=pd_in[npc * 128:(npc + 1) * 128,
                                                    ds * 1024:(ds + 1) * 1024]),
                      writes=[bpk])
                rep = bass.AP(sd_in.tensor, 8 * npc * 2048 + ds * 1024,
                              [[2048, 8], [0, 16], [1, 1024]])
                b.dma("sp", lambda E, rep=rep, sc=sc: E.dma_start(out=sc, in_=rep),
                      writes=[bsc])
            q = DQ
            Bq = B_dq
            b.op("dve", lambda E, pk=pk, pw=q["pw"]: E.tensor_copy(pw, pk),
                 reads=[bpk], writes=[Bq["pw"]])
            for par in (0, 1):
                if par == 0:
                    b.op("dve", lambda E, pw=q["pw"], n=q["n"]:
                         E.tensor_single_scalar(n, pw, 15, OP.bitwise_and),
                         reads=[Bq["pw"]], writes=[Bq["n"]])
                else:
                    b.op("dve", lambda E, pw=q["pw"], n=q["n"]:
                         E.tensor_single_scalar(n, pw, 4, OP.logical_shift_right),
                         reads=[Bq["pw"]], writes=[Bq["n"]])
                b.op("dve", lambda E, n=q["n"], t1=q["t1"]:
                     E.tensor_scalar(t1, n, 8, 0x7E00, OP.is_ge, OP.mult),
                     reads=[Bq["n"]], writes=[Bq["t1"]])
                b.op("dve", lambda E, n=q["n"], b0=q["b0"]:
                     E.tensor_scalar(b0, n, 64, 0x3F00, OP.mult, OP.add),
                     reads=[Bq["n"]], writes=[Bq["b0"]])
                b.op("dve", lambda E, b0=q["b0"], t1=q["t1"], bi=q["bits"]:
                     E.tensor_tensor(bi, b0, t1, OP.add),
                     reads=[Bq["t1"]], writes=[Bq["bits"]])
                b.op("dve", lambda E, n=q["n"], m=q["m"]:
                     E.tensor_single_scalar(m, n, 7, OP.bitwise_and),
                     reads=[Bq["n"]], writes=[Bq["m"]])
                b.op("dve", lambda E, m=q["m"], g0=q["g0"]:
                     E.tensor_scalar(g0, m, 2.0 / 3.0, 1.0, OP.mult, OP.min),
                     reads=[Bq["m"]], writes=[Bq["g0"]])
                b.op("pool", lambda E, g0=q["g0"], sc=sc:
                     E.tensor_tensor(g0, g0, sc, OP.mult),
                     reads=[bsc], writes=[Bq["g0"]])
                b.op("pool", lambda E, g0=q["g0"], bi=q["bits"], ot=q["out"]:
                     E.tensor_tensor(ot, g0, bi.bitcast(BF16), OP.mult),
                     reads=[Bq["g0"], Bq["bits"]], writes=[Bq["out"]])
                ot = q["out"]
                if kind in ("g", "u"):
                    tgt = wg_s if kind == "g" else wu_s
                    Bt = B_wg if kind == "g" else B_wu
                    m_ = 2 * a0 + par
                    o_ap = bass.AP(tgt.tensor, (4 * a1) * 128 * 16 * 256 + m_ * 256,
                                   [[16 * 256, 128], [128 * 16 * 256, 4], [1, 256]])
                    b.dma("act", lambda E, o_ap=o_ap, ot=ot:
                          E.dma_start(out=o_ap, in_=ot.rearrange("p (w n) -> p w n", w=4)),
                          reads=[Bq["out"]],
                          writes=[Bt[4 * a1 + w] for w in range(4)])
                else:
                    npc, ds = a0, a1
                    j = 2 * npc + par
                    sup, nt = j // 8, j % 8
                    o_ap = bass.AP(wd_s.tensor,
                                   (sup * 16 + ds * 8) * 128 * 8 * 128 + nt * 128,
                                   [[8 * 128, 128], [128 * 8 * 128, 8], [1, 128]])
                    b.dma("act", lambda E, o_ap=o_ap, ot=ot:
                          E.dma_start(out=o_ap, in_=ot.rearrange("p (t d) -> p t d", t=8)),
                          reads=[Bq["out"]], writes=[B_wd[sup]])

    B_z = {}
    B_bo_shared = nb()
    B_xmid = {(s, tg): nb() for s in range(NSEQ) for tg in range(4)}
    B_xn2 = {(s, tg): nb() for s in range(NSEQ) for tg in range(4)}
    B_out = nb()

    if cfg["do_attn"]:
        for seq in cfg["seqs"]:
            z_all = arA16
            B_z[seq] = nb("A")
            B_xw = nb("B")
            B_vtm = nb("B")
            B_ssb = [nb("B"), nb("B")]
            B_qk = [[nb("C"), nb("C")], [nb("C"), nb("C")]]
            B_wvst = nb("C")
            B_wq = nb("C")
            B_wk = nb("C")
            B_psb = [nb("D"), nb("D")]
            B_pT = [nb("D"), nb("D")]
            B_bqk = B_bo_shared
            B_colside = [nb(), nb()]
            xw = arB16[:, 0:8192]
            v_tm = arB16[:, 8192:12288]
            s_sb = [arB16[:, 12288 + sd * 2048:12288 + (sd + 1) * 2048] for sd in range(2)]
            qk_sb = [[arC16[:, (2 * sd) * 2048:(2 * sd) * 2048 + 2048],
                      arC16[:, (2 * sd + 1) * 2048:(2 * sd + 1) * 2048 + 2048]]
                     for sd in range(2)]
            wvst = arC16[:, 8192:12288]
            wqst = arC16[:, 12288:14336]
            wkst = arC16[:, 14336:16384]
            p_sb = [arD16[:, sd * 2048:(sd + 1) * 2048] for sd in range(2)]
            pT = [arD16[:, 4096 + sd * 2048:4096 + (sd + 1) * 2048] for sd in range(2)]
            for pair in range(8):
                heads = [h for h in cfg["heads"] if h // 2 == pair]
                if not heads:
                    continue
                b.dma("act", lambda E, pair=pair, d=wvst:
                      E.dma_start(out=d, in_=wvb_s[pair, :, :]),
                      reads=[B_wvb], writes=[B_wvst])
                b.dma("sp", lambda E, pair=pair:
                      E.dma_start(out=cols[:, 8:12], in_=bqk_in[pair, :, :]),
                      writes=[B_bqk])
                b.dma("act", lambda E, h_=heads[0], d=wqst:
                      E.dma_start(out=d, in_=wqb_s[h_, :, :]),
                      reads=[B_wqb], writes=[B_wq])
                b.dma("act", lambda E, h_=heads[0], d=wkst:
                      E.dma_start(out=d, in_=wkb_s[h_, :, :]),
                      reads=[B_wkb], writes=[B_wk])
                wq2st = arD16[:, 8192:10240]
                wk2st = arD16[:, 10240:12288]
                B_wq2 = nb("D")
                B_wk2 = nb("D")
                if len(heads) > 1:
                    b.dma("act", lambda E, h_=heads[1], d=wq2st:
                          E.dma_start(out=d, in_=wqb_s[h_, :, :]),
                          reads=[B_wqb], writes=[B_wq2])
                    b.dma("act", lambda E, h_=heads[1], d=wk2st:
                          E.dma_start(out=d, in_=wkb_s[h_, :, :]),
                          reads=[B_wkb], writes=[B_wk2])
                B_xwh = [nb("B"), nb("B")]
                for ts_ in range(4):
                    t0 = ts_ * 512
                    for hf in range(2):
                        i_ap = bass.AP(xln1_s.tensor,
                                       (seq * 16 + 8 * hf) * 128 * S + t0,
                                       [[S, 128], [128 * S, 8], [1, 512]])
                        d = xw[:, hf * 4096:(hf + 1) * 4096]
                        b.dma("sp" if hf == 0 else "act",
                              lambda E, d=d, i_ap=i_ap:
                              E.dma_start(out=d.rearrange("p (c t) -> p c t", c=8), in_=i_ap),
                              reads=[B_xln1[seq]], writes=[B_xwh[hf]])
                    for tt in range(4):
                        ps, Bp = (p7, P7) if tt % 2 == 0 else (p6, P6)
                        for hf in range(2):
                            fns = []
                            for c in range(8 * hf, 8 * hf + 8):
                                lhs = xw[:, c * 512 + tt * 128:c * 512 + tt * 128 + 128]
                                rhs = wvst[:, c * 256:(c + 1) * 256]
                                fns.append(lambda E, lhs=lhs, rhs=rhs, ps=ps, st=(c == 0):
                                           E.matmul(ps[:, 0:256], lhs, rhs, start=st, stop=False))
                            if hf == 1:
                                fns.append(lambda E, ps=ps, pair=pair:
                                           E.matmul(ps[:, 0:256], ones_rb[:],
                                                    bvg_sb[0:1, pair * 256:(pair + 1) * 256],
                                                    start=False, stop=True))
                            b.group("pe", fns, reads=[B_xwh[hf], B_wvst, c_bvg, c_ones], writes=[Bp])
                        gt = ts_ * 4 + tt
                        b.op("dve", lambda E, ps=ps, d=v_tm[:, gt * 256:(gt + 1) * 256]:
                             E.tensor_copy(d, ps[:, 0:256]), reads=[Bp], writes=[B_vtm])
                    for sd, h_ in enumerate(heads):
                        wq_st = wqst if sd == 0 else wq2st
                        wk_st = wkst if sd == 0 else wk2st
                        bwq = B_wq if sd == 0 else B_wq2
                        bwk = B_wk if sd == 0 else B_wk2
                        for which, ps, Bp in ((0, p7, P7), (1, p6, P6)):
                            wst = wq_st if which == 0 else wk_st
                            bwst = bwq if which == 0 else bwk
                            bcol = cols[:, 8 + 2 * sd + which:9 + 2 * sd + which]
                            for hf in range(2):
                                fns = []
                                for c in range(8 * hf, 8 * hf + 8):
                                    lhs = wst[:, c * 128:(c + 1) * 128]
                                    rhs = xw[:, c * 512:(c + 1) * 512]
                                    fns.append(lambda E, lhs=lhs, rhs=rhs, ps=ps,
                                               st=(c == 0), sp=(c == 15):
                                               E.matmul(ps[:, :], lhs, rhs, start=st, stop=sp))
                                b.group("pe", fns, reads=[B_xwh[hf], bwst], writes=[Bp])
                            b.op("dve", lambda E, ps=ps, d=qk_sb[sd][which][:, t0:t0 + 512], bc=bcol:
                                 E.tensor_single_scalar(d, ps[:, :], bc, OP.add),
                                 reads=[Bp, B_bqk], writes=[B_qk[sd][which]])
                    emit_dequant(1)
                for band in ([] if cfg.get("no_bands") else range(4)):
                    for qs in range(4):
                        qi = band * 4 + qs
                        L = (qi + 1) * 128
                        nbank = (L + 511) // 512
                        for sd, h_ in enumerate(heads):
                            for j in range(nbank):
                                n = min(512, L - j * 512)
                                bank = 2 * sd + (j % 2)
                                b.group("pe", [lambda E, qi=qi, j=j, n=n, bank=bank, sd=sd:
                                               E.matmul(pbig[:, bank * 512:bank * 512 + n],
                                                        qk_sb[sd][0][:, qi * 128:qi * 128 + 128],
                                                        qk_sb[sd][1][:, j * 512:j * 512 + n],
                                                        start=True, stop=True)],
                                        reads=[B_qk[sd][0], B_qk[sd][1]], writes=[PB[bank]])
                                b.op("act", lambda E, j=j, n=n, bank=bank, sd=sd:
                                     E.activation(s_sb[sd][:, j * 512:j * 512 + n],
                                                  pbig[:, bank * 512:bank * 512 + n], AF.Copy),
                                     reads=[PB[bank]], writes=[B_ssb[sd]])
                            b.op("dve", lambda E, L=L, sd=sd:
                                 E.reduce_max(C_mx[sd], s_sb[sd][:, 0:L], axis=AX.X),
                                 reads=[B_ssb[sd]], writes=[B_colside[sd]])
                            b.op("dve", lambda E, L=L, sd=sd:
                                 E.tensor_tensor(s_sb[sd][:, L - 128:L], s_sb[sd][:, L - 128:L],
                                                 maskb[:], OP.add),
                                 reads=[c_mask], writes=[B_ssb[sd]])
                            b.op("dve", lambda E, sd=sd:
                                 E.tensor_scalar(C_nb[sd], C_mx[sd], -ASCALE, None, OP.mult),
                                 reads=[B_colside[sd]], writes=[B_colside[sd]])
                            b.op("act", lambda E, L=L, sd=sd:
                                 E.activation(p_sb[sd][:, 0:L], s_sb[sd][:, 0:L], AF.Exp,
                                              bias=C_nb[sd], scale=ASCALE, accum_out=C_sum[sd]),
                                 reads=[B_ssb[sd], B_colside[sd]],
                                 writes=[B_psb[sd], B_colside[sd]])
                            b.op("dve", lambda E, sd=sd: E.reciprocal(C_rs[sd], C_sum[sd]),
                                 reads=[B_colside[sd]], writes=[B_colside[sd]])
                            b.op("dve", lambda E, L=L, sd=sd:
                                 E.tensor_single_scalar(p_sb[sd][:, 0:L], p_sb[sd][:, 0:L],
                                                        C_rs[sd], OP.mult),
                                 reads=[B_colside[sd]], writes=[B_psb[sd]])
                        emit_dequant(1)
                        for sd, h_ in enumerate(heads):
                            hh = h_ % 2
                            for j0 in range(0, qi + 1, 4):
                                jn = min(4, qi + 1 - j0)
                                fns = [lambda E, j=j0 + jj, jj=jj, sd=sd:
                                       E.transpose(p4b[:, sd * 512 + jj * 128:sd * 512 + (jj + 1) * 128],
                                                   p_sb[sd][:, j * 128:(j + 1) * 128], ident[:])
                                       for jj in range(jn)]
                                b.group("pe", fns, reads=[B_psb[sd], c_ident], writes=[P4])
                                b.op("dve", lambda E, j0=j0, jn=jn, sd=sd:
                                     E.tensor_copy(pT[sd][:, j0 * 128:(j0 + jn) * 128],
                                                   p4b[:, sd * 512:sd * 512 + jn * 128]),
                                     reads=[P4], writes=[B_pT[sd]])
                            zp, Bzp = (p5, P5) if sd == 0 else (p6, P6)
                            fns = []
                            for j in range(qi + 1):
                                lhs = v_tm[:, j * 256 + hh * 128:j * 256 + hh * 128 + 128]
                                rhs = pT[sd][:, j * 128:(j + 1) * 128]
                                fns.append(lambda E, lhs=lhs, rhs=rhs, zp=zp, qs=qs,
                                           st=(j == 0), sp=(j == qi):
                                           E.matmul(zp[:, qs * 128:qs * 128 + 128], lhs, rhs,
                                                    start=st, stop=sp))
                            b.group("pe", fns, reads=[B_vtm, B_pT[sd]], writes=[Bzp])
                    for sd, h_ in enumerate(heads):
                        zp, Bzp = (p5, P5) if sd == 0 else (p6, P6)
                        b.op("dve", lambda E, zp=zp, h_=h_,
                             d=z_all[:, h_ * 2048 + band * 512:h_ * 2048 + (band + 1) * 512]:
                             E.tensor_copy(d, zp[:, :]), reads=[Bzp], writes=[B_z[seq]])

            if not cfg["do_wo"]:
                continue
            wsl_ap = [arD16[:, 8192 + sl_ * 2048:8192 + (sl_ + 1) * 2048] for sl_ in range(2)]
            B_wsl = [nb("D"), nb("D")]
            ao = [arB[:, i * 512:(i + 1) * 512] for i in range(16)]
            ao_bufs = [nb("B") for _ in range(16)]
            xm = [arC[:, i * 512:(i + 1) * 512] for i in range(16)]
            xm_bufs = [nb("C") for _ in range(16)]
            stg2 = [arB16[:, i * 512:(i + 1) * 512] for i in range(16)]
            stg2_bufs = [nb("B") for _ in range(16)]
            B_bo = B_bo_shared
            for tg in range(4):
                t0 = tg * 512
                for dt in range(16):
                    wsl = wsl_ap[dt % 2]
                    b.dma("pool", lambda E, dt=dt, d=wsl:
                          E.dma_start(out=d, in_=wob_s[dt, :, :]),
                          reads=[B_wob], writes=[B_wsl[dt % 2]])
                    b.dma("sp", lambda E, dt=dt:
                          E.dma_start(out=C_bo, in_=bo_in[dt:dt + 1, :].rearrange("a p -> p a"),
                                      allow_slow_non_contiguous=True), writes=[B_bo])
                    ps, Bp = (p5, P5) if dt % 2 == 0 else (p6, P6)
                    fns = []
                    for hc in range(16):
                        lhs = wsl[:, hc * 128:(hc + 1) * 128]
                        rhs = z_all[:, hc * 2048 + t0:hc * 2048 + t0 + 512]
                        fns.append(lambda E, lhs=lhs, rhs=rhs, ps=ps, st=(hc == 0), sp=(hc == 15):
                                   E.matmul(ps[:, :], lhs, rhs, start=st, stop=sp))
                    b.group("pe", fns, reads=[B_wsl[dt % 2], B_z[seq]], writes=[Bp])
                    b.op("dve", lambda E, ps=ps, d=ao[dt]:
                         E.tensor_single_scalar(d, ps[:, :], C_bo, OP.add),
                         reads=[Bp, B_bo], writes=[ao_bufs[dt]])
                ln_stats(None, wide=[(arB[:, i * 1024:(i + 1) * 1024],
                                      [ao_bufs[2 * i], ao_bufs[2 * i + 1]]) for i in range(8)])
                for c in range(16):
                    b.dma("sp", lambda E, c=c, t0=t0, seq=seq, d=xm[c]:
                          E.dma_start(out=d, in_=x_in[seq, c, :, t0:t0 + 512]),
                          writes=[xm_bufs[c]])
                bc6 = bass.AP(p6, 0, [[512, 128], [0, 4], [1, 512]])
                bc5 = bass.AP(p5, 0, [[512, 128], [0, 4], [1, 512]])
                for i in range(4):
                    a4 = arB[:, i * 2048:(i + 1) * 2048].rearrange("p (c t) -> p c t", c=4)
                    x4 = arC[:, i * 2048:(i + 1) * 2048].rearrange("p (c t) -> p c t", c=4)
                    rda = ao_bufs[4 * i:4 * i + 4]
                    rdx = xm_bufs[4 * i:4 * i + 4]
                    b.op("dve", lambda E, a4=a4, bc6=bc6: E.tensor_tensor(a4, a4, bc6, OP.mult),
                         reads=[P6], writes=rda)
                    b.op("dve", lambda E, a4=a4, bc5=bc5: E.tensor_tensor(a4, a4, bc5, OP.subtract),
                         reads=[P5], writes=rda)
                    b.op("dve", lambda E, a4=a4, x4=x4: E.tensor_tensor(x4, x4, a4, OP.add),
                         reads=rda, writes=rdx)
                    o_ap = bass.AP(xmid_s.tensor, (seq * 16 + 4 * i) * 128 * S + t0,
                                   [[S, 128], [128 * S, 4], [1, 512]])
                    b.dma("act", lambda E, x4=x4, o_ap=o_ap: E.dma_start(out=o_ap, in_=x4),
                          reads=rdx, writes=[B_xmid[(seq, tg)]])
                ln_stats(None, wide=[(arC[:, i * 1024:(i + 1) * 1024],
                                      [xm_bufs[2 * i], xm_bufs[2 * i + 1]]) for i in range(8)])
                for i in range(4):
                    x4 = arC[:, i * 2048:(i + 1) * 2048].rearrange("p (c t) -> p c t", c=4)
                    d4 = arB16[:, i * 2048:(i + 1) * 2048].rearrange("p (c t) -> p c t", c=4)
                    rdx = xm_bufs[4 * i:4 * i + 4]
                    wrs = stg2_bufs[4 * i:4 * i + 4]
                    b.op("dve", lambda E, x4=x4, d4=d4, bc6=bc6: E.tensor_tensor(d4, x4, bc6, OP.mult),
                         reads=rdx + [P6], writes=wrs)
                    b.op("dve", lambda E, d4=d4, bc5=bc5: E.tensor_tensor(d4, d4, bc5, OP.subtract),
                         reads=[P5], writes=wrs)
                    o_ap = bass.AP(xn2_s.tensor, (seq * 16 + 4 * i) * 128 * S + t0,
                                   [[S, 128], [128 * S, 4], [1, 512]])
                    b.dma("act", lambda E, d4=d4, o_ap=o_ap: E.dma_start(out=o_ap, in_=d4),
                          reads=wrs, writes=[B_xn2[(seq, tg)]])
                emit_dequant(4)

    if cfg["do_ffn"]:
        emit_dequant(len(deq_tasks))
        wdst = arD16[:, 16384:18432]
        xmst = arD[:, 9216:10240]
        outst = arD[:, 10240:11264]
        gel = arD[:, 11264:12288]
        B_aux = [nb(), nb()]
        xnw = arB16
        B_xnw = nb("B")
        inter = arC16
        inter_bufs = {(ip, nt, th): nb("C") for ip in range(2) for nt in range(8) for th in range(2)}
        gsl_ap = [arD16[:, s_ * 4096:(s_ + 1) * 4096] for s_ in range(2)]
        usl_ap = [arD16[:, 8192 + s_ * 4096:8192 + (s_ + 1) * 4096] for s_ in range(2)]
        B_gsl = [nb("D") for _ in range(2)]
        B_usl = [nb("D") for _ in range(2)]
        B_wdst = [nb("D"), nb("D")]
        B_xmst = [nb("D"), nb("D")]
        B_outst = [nb("D"), nb("D")]
        B_gel = nb("D")
        for g in cfg["ffn_groups"]:
            seq, half = g // 2, g % 2
            t0 = half * 1024
            b.dma("sp", lambda E, seq=seq, t0=t0:
                  E.dma_start(out=xnw[:, 0:16384].rearrange("p (c t) -> p c t", c=16),
                              in_=bass.AP(xn2_s.tensor, seq * 16 * 128 * S + t0,
                                          [[S, 128], [128 * S, 16], [1, 1024]])),
                  reads=[B_xn2[(seq, 2 * half)], B_xn2[(seq, 2 * half + 1)]],
                  writes=[B_xnw])
            acc = arA
            acc_bufs = [nb("A") for _ in range(16)]
            for sl in range(64):
                sup, sli = sl // 4, sl % 4
                gs_ = sl % 2
                gsl = gsl_ap[gs_]
                usl = usl_ap[gs_]
                b.dma("sp", lambda E, sl=sl, d=gsl:
                      E.dma_start(out=d, in_=bass.AP(wg_s.tensor, sl * 128 * 16 * 256,
                                                     [[16 * 256, 128], [1, 16 * 256]])),
                      reads=[B_wg[sl]], writes=[B_gsl[gs_]])
                b.dma("sp", lambda E, sl=sl, d=usl:
                      E.dma_start(out=d, in_=bass.AP(wu_s.tensor, sl * 128 * 16 * 256,
                                                     [[16 * 256, 128], [1, 16 * 256]])),
                      reads=[B_wu[sl]], writes=[B_usl[gs_]])
                ip = sup % 2
                for par in (0, 1):
                    j = 2 * sl + par
                    nt = j % 8
                    for th in range(2):
                        tof = th * 512
                        gps = pbig[:, th * 512:th * 512 + 512]
                        ups = pbig[:, 1024 + th * 512:1024 + th * 512 + 512]
                        Bg = PB[th]
                        Bu = PB[2 + th]
                        for wbase, ps_, bps, bw in ((gs_ * 4096, gps, Bg, B_gsl[gs_]),
                                                    (8192 + gs_ * 4096, ups, Bu, B_usl[gs_])):
                            fns = []
                            for c in range(16):
                                lhs = bass.AP(arD16, wbase + c * 256 + par,
                                              [[24576, 128], [2, 128]])
                                rhs = xnw[:, c * 1024 + tof:c * 1024 + tof + 512]
                                fns.append(lambda E, lhs=lhs, rhs=rhs, ps_=ps_,
                                           st=(c == 0), sp=(c == 15):
                                           E.matmul(ps_, lhs, rhs, start=st, stop=sp))
                            b.group("pe", fns, reads=[bw, B_xnw], writes=[bps])
                    gfull = pbig[:, 0:1024]
                    ufull = pbig[:, 1024:2048]
                    it = inter[:, ip * 8192 + nt * 1024:ip * 8192 + nt * 1024 + 1024]
                    Bits = [inter_bufs[(ip, nt, 0)], inter_bufs[(ip, nt, 1)]]
                    gbuf = aux8[:, 0:1024]
                    ubuf = aux8[:, 1024:2048]
                    tA = gel[:, 0:1024]
                    b.op("dve", lambda E, g_=gfull, gbuf=gbuf: E.tensor_copy(gbuf, g_),
                         reads=[PB[0], PB[1]], writes=[B_aux[0]])
                    b.op("act", lambda E, u=ufull, ubuf=ubuf: E.activation(ubuf, u, AF.Copy),
                         reads=[PB[2], PB[3]], writes=[B_aux[1]])
                    b.op("act", lambda E, gbuf=gbuf, tA=tA: E.activation(tA, gbuf, AF.Square),
                         reads=[B_aux[0]], writes=[B_gel])
                    b.op("act", lambda E, tA=tA:
                         E.activation(tA, tA, AF.Copy, scale=C0 * C1, bias=C0),
                         reads=[B_gel], writes=[B_gel])
                    b.op("pool", lambda E, tA=tA, gbuf=gbuf: E.tensor_tensor(tA, tA, gbuf, OP.mult),
                         reads=[B_aux[0]], writes=[B_gel])
                    b.op("pool", lambda E, gbuf=gbuf, ubuf=ubuf:
                         E.tensor_tensor(ubuf, gbuf, ubuf, OP.mult),
                         reads=[B_aux[0]], writes=[B_aux[1]])
                    b.op("act", lambda E, tA=tA: E.activation(tA, tA, AF.Tanh),
                         reads=[B_gel], writes=[B_gel])
                    b.op("act", lambda E, tA=tA: E.activation(tA, tA, AF.Copy, scale=0.5, bias=0.5),
                         reads=[B_gel], writes=[B_gel])
                    b.op("dve", lambda E, tA=tA, u=ubuf, it=it: E.tensor_tensor(it, tA, u, OP.mult),
                         reads=[B_gel, B_aux[1]], writes=Bits)
                if sli == 3 and sup % 2 == 1:
                    for dt in range(16):
                        for s2 in (sup - 1, sup):
                            wds = wdst[:, ((2 * dt + s2) % 2) * 1024:((2 * dt + s2) % 2) * 1024 + 1024]
                            Bw = B_wdst[(2 * dt + s2) % 2]
                            b.dma("sp", lambda E, s2=s2, dt=dt, d=wds:
                                  E.dma_start(out=d, in_=bass.AP(wd_s.tensor,
                                                                 (s2 * 16 + dt) * 128 * 8 * 128,
                                                                 [[8 * 128, 128], [1, 8 * 128]])),
                                  reads=[B_wd[s2]], writes=[Bw])
                            for th in range(2):
                                dps, Bdps = (p5, P5) if (dt + th) % 2 == 0 else (p6, P6)
                                fns = []
                                rd = [Bw]
                                ip2 = s2 % 2
                                for nt in range(8):
                                    lhs = wds[:, nt * 128:(nt + 1) * 128]
                                    rhs = inter[:, ip2 * 8192 + nt * 1024 + th * 512:
                                                ip2 * 8192 + nt * 1024 + th * 512 + 512]
                                    rd.append(inter_bufs[(ip2, nt, th)])
                                    fns.append(lambda E, lhs=lhs, rhs=rhs, dps=dps,
                                               st=(nt == 0 and s2 == sup - 1),
                                               sp=(nt == 7 and s2 == sup):
                                               E.matmul(dps[:, :], lhs, rhs, start=st, stop=sp))
                                b.group("pe", fns, reads=rd, writes=[Bdps])
                                if s2 == sup:
                                    a = acc[:, dt * 1024 + th * 512:dt * 1024 + th * 512 + 512]
                                    if sup == 1:
                                        b.op("dve", lambda E, a=a, dps=dps:
                                             E.tensor_copy(a, dps[:, :]),
                                             reads=[Bdps], writes=[acc_bufs[dt]])
                                    else:
                                        b.op("dve", lambda E, a=a, dps=dps:
                                             E.tensor_tensor(a, a, dps[:, :], OP.add),
                                             reads=[Bdps], writes=[acc_bufs[dt]])
            for th in range(2):
                tof = th * 512
                wide = [(bass.AP(arA, 2 * i * 1024 + tof, [[16384, 128], [1024, 2], [1, 512]]),
                         [acc_bufs[2 * i], acc_bufs[2 * i + 1]]) for i in range(8)]
                ln_stats(None, wide=wide)
                bc6 = bass.AP(p6, 0, [[512, 128], [0, 2], [1, 512]])
                bc5 = bass.AP(p5, 0, [[512, 128], [0, 2], [1, 512]])
                for i in range(8):
                    a2 = bass.AP(arA, 2 * i * 1024 + tof, [[16384, 128], [1024, 2], [1, 512]])
                    sl2 = i % 2
                    xm2 = xmst[:, 0:1024] if sl2 == 0 else outst[:, 0:1024]
                    Bxm = B_xmst[0] if sl2 == 0 else B_outst[0]
                    rda = [acc_bufs[2 * i], acc_bufs[2 * i + 1]]
                    i_ap = bass.AP(xmid_s.tensor, (seq * 16 + 2 * i) * 128 * S + t0 + tof,
                                   [[S, 128], [128 * S, 2], [1, 512]])
                    b.dma("sp", lambda E, xm2=xm2, i_ap=i_ap:
                          E.dma_start(out=xm2.rearrange("p (c t) -> p c t", c=2), in_=i_ap),
                          reads=[B_xmid[(seq, 2 * half + th)]], writes=[Bxm])
                    b.op("dve", lambda E, a2=a2, bc6=bc6: E.tensor_tensor(a2, a2, bc6, OP.mult),
                         reads=[P6], writes=rda)
                    b.op("dve", lambda E, a2=a2, bc5=bc5: E.tensor_tensor(a2, a2, bc5, OP.subtract),
                         reads=[P5], writes=rda)
                    b.op("dve", lambda E, a2=a2, xm2=xm2: E.tensor_tensor(
                        a2, a2, xm2.rearrange("p (c t) -> p c t", c=2), OP.add),
                         reads=[Bxm], writes=rda)
                    o_ap = bass.AP(out_d.tensor, (seq * 16 + 2 * i) * 128 * S + t0 + tof,
                                   [[S, 128], [128 * S, 2], [1, 512]])
                    b.dma("act", lambda E, a2=a2, o_ap=o_ap: E.dma_start(out=o_ap, in_=a2),
                          reads=rda, writes=[B_out])

    allb = Buf()
    allb.w = dict(b.cnt)
    for name in cfg.get("dump_sbuf", []):
        ap = {"q_sb": arC16[:, 0:2048], "k_sb": arC16[:, 2048:4096],
              "q1_sb": arC16[:, 4096:6144], "k1_sb": arC16[:, 6144:8192],
              "p1_sb": arD16[:, 2048:4096], "s0_sb": arB16[:, 12288:14336],
              "s1_sb": arB16[:, 14336:16384],
              "v_tm": arB16[:, 8192:12288], "p_sb": arD16[:, 0:2048],
              "pT": arD16[:, 4096:6144], "z_all": arA16[:, 0:32768],
              "xw": arB16[:, 0:8192], "acc": arA[:, :], "xnw": arB16[:, 0:16384],
              "rows": rows[:, :], "gelx": arD[:, 9216:12288],
              "inter": arC16[:, 0:16384]}[name]
        shp = list(ap.shape)
        dt_ = {2: BF16, 4: F32}[4 if name in ("acc", "rows", "gelx") else 2]
        dd = DT("dbg_" + name, shp, dt_, kind="ExternalOutput").ap()
        b.dma("sp", lambda E, ap=ap, dd=dd: E.dma_start(out=dd[:, :], in_=ap), reads=[allb])
    for name, src, dst in dbg:
        flat_src = bass.AP(src.tensor, 0,
                           [[src.shape[-1], int(np.prod(src.shape)) // src.shape[-1]],
                            [1, src.shape[-1]]])
        b.dma("sp", lambda E, s=flat_src, d=dst: E.dma_start(out=d, in_=s), reads=[allb])

    b.finalize()
    return nc


def _pi():
    pi = np.zeros(2048, np.int64)
    for c in range(8):
        for par in range(2):
            m = 2 * c + par
            pi[m * 128:(m + 1) * 128] = 256 * c + 2 * np.arange(128) + par
    return pi


def prep_inputs(L0_attn_x, L0_attn_W_Q, L0_attn_W_K, L0_attn_W_V, L0_attn_W_O,
                L0_attn_b_Q, L0_attn_b_K, L0_attn_b_V, L0_attn_b_O,
                L0_ffn_gate_W_packed, L0_ffn_gate_scales, L0_ffn_up_W_packed,
                L0_ffn_up_scales, L0_ffn_down_W_packed, L0_ffn_down_scales):
    pi = _pi()
    x = np.ascontiguousarray(
        np.asarray(L0_attn_x).transpose(0, 2, 1)[:, pi, :]).reshape(16, 16, 128, 2048)

    def head_img(w):
        wp = np.asarray(w)[:, pi, :]
        return np.ascontiguousarray(
            wp.reshape(16, 16, 128, 128).transpose(0, 2, 1, 3).reshape(16, 128, 2048))

    wq = head_img(L0_attn_W_Q)
    wk = head_img(L0_attn_W_K)
    wvp = np.asarray(L0_attn_W_V)[:, pi, :]
    wv = np.ascontiguousarray(
        wvp.transpose(1, 0, 2).reshape(16, 128, 2048)
        .reshape(16, 128, 8, 256).transpose(2, 1, 0, 3).reshape(8, 128, 4096))
    wo = np.asarray(L0_attn_W_O).reshape(2048, 2048)[:, pi]
    wo2 = np.ascontiguousarray(
        wo.reshape(16, 128, 16, 128).transpose(2, 1, 0, 3).reshape(16, 128, 2048))
    bq_ = np.asarray(L0_attn_b_Q, dtype=np.float32)
    bk_ = np.asarray(L0_attn_b_K, dtype=np.float32)
    bqk = np.zeros((8, 128, 4), np.float32)
    for pr in range(8):
        bqk[pr, :, 0] = bq_[2 * pr]
        bqk[pr, :, 1] = bk_[2 * pr]
        bqk[pr, :, 2] = bq_[2 * pr + 1]
        bqk[pr, :, 3] = bk_[2 * pr + 1]
    bqk = np.ascontiguousarray(bqk)
    bvg = np.ascontiguousarray(np.asarray(L0_attn_b_V, dtype=np.float32).reshape(1, 2048))
    bo = np.ascontiguousarray(np.asarray(L0_attn_b_O, dtype=np.float32)[pi].reshape(16, 128))

    def u8(a):
        return np.asarray(a).astype(np.uint8)

    pg = np.ascontiguousarray(u8(L0_ffn_gate_W_packed).T)
    pu = np.ascontiguousarray(u8(L0_ffn_up_W_packed).T)
    pd = np.ascontiguousarray(u8(L0_ffn_down_W_packed).T[:, pi])
    sg = np.ascontiguousarray(np.asarray(L0_ffn_gate_scales, dtype=np.float32).T)
    su = np.ascontiguousarray(np.asarray(L0_ffn_up_scales, dtype=np.float32).T)
    sd = np.ascontiguousarray(np.asarray(L0_ffn_down_scales, dtype=np.float32).T[:, pi])
    idn = np.eye(128, dtype=np.float32)
    msk = np.where(np.arange(128)[None, :] > np.arange(128)[:, None],
                   np.float32(NEG), np.float32(0.0)).astype(np.float32)
    shared = dict(wq=wq, wk=wk, wv=wv, wo=wo2, bqk=bqk, bvg=bvg, bo=bo,
                  pg=pg, pu=pu, pd=pd, sg=sg, su=su, sd=sd, idn=idn, msk=msk)
    in_maps = []
    for core in range(8):
        m = dict(shared)
        m["x"] = np.ascontiguousarray(x[2 * core:2 * core + 2])
        in_maps.append(m)
    return in_maps, pi


def kernel(**inputs):
    in_maps, pi = prep_inputs(**inputs)
    nc = build()
    res = run_bass_kernel_spmd(nc, in_maps, core_ids=list(range(8)))
    inv = np.argsort(pi)
    out = np.empty((16, 2048, 2048), np.float32)
    for core in range(8):
        o = np.asarray(res.results[core]["out"]).reshape(2, 2048, 2048)
        out[2 * core:2 * core + 2] = o[:, inv, :].transpose(0, 2, 1)
    return out


if __name__ == "__main__":
    n = build()
    print("built ok; instructions:", n.m.n_instructions() if hasattr(n, "m") else "?")
